# version 03
# baseline (speedup 1.0000x reference)
from concourse.bass_utils import run_bass_kernel_spmd

import math
from contextlib import ExitStack

import numpy as np
import ml_dtypes
import concourse.bass as bass
import concourse.mybir as mybir

F32 = mybir.dt.float32
BF16 = mybir.dt.bfloat16
I32 = mybir.dt.int32
AF = mybir.ActivationFunctionType
ALU = mybir.AluOpType
AX = mybir.AxisListType
bfdt = ml_dtypes.bfloat16

ENGINES = ("sync", "gpsimd", "tensor", "vector", "scalar")

T = 2048
D = 1024
TT = T // 128
DC = D // 128
NEG = -30000.0


class Sem:
    __slots__ = ("h", "n", "name")

    def __init__(self, h, name):
        self.h = h
        self.n = 0
        self.name = name


class Prog:
    def __init__(self, nc, es):
        self.nc = nc
        self.es = es
        self.ops = {e: [] for e in ENGINES}

    def sem(self, name):
        return Sem(self.es.enter_context(self.nc.semaphore(name)), name)

    def emit(self, eng, fn):
        self.ops[eng].append(fn)

    def dma(self, queue, out, in_, sem, deps=(), **kw):
        for d in deps:
            if d is not None:
                self.wait(queue, d[0], d[1])
        sem.n += 16

        def f(e, out=out, in_=in_, s=sem.h, kw=kw):
            e.dma_start(out=out, in_=in_, **kw).then_inc(s, 16)

        self.emit(queue, f)
        return (sem, sem.n)

    def wait(self, eng, sem, v=None):
        v = sem.n if v is None else v

        def f(e, s=sem.h, v=v):
            e.wait_ge(s, v)

        self.emit(eng, f)

    def wtok(self, eng, *toks):
        for d in toks:
            if d is not None:
                self.wait(eng, d[0], d[1])

    def op(self, eng, method, *args, sem=None, deps=(), inc=1, **kw):
        self.wtok(eng, *deps)
        if sem is not None:
            sem.n += inc

        def f(e, args=args, kw=kw, sem=sem, inc=inc):
            inst = getattr(e, method)(*args, **kw)
            if sem is not None:
                inst.then_inc(sem.h, inc)

        self.emit(eng, f)
        return (sem, sem.n) if sem is not None else None

    def run(self):
        with self.nc.Block() as block:
            for eng in ENGINES:
                ops = self.ops[eng]
                if not ops:
                    continue

                def body(e, ops=ops):
                    for fn in ops:
                        fn(e)

                getattr(block, eng)(body)




class _PlainAug:

    def __init__(self, plain, aug):
        self.plain = plain
        self.aug = aug

    def __getitem__(self, idx):
        rows, cols = idx
        if rows == slice(0, 64):
            return self.plain[:, cols]
        if rows == slice(64, 65):
            return self.aug[64:65, cols]
        raise KeyError(idx)


def cdiv(a, b):
    return (a + b - 1) // b


class Ctx:
    def __init__(self, nc, es, wdram):
        self.nc = nc
        self.es = es
        self.w = wdram
        self.P = Prog(nc, es)
        P = self.P
        self._streams = {}
        self.st = P.sem("st")
        self.sv = P.sem("sv")
        self.sa = P.sem("sa")
        self.sg = P.sem("sg")
        self.cc = P.sem("cc")
        sb = lambda name, shape, dt: es.enter_context(nc.sbuf_tensor(name, shape, dt))
        self.sb = sb
        self.xnT = sb("xnT", [128, DC * T], BF16)
        self.sT = sb("sT", [128, DC * T], BF16)
        ps = lambda name, shape, dt: es.enter_context(nc.psum_tensor(name, shape, dt))
        self.psA = ps("psA", [128, 2048], F32)
        self.psB = ps("psB", [128, 512], F32)
        self.psC = ps("psC", [128, 512], F32)
        self.psD = ps("psD", [128, 512], F32)
        self.psT16 = ps("psT16", [128, 1024], BF16)
        self.ident = sb("ident", [128, 128], BF16)
        identd = nc.inline_tensor(np.eye(128, dtype=bfdt), "identconst")
        cst = self.stream("const")
        P.dma("sync", self.ident[:, :], identd[:, :], cst)
        self.onescol = sb("onescol", [128, 1], BF16)
        self.onesrow = sb("onesrow", [1, 512], BF16)
        P.op("vector", "memset", self.onescol[:, :], 1.0)
        self.tok_ones = P.op("vector", "memset", self.onesrow[:, :], 1.0, sem=self.sv)
        mk = np.where(np.arange(128)[:, None] >= np.arange(128)[None, :], 0.0, NEG)
        self.diagmask = sb("diagmask", [128, 128], F32)
        md = nc.inline_tensor(mk.astype(np.float32), "diagmaskc")
        P.dma("sync", self.diagmask[:, :], md[:, :], cst)
        self.diagmaskT = sb("diagmaskT", [128, 128], F32)
        mdT = nc.inline_tensor(mk.T.copy().astype(np.float32), "diagmaskTc")
        self.negmask = sb("negmask", [128, 128], F32)
        nm = nc.inline_tensor(np.full((128, 128), NEG, np.float32), "negmaskc")
        P.dma("sync", self.negmask[:, :], nm[:, :], cst)
        self.tok_consts = P.dma("sync", self.diagmaskT[:, :], mdT[:, :], cst)
        self.tok_ident = self.tok_diag = self.tok_diagT = self.tok_consts
        self.x_dram = nc.dram_tensor("x_stream", [T, D], F32)
        self._proj_ps_free = {}
        self._uid = 0
        self.dbg_on = False
        self.dbg_names = []
        self._dbg_tok = None

    def dbg(self, name, ap, tok, dt=BF16):
        if not self.dbg_on:
            return
        shape = [ap.shape[0], ap.shape[1]]
        d = self.nc.dram_tensor("dbg_" + name, shape, dt, kind="ExternalOutput")
        self._dbg_tok = self.P.dma("sync", d[:, :], ap, self.stream("dbg"), deps=[tok])
        self.dbg_names.append(("dbg_" + name, dt))

    def uname(self, base):
        self._uid += 1
        return f"{base}_u{self._uid}"

    def stream(self, name):
        s = self._streams.get(name)
        if s is None:
            s = self.P.sem("D_" + name)
            self._streams[name] = s
        return s

    def finish(self):
        for s in self._streams.values():
            self.P.wait("sync", s)

    def ln_phase(self, x_src_dram, xn_tm_dst=None, deps=()):
        P, st, sv, sa = self.P, self.st, self.sv, self.sa
        lnes = ExitStack()
        xs_streams = [self.stream(f"xs{p}") for p in range(3)]
        cons = {}
        evac = {}
        nst = {}
        with lnes:
            S = lnes.enter_context(self.nc.sbuf_tensor(self.uname("lnstats"), [128, 64], F32))
            self.xstg = lnes.enter_context(self.nc.sbuf_tensor(self.uname("xstg"), [128, 3 * 1024], F32))
            self.xsqbuf = lnes.enter_context(self.nc.sbuf_tensor(self.uname("xsqbuf"), [128, 2 * 1024], F32))
            self.nstg = lnes.enter_context(self.nc.sbuf_tensor(self.uname("nstg"), [128, 2 * 1024], BF16))
            self._ln_loop(P, st, sv, sa, S, x_src_dram, xn_tm_dst, xs_streams, cons, evac, nst, deps)
        return self.tok_xnT

    def _ln_loop(self, P, st, sv, sa, S, x_src_dram, xn_tm_dst, xs_streams, cons, evac, nst, ldeps0=()):
        for t in range(TT):
            p3 = t % 3
            p2 = t % 2
            xs = self.xstg[:, p3 * 1024:(p3 + 1) * 1024]
            xq = self.xsqbuf[:, p2 * 1024:(p2 + 1) * 1024]
            ns = self.nstg[:, p2 * 1024:(p2 + 1) * 1024]
            so = p2 * 8
            s1, s2, mean, msq, var, rstd, nmr = (S[:, so + i:so + i + 1] for i in range(7))
            deps = [cons.get(t - 3, [None, None])[0], cons.get(t - 3, [None, None])[1]]
            if t < 3:
                deps += list(ldeps0) + [(xs_streams[p3], xs_streams[p3].n)]
            ld = P.dma("sync", xs, x_src_dram[t * 128:(t + 1) * 128, :], xs_streams[p3], deps=deps)
            t1 = P.op("vector", "tensor_reduce", s1, xs, AX.X, ALU.add, sem=sv, deps=[ld])
            t2 = P.op("vector", "tensor_tensor", xq, xs, xs, ALU.mult, sem=sv)
            t3 = P.op("vector", "tensor_reduce", s2, xq, AX.X, ALU.add, sem=sv, deps=[t2])
            t4 = P.op("vector", "tensor_scalar", mean, s1, 1.0 / D, None, ALU.mult, sem=sv, deps=[t1])
            t5 = P.op("vector", "tensor_tensor", msq, mean, mean, ALU.mult, sem=sv, deps=[t4])
            t6 = P.op("vector", "scalar_tensor_tensor", var, s2, 1.0 / D, msq, ALU.mult,
                      ALU.subtract, sem=sv, deps=[t3, t5])
            t6b = P.op("vector", "tensor_scalar", var, var, 1e-5, None, ALU.add, sem=sv, deps=[t6])
            t7 = P.op("scalar", "activation", rstd, var, AF.Sqrt, sem=sa, deps=[t6b])
            t8 = P.op("vector", "reciprocal", rstd, rstd, sem=sv, deps=[t7])
            t9 = P.op("vector", "scalar_tensor_tensor", nmr, mean, -1.0, rstd, ALU.mult,
                      ALU.mult, sem=sv, deps=[t8])
            ndeps = [t9, nst.get(t - 2)]
            t10 = P.op("scalar", "activation", ns, xs, AF.Identity, bias=nmr, scale=rstd,
                       sem=sa, deps=ndeps)
            cons[t] = [t10, t6]
            if xn_tm_dst is not None:
                P.op("vector", "tensor_copy", xn_tm_dst[:, t * 1024:(t + 1) * 1024], ns,
                     sem=sv, deps=[t10])
            tdeps = [t10, evac.get(t - 1), self.tok_ident] + (list(ldeps0) if t == 0 else [])
            last = None
            for c in range(DC):
                last = P.op("tensor", "transpose", self.psT16[:, c * 128:(c + 1) * 128],
                            ns[:, c * 128:(c + 1) * 128], self.ident[:, :],
                            sem=(st if c == DC - 1 else None),
                            deps=(tdeps if c == 0 else ()))
            dst = self.xnT[:, :].rearrange("p (c j) -> p c j", j=T)[:, :, t * 128:(t + 1) * 128]
            src = self.psT16[:, :].rearrange("p (c j) -> p c j", j=128)
            ev = P.op("vector", "tensor_copy", dst, src, sem=sv, deps=[last])
            evac[t] = ev
            nst[t] = last
        self.tok_xnT = ev

    def mm_acc(self, out_ap, pairs, sem=None, deps=(), extra=()):
        P = self.P
        n = len(pairs) + len(extra)
        tok = None
        for i, (l, r) in enumerate(list(pairs) + list(extra)):
            tok = P.op("tensor", "matmul", out_ap, l, r,
                       start=(i == 0), stop=(i == n - 1),
                       sem=(sem if i == n - 1 else None),
                       deps=(deps if i == 0 else ()))
        return tok

    def cast_weight_to_dram(self, name, dst_shape=None):
        src = self.w[name]
        shp = list(src.shape) if dst_shape is None else dst_shape
        dst = self.nc.dram_tensor(name + "_bf", shp, BF16)
        strm = self.stream("castall")
        self.P.dma("gpsimd", dst[:, :] if len(shp) == 2 else dst[:, :, :],
                   src[:, :] if len(src.shape) == 2 else src[:, :, :], strm)
        return dst, (strm, None)

    def residual_phase(self, x_src_dram, x_dst_dram, deps=()):
        P, st, sv = self.P, self.st, self.sv
        xs_streams = [self.stream(f"xs{p}") for p in range(3)]
        evtok = {}
        sttok = {}
        res_es = ExitStack()
        with res_es:
            self.xstg = res_es.enter_context(self.nc.sbuf_tensor(self.uname("xstg_r"), [128, 3 * 1024], F32))
            stw = self._res_loop(P, st, sv, x_src_dram, x_dst_dram, deps, xs_streams, evtok, sttok)
        return stw

    def _res_loop(self, P, st, sv, x_src_dram, x_dst_dram, deps, xs_streams, evtok, sttok):
        for t in range(TT):
            p3 = t % 3
            xs = self.xstg[:, p3 * 1024:(p3 + 1) * 1024]
            lddeps = [sttok.get(t - 3)] + (list(deps) if t == 0 else [])
            if t < 3:
                lddeps += [(xs_streams[p3], xs_streams[p3].n)]
            ld = P.dma("sync", xs, x_src_dram[t * 128:(t + 1) * 128, :], xs_streams[p3], deps=lddeps)
            tdeps = [evtok.get(t - 1)] + (list(deps) if t == 0 else [])
            last = None
            sgv = self.sT[:, :].rearrange("p (c j) -> p c j", j=T)
            for c in range(DC):
                last = P.op("tensor", "transpose", self.psT16[:, c * 128:(c + 1) * 128],
                            sgv[:, c, t * 128:(t + 1) * 128], self.ident[:, :],
                            sem=(st if c == DC - 1 else None),
                            deps=(tdeps if c == 0 else ()))
            ad = P.op("vector", "tensor_tensor", xs, xs,
                      self.psT16[:, :], ALU.add, sem=sv, deps=[last, ld])
            evtok[t] = ad
            stw = P.dma("sync", x_dst_dram[t * 128:(t + 1) * 128, :], xs, xs_streams[p3], deps=[ad])
            sttok[t] = stw
        return stw

    def l0_ffn(self, sT_ready=None, deps=()):
        P = self.P
        st, sv, sa = self.st, self.sv, self.sa
        SC = 1.0 / 127.0
        wst = [self.stream("l0w0"), self.stream("l0w1")]
        wdst = self.stream("l0wd")
        wg_bf, tg = self.cast_weight_to_dram("L0_ffn_w_gate")
        wu_bf, tu = self.cast_weight_to_dram("L0_ffn_w_up")
        wd_bf, td = self.cast_weight_to_dram("L0_ffn_w_down")
        es0 = ExitStack()
        sb0 = lambda n, sh, dt: es0.enter_context(self.nc.sbuf_tensor(n, sh, dt))
        wgbuf = sb0("l0_wg", [128, 2 * 8 * 512], BF16)
        wubuf = sb0("l0_wu", [128, 2 * 8 * 512], BF16)
        wdbuf = sb0("l0_wd", [128, 24 * 1024], BF16)
        hbuf = sb0("l0_h", [128, 24 * 512], BF16)
        gstg = sb0("l0_gs", [128, 2 * 512], BF16)
        dstg = sb0("l0_ds", [128, 2 * 512], BF16)
        NCH, HC = 4, 6
        psB_free = psC_free = psD_free = None
        gsl_free = {}
        dsl_free = {}
        wpar_free = {}
        down_done = None
        gidx = 0
        didx = 0
        mv = None
        for nc4 in range(NCH):
            wdeps = [td, down_done] + (list(deps) if nc4 == 0 else [])
            wd_ld = None
            for j in range(24):
                wd_ld = P.dma("sync", wdbuf[:, j * 1024:(j + 1) * 1024],
                              wd_bf[:, j * 128:(j + 1) * 128], wdst,
                              deps=(wdeps if j == 0 else ()), transpose=True)
            hv = None
            u_tok = None
            for hc in range(HC):
                par = hc % 2
                wg_sl = wgbuf[:, par * 4096:(par + 1) * 4096]
                wu_sl = wubuf[:, par * 4096:(par + 1) * 4096]
                h0 = hc * 512
                pi = nc4 * HC + hc
                ldeps = [tg, tu, wpar_free.get(pi - 2)] + (list(deps) if pi == 0 else [])
                lg = None
                for dk in range(8):
                    lg = P.dma("sync", wg_sl[:, dk * 512:(dk + 1) * 512],
                               wg_bf[h0:h0 + 512, dk * 128:(dk + 1) * 128], wst[par],
                               deps=(ldeps if dk == 0 else ()), transpose=True)
                lu = None
                for dk in range(8):
                    lu = P.dma("sync", wu_sl[:, dk * 512:(dk + 1) * 512],
                               wu_bf[h0:h0 + 512, dk * 128:(dk + 1) * 128], wst[par],
                               transpose=True)
                for mt in range(4):
                    gp = gidx % 2
                    gsl = gstg[:, gp * 512:(gp + 1) * 512]
                    xsl = lambda dk: self.xnT[:, dk * T + nc4 * 512: dk * T + nc4 * 512 + 512]
                    prs = [(wg_sl[:, dk * 512 + mt * 128: dk * 512 + mt * 128 + 128], xsl(dk))
                           for dk in range(8)]
                    gdeps = [lu, self.tok_xnT, psB_free] + (list(deps) if gidx == 0 else [])
                    g_tok = self.mm_acc(self.psB[:, :], prs, sem=st, deps=gdeps)
                    sil = P.op("scalar", "activation", gsl, self.psB[:, :], AF.Silu,
                               scale=SC, sem=sa, deps=[g_tok, gsl_free.get(gidx - 2)])
                    psB_free = sil
                    pru = [(wu_sl[:, dk * 512 + mt * 128: dk * 512 + mt * 128 + 128], xsl(dk))
                           for dk in range(8)]
                    u_tok = self.mm_acc(self.psC[:, :], pru, sem=st,
                                        deps=[psC_free] + (list(deps) if gidx == 1 else []))
                    kh = hc * 4 + mt
                    hdeps = [u_tok, sil]
                    if nc4 > 0 and hc == 0 and mt == 0:
                        hdeps.append(down_done)
                    hv = P.op("vector", "scalar_tensor_tensor",
                              hbuf[:, kh * 512:(kh + 1) * 512], self.psC[:, :], SC, gsl,
                              ALU.mult, ALU.mult, sem=sv, deps=hdeps)
                    psC_free = hv
                    gsl_free[gidx] = hv
                    gidx += 1
                wpar_free[pi] = u_tok
            d_tok = None
            for dm in range(8):
                prd = [(wdbuf[:, kh * 1024 + dm * 128: kh * 1024 + dm * 128 + 128],
                        hbuf[:, kh * 512:(kh + 1) * 512]) for kh in range(24)]
                d_tok = self.mm_acc(self.psD[:, :], prd, sem=st,
                                    deps=[hv, wd_ld, psD_free] + (list(deps) if didx == 0 else []))
                dp = didx % 2
                dsl = dstg[:, dp * 512:(dp + 1) * 512]
                sil2 = P.op("scalar", "activation", dsl, self.psD[:, :], AF.Silu,
                            scale=SC, sem=sa, deps=[d_tok, dsl_free.get(didx - 2)])
                psD_free = sil2
                ssl = self.sT[:, dm * T + nc4 * 512: dm * T + nc4 * 512 + 512]
                mdeps = [sil2]
                if nc4 == 0 and dm == 0 and sT_ready is not None:
                    mdeps.append(sT_ready)
                mv = P.op("vector", "tensor_tensor", ssl, ssl, dsl, ALU.mult,
                          sem=sv, deps=mdeps)
                dsl_free[didx] = mv
                didx += 1
            down_done = d_tok
        es0.close()
        self.tok_sT = mv
        return mv

    def load_bias_cols2(self, scope, name, m, deps=()):
        b = scope("b_" + name, [128, m], F32)
        ap = self.w[name].ap().rearrange("(m p) -> p m", p=128)
        strm = self.stream("biasS")
        self.P.dma("sync", b[:, :], ap, strm, deps=deps,
                   allow_slow_non_contiguous=True)
        return b, (strm, None)

    def load_bias_row2(self, scope, name, n, deps=()):
        b = scope("br_" + name, [1, n], BF16)
        strm = self.stream("biasG")
        self.P.dma("gpsimd", b[0:1, :], self.w[name].ap().rearrange("(o n) -> o n", o=1),
                   strm, deps=deps)
        return b, (strm, None)

    def load_w_nat(self, name, kin, nout, scope_sb, deps=(), tag=""):
        kt = kin // 128
        sb = scope_sb("w_" + name + tag, [128, kt * nout], BF16)
        strm = self.stream("wnat")
        for k in range(kt):
            self.P.dma("gpsimd", sb[:, k * nout:(k + 1) * nout],
                       self.w[name][k * 128:(k + 1) * 128, :], strm,
                       deps=((list(deps) + [self._dbg_tok]) if k == 0 else ()))
        return sb, (strm, None)

    def attn_heads(self, sbf, nheads, augq, augk, vcol, attn_dst, kmax_fn, causal,
                   sT_extra=None, qk_extra=None, deps=(), pst_free=None, pt_buf=None):
        P = self.P
        st, sv, sa = self.st, self.sv, self.sa
        PT = pt_buf if pt_buf is not None else sbf(self.uname("PTbuf"), [128, 8 * 512], BF16)
        pv_hist = {}
        rcrow = sbf(self.uname("rcrow"), [1, 512], F32)
        rcbf = sbf(self.uname("rcbf"), [1, 512], BF16)
        rcsb = sbf(self.uname("rcsb"), [64, 512], BF16)
        mrow = sbf(self.uname("mrow"), [128, 1], F32)
        mneg = sbf(self.uname("mneg"), [128, 1], BF16)
        tok_qk = None
        tok_red = None
        tok_exp = None
        tok_sTmm = None
        tok_pv = None
        tok_attn = None
        tok_bc = None
        tok_rc = None
        tok_stat = pst_free
        for h in range(nheads):
            prev = {"attn": tok_attn, "red": tok_red, "exp": tok_exp, "pv": tok_pv}
            aq, tq = augq(h, prev)
            ak, tk = augk(h, prev)
            for qg in range(4):
                qts = [4 * qg + i for i in range(4)]
                qke = qk_extra(h) if qk_extra is not None else None
                for qt in qts:
                    kmax = kmax_fn(qt)
                    nch = cdiv(kmax, 512)
                    first = True
                    for c in range(nch):
                        k1 = min(512, kmax - c * 512)
                        dd = [tq, tk, tok_red] + list(deps) if first else []
                        tok_qk = P.op("tensor", "matmul", self.psA[:, c * 512:c * 512 + k1],
                                      aq[0:64, qt * 128:(qt + 1) * 128],
                                      ak[0:64, c * 512:c * 512 + k1],
                                      start=True, stop=(qke is None), sem=st, deps=dd,
                                      skip_group_check=True)
                        if qke is not None:
                            ql, qr = qke
                            tok_qk = P.op("tensor", "matmul",
                                          self.psA[:, c * 512:c * 512 + k1],
                                          ql(qt), qr(c, k1),
                                          start=False, stop=True, sem=st,
                                          skip_group_check=True)
                        first = False
                    if causal:
                        tok_qk = P.op("vector", "tensor_tensor",
                                      self.psA[:, qt * 128:qt * 128 + 128],
                                      self.psA[:, qt * 128:qt * 128 + 128],
                                      self.diagmask[:, :], ALU.add, sem=sv,
                                      deps=[tok_qk])
                    tok_red = P.op("vector", "tensor_reduce", mrow[:, :],
                                   self.psA[:, 0:kmax], AX.X, ALU.max, sem=sv,
                                   deps=[tok_qk, tok_stat])
                    tneg = P.op("scalar", "activation", mneg[:, :], mrow[:, :],
                                AF.Identity, scale=-1.0, sem=sa, deps=[tok_red])
                    ttr = P.op("tensor", "transpose", self.psT16[0:1, 0:128],
                               mneg[:, :], self.ident[:, :], sem=st,
                               deps=[tneg, tok_stat])
                    tok_stat = P.op("scalar", "copy", aq[64:65, qt * 128:(qt + 1) * 128],
                                    self.psT16[0:1, 0:128], sem=sa, deps=[ttr])
                ktn = cdiv(kmax_fn(qts[-1]), 128)
                extra = sT_extra(h) if sT_extra is not None else None
                for kt in range(ktn):
                    ks = kt % 8
                    dd = [tok_stat, tok_exp]
                    if extra is None:
                        tok_sTmm = P.op("tensor", "matmul", self.psB[:, :],
                                        ak[0:65, kt * 128:(kt + 1) * 128],
                                        aq[0:65, qg * 512:(qg + 1) * 512],
                                        start=True, stop=True, sem=st, deps=dd,
                                        skip_group_check=True)
                    else:
                        l2, r2 = extra
                        P.op("tensor", "matmul", self.psB[:, :],
                             ak[0:64, kt * 128:(kt + 1) * 128],
                             aq[0:64, qg * 512:(qg + 1) * 512],
                             start=True, stop=False, sem=None, deps=dd,
                             skip_group_check=True)
                        tok_sTmm = P.op("tensor", "matmul", self.psB[:, :],
                                        l2(kt), r2(qg),
                                        start=False, stop=True, sem=st,
                                        skip_group_check=True)
                    mt = self._mask_subtiles(qg, kt, kmax_fn) if causal else []
                    tok_m = tok_sTmm
                    for (sub, mask) in mt:
                        tok_m = P.op("vector", "tensor_tensor",
                                     self.psB[:, sub * 128:(sub + 1) * 128],
                                     self.psB[:, sub * 128:(sub + 1) * 128],
                                     mask, ALU.add, sem=sv, deps=[tok_m])
                    tok_exp = P.op("scalar", "activation", PT[:, ks * 512:(ks + 1) * 512],
                                   self.psB[:, :], AF.Exp, sem=sa,
                                   deps=[tok_m, pv_hist.get(kt - 8)])
                    tok_pv = P.op("tensor", "matmul", self.psC[0:65, :],
                                  vcol(h, kt), PT[:, ks * 512:(ks + 1) * 512],
                                  start=(kt == 0), stop=(kt == ktn - 1), sem=st,
                                  deps=[tok_exp, tok_attn if kt == 0 else None],
                                  skip_group_check=True)
                    pv_hist[kt] = tok_pv
                tok_rc = P.op("vector", "reciprocal", rcrow[0:1, :], self.psC[64:65, :],
                              sem=sv, deps=[tok_pv, tok_bc])
                trc2 = P.op("scalar", "copy", rcbf[0:1, :], rcrow[0:1, :], sem=sa,
                            deps=[tok_rc])
                tok_bc = P.op("tensor", "matmul", self.psD[0:64, :],
                              self.onesrow[0:1, 0:64], rcbf[0:1, :],
                              start=True, stop=True, sem=st, deps=[trc2],
                              skip_group_check=True)
                trc3 = P.op("scalar", "copy", rcsb[:, :], self.psD[0:64, :], sem=sa,
                            deps=[tok_bc])
                tok_attn = P.op("vector", "tensor_tensor", attn_dst(h, qg),
                                self.psC[0:64, :], rcsb[:, :], ALU.mult, sem=sv,
                                deps=[trc3])
        return tok_attn

    def _mask_subtiles(self, qg, kt, kmax_fn):
        out = []
        for i in range(4):
            qt = 4 * qg + i
            kmax = kmax_fn(qt)
            if kt * 128 >= kmax:
                out.append((i, self.negmask[:, :]))
            elif kt == qt:
                out.append((i, self.diagmaskT[:, :]))
        return out

    def proj_T(self, wsb, wtok, kt, dst_sb, dst_cols, mout, bias_sb, rhs_sb, rhs_cols,
               extra_dep=None, scale=None, psum=None, dst_rowoff=0, evac_bf=True,
               nch=4, ncw=512, rhs_T=T):
        P = self.P
        ps = psum or self.psB
        tok_ev = None
        for m in range(mout):
            for c in range(nch):
                prs = [(wsb[:, k * (mout * 128) + m * 128: k * (mout * 128) + m * 128 + 128],
                        rhs_sb[:, k * rhs_T + rhs_cols + c * ncw: k * rhs_T + rhs_cols + c * ncw + ncw])
                       for k in range(kt)]
                dd = [wtok, self._proj_ps_free.get(id(ps)), extra_dep]
                tokmm = self.mm_acc(ps[:, 0:ncw], prs, sem=self.st, deps=dd)
                dst = dst_cols(m, c)
                kw = {}
                if bias_sb is not None:
                    kw["bias"] = bias_sb[:, m:m + 1]
                if scale is not None:
                    kw["scale"] = scale
                tok_ev = P.op("scalar", "activation", dst, ps[:, 0:ncw], AF.Identity,
                              sem=self.sa, deps=[tokmm], **kw)
                self._proj_ps_free[id(ps)] = tok_ev
        return tok_ev

    def _rope_stream(self, src_sb, src_kt, wsb, wt, brow, bt, dstb, prevdep,
                     rope_cos, rope_sin, base_ii, scope):
        P, st, sv = self.P, self.st, self.sv
        csbuf = scope(f"csbuf{base_ii}", [128, 2 * 1024], F32)
        rtmp = scope(f"rtmp{base_ii}", [128, 2 * 512], F32)
        ropestg = scope(f"ropestg{base_ii}", [128, 2 * 1024], BF16)
        cs_str = [self.stream(f"cs{base_ii}_0"), self.stream(f"cs{base_ii}_1")]
        ta = None
        for tt in range(TT):
            par = tt % 2
            ii = base_ii * TT + tt
            cs = csbuf[:, par * 1024:(par + 1) * 1024]
            lc = P.dma("sync", cs[:, 0:512], rope_cos[tt * 128:(tt + 1) * 128, :],
                       cs_str[par], deps=[self._rp_csfree.get(ii - 2)])
            ls = P.dma("sync", cs[:, 512:1024], rope_sin[tt * 128:(tt + 1) * 128, :],
                       cs_str[par])
            prs = [(src_sb[:, k * T + tt * 128: k * T + tt * 128 + 128],
                    wsb[:, k * 1024: k * 1024 + 512]) for k in range(src_kt)]
            prs2 = [(src_sb[:, k * T + tt * 128: k * T + tt * 128 + 128],
                     wsb[:, k * 1024 + 512: k * 1024 + 1024]) for k in range(src_kt)]
            exd = [wt, bt, prevdep, self._rp_psfree.get(ii - 1), self.tok_ones]
            m1 = self.mm_acc(self.psA[:, 0:512],
                             prs + [(self.onesrow[0:1, 0:128], brow[0:1, 0:512])],
                             sem=st, deps=exd)
            m2 = self.mm_acc(self.psA[:, 512:1024],
                             prs2 + [(self.onesrow[0:1, 0:128], brow[0:1, 512:1024])],
                             sem=st)
            pv = self.psA[:, 0:1024].rearrange("p (i two) -> p i two", two=2)
            ev = pv[:, :, 0]
            od = pv[:, :, 1]
            rs = ropestg[:, par * 1024:(par + 1) * 1024]
            rv = rs.rearrange("p (i two) -> p i two", two=2)
            t1 = P.op("vector", "tensor_tensor", rtmp[:, 0:512], ev, cs[:, 0:512],
                      ALU.mult, sem=sv, deps=[m2, ls, self._rp_rsfree.get(ii - 2)])
            t2 = P.op("vector", "scalar_tensor_tensor", rtmp[:, 512:1024], od, -1.0,
                      cs[:, 512:1024], ALU.mult, ALU.mult, sem=sv)
            t3 = P.op("vector", "tensor_tensor", rv[:, :, 0], rtmp[:, 0:512],
                      rtmp[:, 512:1024], ALU.add, sem=sv, deps=[t1, t2])
            t4 = P.op("vector", "tensor_tensor", rtmp[:, 0:512], od, cs[:, 0:512],
                      ALU.mult, sem=sv, deps=[t3])
            t5 = P.op("vector", "tensor_tensor", rtmp[:, 512:1024], ev, cs[:, 512:1024],
                      ALU.mult, sem=sv)
            t6 = P.op("vector", "tensor_tensor", rv[:, :, 1], rtmp[:, 0:512],
                      rtmp[:, 512:1024], ALU.add, sem=sv, deps=[t4, t5])
            self._rp_csfree[ii] = t5
            self._rp_psfree[ii] = t5
            last = None
            for c in range(DC):
                last = P.op("tensor", "transpose", self.psT16[:, c * 128:(c + 1) * 128],
                            rs[:, c * 128:(c + 1) * 128], self.ident[:, :],
                            sem=(st if c == DC - 1 else None),
                            deps=([t6, self._rp_trfree.get(ii - 1), self.tok_consts]
                                  if c == 0 else ()))
            dstv = dstb[:, :].rearrange("p (m t) -> p m t", t=T)[:, :, tt * 128:(tt + 1) * 128]
            srcv = self.psT16[:, :].rearrange("p (c j) -> p c j", j=128)
            ta = P.op("vector", "tensor_tensor", dstv, dstv, srcv, ALU.add,
                      sem=sv, deps=[last])
            self._rp_trfree[ii] = ta
            self._rp_rsfree[ii] = last
        return ta

    def l0_attn(self, rope_cos, rope_sin, deps=()):
        P = self.P
        st, sv, sa = self.st, self.sv, self.sa
        self._rp_csfree, self._rp_psfree, self._rp_trfree, self._rp_rsfree = {}, {}, {}, {}
        scale = 1.0 / math.sqrt(16.0 + 512.0 + 512.0)
        es2 = ExitStack()
        with es2:
            sb2 = lambda name, shape, dt: es2.enter_context(self.nc.sbuf_tensor(name, shape, dt))
            Qb = self.sT
            Kb = sb2("Kb", [128, DC * T], BF16)
            Vt = sb2("Vt", [128, 16 * 16 * 65], BF16)
            tok_addq = tok_addk = tok_v = None
            with ExitStack() as s1:
                sc1 = lambda n, sh, dt: s1.enter_context(self.nc.sbuf_tensor(n, sh, dt))
                ctq = sc1("ctq", [128, 4 * T], BF16)
                ctkv = sc1("ctkv", [128, 4 * T], BF16)
                with ExitStack() as s1a:
                    sc1a = lambda n, sh, dt: s1a.enter_context(self.nc.sbuf_tensor(n, sh, dt))
                    wdq, t_wdq = self.load_w_nat("L0_attn_Wd_q_w", 1024, 512, sc1a, deps=[self.tok_xnT] + list(deps))
                    wdkv, t_wdkv = self.load_w_nat("L0_attn_Wd_kv_w", 1024, 512, sc1a, deps=[self.tok_xnT])
                    bdq, t_bdq = self.load_bias_cols2(sc1a, "L0_attn_Wd_q_b", 4, deps=[self.tok_xnT])
                    bdkv, t_bdkv = self.load_bias_cols2(sc1a, "L0_attn_Wd_kv_b", 4, deps=[self.tok_xnT])
                    P.wtok("scalar", t_bdq, t_bdkv)
                    tctq = self.proj_T(wdq, t_wdq, 8, ctq,
                                       lambda m, c: ctq[:, m * T + c * 512: m * T + c * 512 + 512],
                                       4, bdq, self.xnT, 0, extra_dep=self.tok_xnT)
                    tctkv = self.proj_T(wdkv, t_wdkv, 8, ctkv,
                                        lambda m, c: ctkv[:, m * T + c * 512: m * T + c * 512 + 512],
                                        4, bdkv, self.xnT, 0)
                    self.dbg("ctq", ctq[:, :], tctq)
                    self.dbg("ctkv", ctkv[:, :], tctkv)
                with ExitStack() as s1b:
                    sc1b = lambda n, sh, dt: s1b.enter_context(self.nc.sbuf_tensor(n, sh, dt))
                    wuq, t_wuq = self.load_w_nat("L0_attn_Wu_q_w", 512, 1024, sc1b, deps=[tctkv])
                    buq, t_buq = self.load_bias_cols2(sc1b, "L0_attn_Wu_q_b", 8, deps=[tctkv])
                    P.wtok("scalar", t_buq)
                    tq_c = self.proj_T(wuq, t_wuq, 4, Qb,
                                       lambda m, c: Qb[:, m * T + c * 512: m * T + c * 512 + 512],
                                       8, buq, ctq, 0, extra_dep=tctq)
                with ExitStack() as s1c:
                    sc1c = lambda n, sh, dt: s1c.enter_context(self.nc.sbuf_tensor(n, sh, dt))
                    wuk, t_wuk = self.load_w_nat("L0_attn_Wu_k_w", 512, 1024, sc1c, deps=[tq_c])
                    buk, t_buk = self.load_bias_cols2(sc1c, "L0_attn_Wu_k_b", 8, deps=[tq_c])
                    P.wtok("scalar", t_buk)
                    tk_c = self.proj_T(wuk, t_wuk, 4, Kb,
                                       lambda m, c: Kb[:, m * T + c * 512: m * T + c * 512 + 512],
                                       8, buk, ctkv, 0, extra_dep=tctkv)
                with ExitStack() as s1d:
                    sc1d = lambda n, sh, dt: s1d.enter_context(self.nc.sbuf_tensor(n, sh, dt))
                    wuv, t_wuv = self.load_w_nat("L0_attn_Wu_v_w", 512, 1024, sc1d, deps=[tk_c])
                    bvr, t_bvr = self.load_bias_row2(sc1d, "L0_attn_Wu_v_b", 1024, deps=[tk_c])
                    vones = Vt[:, :].rearrange("p (x c) -> p x c", c=65)[:, :, 64:65]
                    P.op("vector", "memset", vones, 1.0, sem=sv)
                    tok_vfree = None
                    for tt in range(TT):
                        for c2 in range(2):
                            prs = [(ctkv[:, k * T + tt * 128: k * T + tt * 128 + 128],
                                    wuv[:, k * 1024 + c2 * 512: k * 1024 + c2 * 512 + 512])
                                   for k in range(4)]
                            ex = [(self.onesrow[0:1, 0:128], bvr[0:1, c2 * 512:c2 * 512 + 512])]
                            tokmm = self.mm_acc(self.psC[:, :], prs + ex, sem=st,
                                                deps=[t_wuv, t_bvr, self.tok_ones, tctkv, tok_vfree])
                            for hh in range(8):
                                h = c2 * 8 + hh
                                tok_v = P.op("vector", "tensor_copy",
                                             Vt[:, (h * 16 + tt) * 65:(h * 16 + tt) * 65 + 64],
                                             self.psC[:, hh * 64: hh * 64 + 64],
                                             sem=sv, deps=([tokmm] if hh == 0 else ()))
                            tok_vfree = tok_v
                with ExitStack() as s1e:
                    sc1e = lambda n, sh, dt: s1e.enter_context(self.nc.sbuf_tensor(n, sh, dt))
                    wqr, t_wqr = self.load_w_nat("L0_attn_Wq_r_w", 512, 1024, sc1e, deps=[tok_v])
                    bqr, t_bqr = self.load_bias_row2(sc1e, "L0_attn_Wq_r_b", 1024, deps=[tok_v])
                    tok_addq = self._rope_stream(ctq, 4, wqr, t_wqr, bqr, t_bqr, Qb, tq_c,
                                                 rope_cos, rope_sin, 0, sc1e)
            self.dbg("Qb", Qb[:, :], tok_addq)
            with ExitStack() as s2:
                sc2 = lambda n, sh, dt: s2.enter_context(self.nc.sbuf_tensor(n, sh, dt))
                wkr, t_wkr = self.load_w_nat("L0_attn_Wk_r_w", 1024, 1024, sc2, deps=[tok_addq])
                bkr, t_bkr = self.load_bias_row2(sc2, "L0_attn_Wk_r_b", 1024, deps=[tok_addq])
                tok_addk = self._rope_stream(self.xnT, 8, wkr, t_wkr, bkr, t_bkr, Kb, tk_c,
                                             rope_cos, rope_sin, 1, sc2)
            self.dbg("Kb", Kb[:, :], tok_addk)
            self.dbg("Vt", Vt[:, :], tok_v)
            with ExitStack() as s3:
                sc3 = lambda n, sh, dt: s3.enter_context(self.nc.sbuf_tensor(n, sh, dt))
                attnb = sc3("attnb", [128, DC * T], BF16)
                s3b = ExitStack()
                sc3b = lambda n, sh, dt: s3b.enter_context(self.nc.sbuf_tensor(n, sh, dt))
                aqb = sc3b("aqb", [65, 2 * T], BF16)
                akb = sc3b("akb", [65, 2 * T], BF16)

                def augq(h, prev):
                    prev_attn = prev["attn"]
                    aq = aqb[:, (h % 2) * T:(h % 2 + 1) * T]
                    tok = P.op("vector", "tensor_scalar", aq[0:64, :],
                               Qb[64 * (h % 2):64 * (h % 2) + 64, (h // 2) * T:(h // 2 + 1) * T],
                               scale, None, ALU.mult, sem=sv,
                               deps=[tok_addq, tok_addk, prev_attn])
                    return aq, tok

                def augk(h, prev):
                    prev_attn = prev["attn"]
                    ak = akb[:, (h % 2) * T:(h % 2 + 1) * T]
                    P.op("vector", "tensor_copy", ak[0:64, :],
                         Kb[64 * (h % 2):64 * (h % 2) + 64, (h // 2) * T:(h // 2 + 1) * T],
                         sem=sv, deps=[prev_attn])
                    t2 = P.op("vector", "memset", ak[64:65, :], 1.0, sem=sv)
                    return ak, t2

                def vcol(h, kt):
                    return Vt[:, (h * 16 + kt) * 65:(h * 16 + kt) * 65 + 65]

                def attn_dst(h, qg):
                    return attnb[64 * (h % 2):64 * (h % 2) + 64,
                                 (h // 2) * T + qg * 512:(h // 2) * T + qg * 512 + 512]

                last_rp = self._rp_trfree.get(2 * TT - 1)
                tok_attn = self.attn_heads(sc3b, 16, augq, augk, vcol, attn_dst,
                                           lambda qt: T, causal=False,
                                           deps=[last_rp], pst_free=last_rp)
                self.dbg("attnb", attnb[:, :], tok_attn)
                s3b.close()
                with ExitStack() as s4:
                    sc4 = lambda n, sh, dt: s4.enter_context(self.nc.sbuf_tensor(n, sh, dt))
                    wo, t_wo = self.load_w_nat("L0_attn_Wo_w", 1024, 1024, sc4, deps=[tok_attn])
                    bo, t_bo = self.load_bias_cols2(sc4, "L0_attn_Wo_b", 8, deps=[tok_attn])
                    P.wtok("scalar", t_bo)
                    tok_s = self.proj_T(wo, t_wo, 8, self.sT,
                                        lambda m, c: self.sT[:, m * T + c * 512: m * T + c * 512 + 512],
                                        8, bo, attnb, 0, extra_dep=tok_attn)
        self.tok_sT_attn = tok_s
        return tok_s

    def layer0(self, x_src, x_dst, rope_cos, rope_sin, deps=()):
        self.ln_phase(x_src, deps=deps)
        ta = self.l0_attn(rope_cos, rope_sin, deps=deps)
        tf = self.l0_ffn(sT_ready=ta, deps=[ta])
        return self.residual_phase(x_src, x_dst, deps=[tf])

    def l2_ffn(self, sT_ready=None, deps=()):
        P = self.P
        st, sv, sa = self.st, self.sv, self.sa
        NI = 2816
        NKI = NI // 128
        E = 7
        sg_bf, t_sg = self.cast_weight_to_dram("L2_ffn_sg")
        su_bf, t_su = self.cast_weight_to_dram("L2_ffn_su")
        sd_bf, t_sd = self.cast_weight_to_dram("L2_ffn_sd")
        rg_bf, t_rg = self.cast_weight_to_dram("L2_ffn_rg")
        ru_bf, t_ru = self.cast_weight_to_dram("L2_ffn_ru")
        rd_bf, t_rd = self.cast_weight_to_dram("L2_ffn_rd")
        acc_dram = self.nc.dram_tensor("l2_acc", [D, T], F32)
        es2 = ExitStack()
        tok_gate = None
        with es2:
            sb2 = lambda n, sh, dt: es2.enter_context(self.nc.sbuf_tensor(n, sh, dt))
            self.cw_tm = sb2("l2_cw", [128, TT * 7], F32)
            self.cwT2 = sb2("l2_cwT", [1, 7 * T], BF16)
            with ExitStack() as sr:
                sbr = lambda n, sh, dt: sr.enter_context(self.nc.sbuf_tensor(n, sh, dt))
                wrb = sbr("l2_wr", [7, 1024], BF16)
                t_wr = P.dma("gpsimd", wrb[:, :], self.w["L2_ffn_wr"][:, :],
                             self.stream("l2wr"), deps=list(deps))
                rbb = sbr("l2_rb", [1, 7], BF16)
                t_rb = P.dma("gpsimd", rbb[0:1, :],
                             self.w["L2_ffn_rb"].ap().rearrange("(o n) -> o n", o=1),
                             self.stream("l2rb"), deps=list(deps))
                wrT = sbr("l2_wrT", [128, 8 * 7], BF16)
                tw = None
                for k in range(8):
                    tt1 = P.op("tensor", "transpose", self.psT16[:, 0:7],
                               wrb[0:7, k * 128:(k + 1) * 128], self.ident[0:7, 0:7],
                               sem=st, deps=[t_wr, tw, self.tok_consts])
                    tw = P.op("vector", "tensor_copy", wrT[:, k * 7:(k + 1) * 7],
                              self.psT16[:, 0:7], sem=sv, deps=[tt1])
                lgs = sbr("l2_lg", [128, 7], F32)
                sts = sbr("l2_sts", [128, 8], F32)
                cwbf_t = sbr("l2_cwbf", [128, 7], BF16)
                tok_cwT = None
                tok_lg = None
                for tt in range(TT):
                    prs = [(self.xnT[:, k * T + tt * 128: k * T + tt * 128 + 128],
                            wrT[:, k * 7:(k + 1) * 7]) for k in range(8)]
                    ex = [(self.onesrow[0:1, 0:128], rbb[0:1, :])]
                    tmm = self.mm_acc(self.psB[:, 0:7], prs + ex, sem=st,
                                      deps=[tw, t_rb, self.tok_xnT, self.tok_ones, tok_lg]
                                      + (list(deps) if tt == 0 else []))
                    tok_lg = P.op("vector", "tensor_copy", lgs[:, :], self.psB[:, 0:7],
                                  sem=sv, deps=[tmm])
                    m1 = sts[:, 0:1]; m2 = sts[:, 1:2]; dd = sts[:, 2:3]
                    w1 = sts[:, 3:4]; w2 = sts[:, 4:5]
                    t1 = P.op("vector", "tensor_reduce", m1, lgs[:, :], AX.X, ALU.max,
                              sem=sv, deps=[tok_lg])
                    cw = self.cw_tm[:, tt * 7:(tt + 1) * 7]
                    t2 = P.op("vector", "tensor_scalar", cw, lgs[:, :], m1, None,
                              ALU.is_equal, sem=sv, deps=[t1])
                    t3 = P.op("vector", "scalar_tensor_tensor", lgs[:, :], cw, NEG,
                              lgs[:, :], ALU.mult, ALU.add, sem=sv, deps=[t2])
                    t4 = P.op("vector", "tensor_reduce", m2, lgs[:, :], AX.X, ALU.max,
                              sem=sv, deps=[t3])
                    t5 = P.op("vector", "tensor_scalar", lgs[:, :], lgs[:, :], m2, None,
                              ALU.is_equal, sem=sv, deps=[t4])
                    t6 = P.op("vector", "tensor_tensor", dd, m1, m2, ALU.subtract,
                              sem=sv, deps=[t5])
                    t7 = P.op("scalar", "activation", w1, dd, AF.Sigmoid, sem=sa, deps=[t6])
                    t8 = P.op("scalar", "activation", w2, dd, AF.Sigmoid, scale=-1.0,
                              sem=sa, deps=[t7])
                    t9 = P.op("vector", "tensor_scalar", cw, cw, w1, None, ALU.mult,
                              sem=sv, deps=[t8])
                    t10 = P.op("vector", "scalar_tensor_tensor", cw, lgs[:, :], w2,
                               cw, ALU.mult, ALU.add, sem=sv, deps=[t9])
                    cwb = sts[:, 5:6]
                    for e in range(E):
                        tcb = P.op("vector", "tensor_copy", cwbf_t[:, e:e + 1], cw[:, e:e + 1],
                                   sem=sv, deps=([t10] if e == 0 else ()))
                        tte = P.op("tensor", "transpose", self.psT16[0:1, 0:128],
                                   cwbf_t[:, e:e + 1], self.ident[:, :], sem=st,
                                   deps=[tcb, tok_cwT])
                        tok_cwT = P.op("scalar", "copy",
                                       self.cwT2[0:1, e * T + tt * 128: e * T + tt * 128 + 128],
                                       self.psT16[0:1, 0:128], sem=sa, deps=[tte])
            se = ExitStack()
            sb3 = lambda n, sh, dt: se.enter_context(self.nc.sbuf_tensor(n, sh, dt))
            wdT = sb3("l2_wdT", [128, NKI * 1024], BF16)
            wgb = sb3("l2_wgb", [128, 2 * 8 * 512], BF16)
            wub = sb3("l2_wub", [128, 2 * 8 * 512], BF16)
            hbuf = sb3("l2_h", [128, NKI * 512], BF16)
            gstg = sb3("l2_gs", [128, 2 * 512], BF16)
            dstg = sb3("l2_ds", [128, 2 * 512], F32)
            cwbc = sb3("l2_cwbc", [128, T], BF16)
            wd_str = self.stream("l2wd")
            wg_str = [self.stream("l2wg0"), self.stream("l2wg1")]
            ICH = [512, 512, 512, 512, 512, 256]
            psB_free = psC_free = psD_free = None
            gsl_free = {}
            dsl_free = {}
            wpar_free = {}
            down_done = None
            wd_free = None
            tok_bc = None
            gidx = didx = 0
            accst2 = [self.stream("l2acc0"), self.stream("l2acc1")]
            for ei in range(E + 1):
                routed = ei > 0
                e = ei - 1
                wdep = [t_sd if not routed else t_rd, down_done]
                if ei == 0:
                    wdep += [tok_cwT] + list(deps)
                for j in range(NKI):
                    src = (sd_bf[:, j * 128:(j + 1) * 128] if not routed
                           else rd_bf[e, :, j * 128:(j + 1) * 128])
                    wd_ld = P.dma("sync", wdT[:, j * 1024:(j + 1) * 1024], src, wd_str,
                                  deps=(wdep if j == 0 else ()), transpose=True)
                if routed:
                    bcs = []
                    for c in range(4):
                        tb = P.op("tensor", "matmul", self.psA[:, c * 512:(c + 1) * 512],
                                  self.onesrow[0:1, 0:128],
                                  self.cwT2[0:1, e * T + c * 512: e * T + c * 512 + 512],
                                  start=True, stop=True, sem=st,
                                  deps=[tok_cwT, tok_bc] + (list(deps) if ei == 1 and c == 0 else []),
                                  skip_group_check=True)
                        bcs.append(P.op("scalar", "activation", cwbc[:, c * 512:(c + 1) * 512],
                                        self.psA[:, c * 512:(c + 1) * 512], AF.Identity,
                                        sem=sa, deps=[tb]))
                    tok_bc = bcs[-1]
                for nc4 in range(4):
                    i0k = 0
                    for ci, IC in enumerate(ICH):
                        par = ci % 2
                        nk = IC // 128
                        pi = (ei * 4 + nc4) * 6 + ci
                        wg_sl = wgb[:, par * 4096: par * 4096 + 8 * IC]
                        wu_sl = wub[:, par * 4096: par * 4096 + 8 * IC]
                        i0 = sum(ICH[:ci])
                        ldp = [t_sg if not routed else t_rg,
                               t_su if not routed else t_ru, wpar_free.get(pi - 2)]
                        for dk in range(8):
                            src = (sg_bf[i0:i0 + IC, dk * 128:(dk + 1) * 128] if not routed
                                   else rg_bf[e, i0:i0 + IC, dk * 128:(dk + 1) * 128])
                            lg = P.dma("sync", wg_sl[:, dk * IC:(dk + 1) * IC], src,
                                       wg_str[par], deps=(ldp if dk == 0 else ()),
                                       transpose=True)
                        for dk in range(8):
                            src = (su_bf[i0:i0 + IC, dk * 128:(dk + 1) * 128] if not routed
                                   else ru_bf[e, i0:i0 + IC, dk * 128:(dk + 1) * 128])
                            lu = P.dma("sync", wu_sl[:, dk * IC:(dk + 1) * IC], src,
                                       wg_str[par], transpose=True)
                        for mt in range(nk):
                            gp = gidx % 2
                            gsl = gstg[:, gp * 512:(gp + 1) * 512]
                            xsl = lambda dk: self.xnT[:, dk * T + nc4 * 512: dk * T + nc4 * 512 + 512]
                            prsg = [(wg_sl[:, dk * IC + mt * 128: dk * IC + mt * 128 + 128], xsl(dk))
                                    for dk in range(8)]
                            g_tok = self.mm_acc(self.psB[:, :], prsg, sem=st,
                                                deps=[lu, self.tok_xnT, psB_free]
                                                + (list(deps) if gidx == 0 else []))
                            sil = P.op("scalar", "activation", gsl, self.psB[:, :], AF.Silu,
                                       sem=sa, deps=[g_tok, gsl_free.get(gidx - 2)])
                            psB_free = sil
                            if routed:
                                sil = P.op("vector", "tensor_tensor", gsl, gsl,
                                           cwbc[:, nc4 * 512:(nc4 + 1) * 512], ALU.mult,
                                           sem=sv, deps=[sil, tok_bc])
                            prsu = [(wu_sl[:, dk * IC + mt * 128: dk * IC + mt * 128 + 128], xsl(dk))
                                    for dk in range(8)]
                            u_tok = self.mm_acc(self.psC[:, :], prsu, sem=st,
                                                deps=[psC_free] + (list(deps) if gidx == 0 else []))
                            kh = i0k + mt
                            hdeps = [u_tok, sil]
                            if kh == 0:
                                hdeps.append(down_done)
                            hv = P.op("vector", "tensor_tensor",
                                      hbuf[:, kh * 512:(kh + 1) * 512], self.psC[:, :], gsl,
                                      ALU.mult, sem=sv, deps=hdeps)
                            psC_free = hv
                            gsl_free[gidx] = hv
                            gidx += 1
                        wpar_free[pi] = u_tok
                        i0k += nk
                    for dm in range(8):
                        prd = [(wdT[:, kh * 1024 + dm * 128: kh * 1024 + dm * 128 + 128],
                                hbuf[:, kh * 512:(kh + 1) * 512]) for kh in range(NKI)]
                        d_tok = self.mm_acc(self.psD[:, :], prd, sem=st,
                                            deps=[hv, wd_ld, psD_free]
                                            + (list(deps) if didx == 0 else []))
                        dp = didx % 2
                        dsl = dstg[:, dp * 512:(dp + 1) * 512]
                        ev = P.op("scalar", "activation", dsl, self.psD[:, :], AF.Identity,
                                  sem=sa, deps=[d_tok, dsl_free.get(didx - 2)])
                        psD_free = ev
                        stw = P.dma("gpsimd", acc_dram[dm * 128:(dm + 1) * 128,
                                                       nc4 * 512:(nc4 + 1) * 512], dsl,
                                    accst2[dp], deps=[ev],
                                    accum_op=(ALU.add if routed else ALU.bypass))
                        dsl_free[didx] = stw
                        didx += 1
                    down_done = d_tok
            se.close()
            accrd = sb2("l2_accrd", [128, 2 * 2048], F32)
            gbf = sb2("l2_gbf", [128, 2 * 2048], BF16)
            rd_str2 = [self.stream("l2accr0"), self.stream("l2accr1")]
            rfree = {}
            mv = None
            for dm in range(8):
                par = dm % 2
                arr = accrd[:, par * 2048:(par + 1) * 2048]
                gb = gbf[:, par * 2048:(par + 1) * 2048]
                ld = P.dma("sync", arr, acc_dram[dm * 128:(dm + 1) * 128, :], rd_str2[par],
                           deps=[rfree.get(dm - 2), (accst2[0], accst2[0].n),
                                 (accst2[1], accst2[1].n)])
                sg2 = P.op("scalar", "activation", gb, arr, AF.Silu, sem=sa,
                           deps=[ld, rfree.get(dm - 2)])
                mdeps = [sg2]
                if dm == 0 and sT_ready is not None:
                    mdeps.append(sT_ready)
                mv = P.op("vector", "tensor_tensor", self.sT[:, dm * T:(dm + 1) * T],
                          self.sT[:, dm * T:(dm + 1) * T], gb, ALU.mult, sem=sv,
                          deps=mdeps)
                rfree[dm] = mv
        self.tok_sT = mv
        return mv

    def l2_attn(self, deps=()):
        P = self.P
        st, sv, sa = self.st, self.sv, self.sa
        es2 = ExitStack()
        with es2:
            sb2 = lambda n, sh, dt: es2.enter_context(self.nc.sbuf_tensor(n, sh, dt))
            Qb = self.sT
            Kb = sb2("l2Kb", [128, DC * T], BF16)
            Vt = sb2("l2Vt", [128, 16 * 16 * 65], BF16)
            part_done = None
            for part, dstb in enumerate([Qb, Kb, None]):
                with ExitStack() as s1:
                    sc1 = lambda n, sh, dt: s1.enter_context(self.nc.sbuf_tensor(n, sh, dt))
                    wp = sc1(f"l2wqkv{part}", [128, 8 * 1024], BF16)
                    strm = self.stream(f"l2wqkv{part}")
                    tw = None
                    for k in range(8):
                        tw = P.dma("gpsimd", wp[:, k * 1024:(k + 1) * 1024],
                                   self.w["L2_attn_W_qkv"][k * 128:(k + 1) * 128,
                                                           part * 1024:(part + 1) * 1024],
                                   strm, deps=(([self.tok_xnT, part_done] + list(deps)) if k == 0 else ()))
                    if part < 2:
                        part_done = self.proj_T(wp, tw, 8, dstb,
                                    lambda m, c, dstb=dstb: dstb[:, m * T + c * 512: m * T + c * 512 + 512],
                                    8, None, self.xnT, 0, extra_dep=self.tok_xnT)
                    else:
                        vones = Vt[:, :].rearrange("p (x c) -> p x c", c=65)[:, :, 64:65]
                        P.op("vector", "memset", vones, 1.0, sem=sv)
                        tok_vfree = None
                        for tt in range(TT):
                            for c2 in range(2):
                                prs = [(self.xnT[:, k * T + tt * 128: k * T + tt * 128 + 128],
                                        wp[:, k * 1024 + c2 * 512: k * 1024 + c2 * 512 + 512])
                                       for k in range(8)]
                                tokmm = self.mm_acc(self.psC[:, :], prs, sem=st,
                                                    deps=[tw, self.tok_xnT, tok_vfree])
                                for hh in range(8):
                                    h = c2 * 8 + hh
                                    tok_v = P.op("vector", "tensor_copy",
                                                 Vt[:, (h * 16 + tt) * 65:(h * 16 + tt) * 65 + 64],
                                                 self.psC[:, hh * 64: hh * 64 + 64],
                                                 sem=sv, deps=([tokmm] if hh == 0 else ()))
                                tok_vfree = tok_v
            wql = sb2("l2wql", [128, 16 * 16], BF16)
            wkl = sb2("l2wkl", [128, 16 * 16], BF16)
            corec = sb2("l2core", [16, 16], F32)
            qlst = self.stream("l2qlsr")
            twl = None
            for h in range(16):
                for half in range(2):
                    twl = P.dma("gpsimd", wql[64 * half:64 * half + 64, h * 16:(h + 1) * 16],
                                self.w["L2_attn_W_q_lsr"][h, :, :], qlst,
                                deps=([tok_v] if (h == 0 and half == 0) else ()))
            for h in range(16):
                for half in range(2):
                    twl = P.dma("gpsimd", wkl[64 * half:64 * half + 64, h * 16:(h + 1) * 16],
                                self.w["L2_attn_W_k_lsr"][h, :, :], qlst)
            tco = P.dma("sync", corec[:, :],
                        self.w["L2_attn_core"].ap().rearrange("h r -> r h"),
                        self.stream("l2core"), deps=[tok_v],
                        allow_slow_non_contiguous=True)
            tcs = P.op("vector", "tensor_scalar", corec[:, :], corec[:, :], 0.25, None,
                       ALU.mult, sem=sv, deps=[tco])
            with ExitStack() as s3:
                sc3 = lambda n, sh, dt: s3.enter_context(self.nc.sbuf_tensor(n, sh, dt))
                attnb = sc3("l2attnb", [128, DC * T], BF16)
                s3b = ExitStack()
                sc3b = lambda n, sh, dt: s3b.enter_context(self.nc.sbuf_tensor(n, sh, dt))
                aqb = sc3b("l2aqb", [65, 2 * T], BF16)
                akb = sc3b("l2akb", [65, 2 * T], BF16)

                qev = [None]

                def lr_fill(h, prev, wl, srcb, dst, is_q):
                    zz = P.op("vector", "memset", dst[0:65, :], 0.0, sem=sv,
                              deps=[prev["attn"], prev["exp"] if not is_q else None])
                    mms = []
                    hb = 64 * (h % 2)
                    for c in range(4):
                        mm = P.op("tensor", "matmul", self.psA[0:16, c * 512:(c + 1) * 512],
                                  wl[hb:hb + 64, h * 16:(h + 1) * 16],
                                  srcb[hb:hb + 64,
                                       (h // 2) * T + c * 512:(h // 2) * T + c * 512 + 512],
                                  start=True, stop=True, sem=st,
                                  deps=([twl, tcs, prev["red"], zz,
                                         None if is_q else qev[0]] if c == 0 else ()),
                                  skip_group_check=True)
                        mms.append(mm)
                    kw = {}
                    if is_q:
                        kw["scale"] = corec[:, h:h + 1]
                    ev = P.op("scalar", "activation", dst[0:16, :], self.psA[0:16, :],
                              AF.Identity, sem=sa, deps=[mms[-1]], **kw)
                    if is_q:
                        qev[0] = ev
                    return ev

                def augq(h, prev):
                    aq = aqb[:, (h % 2) * T:(h % 2 + 1) * T]
                    return aq, lr_fill(h, prev, wql, Qb, aq, True)

                def augk(h, prev):
                    ak = akb[:, (h % 2) * T:(h % 2 + 1) * T]
                    ev = lr_fill(h, prev, wkl, Kb, ak, False)
                    t2 = P.op("vector", "memset", ak[64:65, :], 1.0, sem=sv, deps=[ev])
                    return ak, t2

                def vcol(h, kt):
                    return Vt[:, (h * 16 + kt) * 65:(h * 16 + kt) * 65 + 65]

                def attn_dst(h, qg):
                    return attnb[64 * (h % 2):64 * (h % 2) + 64,
                                 (h // 2) * T + qg * 512:(h // 2) * T + qg * 512 + 512]

                tok_attn = self.attn_heads(sc3b, 16, augq, augk, vcol, attn_dst,
                                           lambda qt: (qt + 1) * 128, causal=True,
                                           deps=[tok_v], pst_free=None)
                s3b.close()
                with ExitStack() as s4:
                    sc4 = lambda n, sh, dt: s4.enter_context(self.nc.sbuf_tensor(n, sh, dt))
                    wo, t_wo = self.load_w_nat("L2_attn_W_o", 1024, 1024, sc4, deps=[tok_attn])
                    tok_s = self.proj_T(wo, t_wo, 8, self.sT,
                                        lambda m, c: self.sT[:, m * T + c * 512: m * T + c * 512 + 512],
                                        8, None, attnb, 0, extra_dep=tok_attn)
        self.tok_sT_attn = tok_s
        return tok_s

    def layer2(self, x_src, x_dst, deps=()):
        self.ln_phase(x_src, deps=deps)
        ta = self.l2_attn(deps=deps)
        tf = self.l2_ffn(sT_ready=ta, deps=[ta])
        return self.residual_phase(x_src, x_dst, deps=[tf])

    def load_w_trans(self, name, R, C, scope_sb, deps=(), tag=""):
        bfh, tcast = self.cast_weight_to_dram(name)
        kt = C // 128
        sb = scope_sb("wt_" + name + tag, [128, kt * R], BF16)
        strm = self.stream("wtl")
        for k in range(kt):
            self.P.dma("sync", sb[:, k * R:(k + 1) * R],
                       bfh[:, k * 128:(k + 1) * 128], strm,
                       deps=(list(deps) + [tcast] if k == 0 else ()),
                       transpose=True)
        return sb, (strm, None)

    def l1_attn(self, f_cos, f_sin, deps=()):
        P = self.P
        st, sv, sa = self.st, self.sv, self.sa
        es2 = ExitStack()
        with es2:
            sb2 = lambda n, sh, dt: es2.enter_context(self.nc.sbuf_tensor(n, sh, dt))
            Qb = self.sT
            Kb = sb2("l1Kb", [128, DC * T], BF16)
            Vt = sb2("l1Vt", [128, 16 * 16 * 65], BF16)
            wq_bf, t_wq = self.cast_weight_to_dram("L1_attn_Wqkv")
            part_done = None
            for part, dstb in enumerate([Qb, Kb, None]):
                with ExitStack() as s1:
                    sc1 = lambda n, sh, dt: s1.enter_context(self.nc.sbuf_tensor(n, sh, dt))
                    wp = sc1(f"l1wqkv{part}", [128, 8 * 1024], BF16)
                    strm = self.stream(f"l1wqkv{part}")
                    tw = None
                    for k in range(8):
                        tw = P.dma("sync", wp[:, k * 1024:(k + 1) * 1024],
                                   wq_bf[part * 1024:(part + 1) * 1024, k * 128:(k + 1) * 128],
                                   strm, transpose=True,
                                   deps=(([t_wq, self.tok_xnT, part_done] + list(deps))
                                         if k == 0 else ()))
                    if part < 2:
                        part_done = self.proj_T(
                            wp, tw, 8, dstb,
                            lambda m, c, dstb=dstb: dstb[:, m * T + c * 512: m * T + c * 512 + 512],
                            8, None, self.xnT, 0, extra_dep=self.tok_xnT,
                            scale=(0.125 if part == 0 else None))
                    else:
                        vones = Vt[:, :].rearrange("p (x c) -> p x c", c=65)[:, :, 64:65]
                        P.op("vector", "memset", vones, 1.0, sem=sv)
                        tok_vfree = None
                        for tt in range(TT):
                            for c2 in range(2):
                                prs = [(self.xnT[:, k * T + tt * 128: k * T + tt * 128 + 128],
                                        wp[:, k * 1024 + c2 * 512: k * 1024 + c2 * 512 + 512])
                                       for k in range(8)]
                                tokmm = self.mm_acc(self.psC[:, :], prs, sem=st,
                                                    deps=[tw, self.tok_xnT, tok_vfree])
                                for hh in range(8):
                                    h = c2 * 8 + hh
                                    tok_v = P.op("vector", "tensor_copy",
                                                 Vt[:, (h * 16 + tt) * 65:(h * 16 + tt) * 65 + 64],
                                                 self.psC[:, hh * 64: hh * 64 + 64],
                                                 sem=sv, deps=([tokmm] if hh == 0 else ()))
                                tok_vfree = tok_v
            cs32 = sb2("l1cs32", [32, T], BF16)
            sn32 = sb2("l1sn32", [32, T], BF16)
            Kf = sb2("l1Kf", [65, T], BF16)
            ab = sb2("l1ab", [32, 3 * 16], F32)
            fst = self.stream("l1f")
            t1 = P.dma("gpsimd", cs32[:, :], f_cos[:, :], self.stream("l1fg"), deps=[tok_v])
            t2 = P.dma("gpsimd", sn32[:, :], f_sin[:, :], self.stream("l1fg"))
            tk1 = P.dma("gpsimd", Kf[0:32, :], f_cos[:, :], self.stream("l1fk"), deps=[tok_v])
            tk2 = P.dma("gpsimd", Kf[32:64, :], f_sin[:, :], self.stream("l1fk"))
            tk3 = P.op("vector", "memset", Kf[64:65, :], 1.0, sem=sv, deps=[tk2])
            ta1 = P.dma("sync", ab[:, 0:16],
                        self.w["L1_attn_alpha"].ap().rearrange("h r -> r h"), fst,
                        deps=[tok_v], allow_slow_non_contiguous=True)
            ta2 = P.dma("sync", ab[:, 16:32],
                        self.w["L1_attn_beta"].ap().rearrange("h r -> r h"), fst,
                        allow_slow_non_contiguous=True)
            tneg = P.op("vector", "tensor_scalar", ab[:, 32:48], ab[:, 16:32], -1.0, None,
                        ALU.mult, sem=sv, deps=[ta2, (fst, fst.n)])
            with ExitStack() as s3:
                sc3 = lambda n, sh, dt: s3.enter_context(self.nc.sbuf_tensor(n, sh, dt))
                attnb = sc3("l1attnb", [128, DC * T], BF16)
                s3b = ExitStack()
                sc3b = lambda n, sh, dt: s3b.enter_context(self.nc.sbuf_tensor(n, sh, dt))
                Qf = sc3b("l1Qf", [65, 2 * T], BF16)
                ptb = sc3b("l1pt", [128, 8 * 512], BF16)
                ftmp = ptb[0:32, 0:2048]
                aq0 = sc3b("l1aq0", [64, T], BF16)
                ak0 = sc3b("l1ak0", [64, T], BF16)

                def augq(h, prev):
                    par = h % 2
                    qf = Qf[:, par * T:(par + 1) * T]
                    d1 = [prev["attn"], tneg, t2]
                    x1 = P.op("vector", "tensor_scalar", ftmp, cs32[:, :],
                              ab[:, h:h + 1], None, ALU.mult, sem=sv, deps=d1)
                    x2 = P.op("vector", "scalar_tensor_tensor", qf[0:32, :], sn32[:, :],
                              ab[:, 16 + h:17 + h], ftmp, ALU.mult, ALU.add,
                              sem=sv, deps=[x1])
                    x3 = P.op("vector", "tensor_scalar", ftmp, sn32[:, :],
                              ab[:, h:h + 1], None, ALU.mult, sem=sv, deps=[x2])
                    x4 = P.op("vector", "scalar_tensor_tensor", qf[32:64, :], cs32[:, :],
                              ab[:, 32 + h:33 + h], ftmp, ALU.mult, ALU.add,
                              sem=sv, deps=[x3])
                    aq = aq0[:, 0:T]
                    cq = P.op("vector", "tensor_copy", aq[0:64, :],
                              Qb[64 * (h % 2):64 * (h % 2) + 64, (h // 2) * T:(h // 2 + 1) * T],
                              sem=sv, deps=[x4, prev["attn"]])
                    return _PlainAug(aq, qf), cq

                def augk(h, prev):
                    ak = ak0[:, 0:T]
                    ck = P.op("vector", "tensor_copy", ak[0:64, :],
                              Kb[64 * (h % 2):64 * (h % 2) + 64, (h // 2) * T:(h // 2 + 1) * T],
                              sem=sv, deps=[prev["attn"], tk3 if h == 0 else None])
                    return _PlainAug(ak, Kf), ck

                def sT_extra(h):
                    par = h % 2
                    qf = Qf[:, par * T:(par + 1) * T]
                    return (lambda kt: Kf[0:65, kt * 128:(kt + 1) * 128],
                            lambda qg: qf[0:65, qg * 512:(qg + 1) * 512])

                def qk_extra(h):
                    par = h % 2
                    qf = Qf[:, par * T:(par + 1) * T]
                    return (lambda qt: qf[0:64, qt * 128:(qt + 1) * 128],
                            lambda c, k1: Kf[0:64, c * 512:c * 512 + k1])

                def vcol(h, kt):
                    return Vt[:, (h * 16 + kt) * 65:(h * 16 + kt) * 65 + 65]

                def attn_dst(h, qg):
                    return attnb[64 * (h % 2):64 * (h % 2) + 64,
                                 (h // 2) * T + qg * 512:(h // 2) * T + qg * 512 + 512]

                tok_attn = self.attn_heads(sc3b, 16, augq, augk, vcol, attn_dst,
                                           lambda qt: (qt + 1) * 128, causal=True,
                                           sT_extra=sT_extra, qk_extra=qk_extra,
                                           deps=[tok_v], pst_free=None, pt_buf=ptb)
                s3b.close()
                with ExitStack() as s4:
                    sc4 = lambda n, sh, dt: s4.enter_context(self.nc.sbuf_tensor(n, sh, dt))
                    wo, t_wo = self.load_w_trans("L1_attn_Wout", 1024, 1024, sc4,
                                                 deps=[tok_attn])
                    tok_s = self.proj_T(wo, t_wo, 8, self.sT,
                                        lambda m, c: self.sT[:, m * T + c * 512: m * T + c * 512 + 512],
                                        8, None, attnb, 0, extra_dep=tok_attn)
        self.tok_sT_attn = tok_s
        return tok_s

    def l1_ffn(self, replica_groups, sT_ready=None, deps=()):
        P = self.P
        st, sv, sa = self.st, self.sv, self.sa
        NS = 2048
        lg_dram = self.nc.dram_tensor("l1_logits", [T, NS], BF16)
        d_dram = self.nc.dram_tensor("l1_D", [T, NS], BF16)
        c_dram = self.nc.dram_tensor("l1_C", [T, NS], BF16)
        xp_dram = self.nc.dram_tensor("l1_xpart", [NS, D], F32)
        xr_dram = self.nc.dram_tensor("l1_xred", [NS, D], F32)
        self._l1_xr = xr_dram
        self._l1_c = c_dram
        self._l1ps_free = None
        self._l1_ttfree = None
        self._l1xp_psfree = None
        es2 = ExitStack()
        with es2:
            sb2 = lambda n, sh, dt: es2.enter_context(self.nc.sbuf_tensor(n, sh, dt))
            phib = sb2("l1_phib", [1, NS], BF16)
            tpb = P.dma("gpsimd", phib[0:1, :],
                        self.w["L1_ffn_phi_b"].ap().rearrange("(o n) -> o n", o=1),
                        self.stream("l1pb"), deps=list(deps))
            phi_bf, t_phi = self.cast_weight_to_dram("L1_ffn_phi_w")
            lgst = [self.stream("l1lg0"), self.stream("l1lg1")]
            with ExitStack() as sA:
                scA = lambda n, sh, dt: sA.enter_context(self.nc.sbuf_tensor(n, sh, dt))
                phT = scA("l1_phT", [128, 2 * 8 * 512], BF16)
                lstg = scA("l1_lstg", [128, 2 * 512], BF16)
                phst = [self.stream("l1ph0"), self.stream("l1ph1")]
                wfree = {}
                lfree = {}
                tok_lg = None
                for sc in range(4):
                    par = sc % 2
                    ph = phT[:, par * 4096:(par + 1) * 4096]
                    tw = None
                    for k in range(8):
                        tw = P.dma("sync", ph[:, k * 512:(k + 1) * 512],
                                   phi_bf[sc * 512:(sc + 1) * 512, k * 128:(k + 1) * 128],
                                   phst[par], transpose=True,
                                   deps=(([t_phi, self.tok_xnT, wfree.get(sc - 2)] + list(deps))
                                         if k == 0 else ()))
                    for tt in range(TT):
                        ii = sc * TT + tt
                        lp = ii % 2
                        ls = lstg[:, lp * 512:(lp + 1) * 512]
                        prs = [(self.xnT[:, k * T + tt * 128: k * T + tt * 128 + 128],
                                ph[:, k * 512:(k + 1) * 512]) for k in range(8)]
                        ex = [(self.onesrow[0:1, 0:128], phib[0:1, sc * 512:(sc + 1) * 512])]
                        mm = self.mm_acc(self.psB[:, :], prs + ex, sem=st,
                                         deps=[tw, tpb, self.tok_ones, self.tok_xnT,
                                               self._l1ps_free]
                                         + (list(deps) if (sc == 0 and tt == 0) else []))
                        ev = P.op("scalar", "activation", ls, self.psB[:, :], AF.Identity,
                                  sem=sa, deps=[mm, lfree.get(ii - 2)])
                        self._l1ps_free = ev
                        stw = P.dma("sync", lg_dram[tt * 128:(tt + 1) * 128,
                                                    sc * 512:(sc + 1) * 512], ls,
                                    lgst[lp], deps=[ev])
                        lfree[ii] = stw
                        tok_lg = stw
                    wfree[sc] = mm
            with ExitStack() as sB:
                scB = lambda n, sh, dt: sB.enter_context(self.nc.sbuf_tensor(n, sh, dt))
                xn_tm = scB("l1_xntm", [128, TT * 1024], BF16)
                ttok = None
                etok = None
                for c in range(DC):
                    for t in range(TT):
                        tdeps = [self._l1_ttfree, sT_ready] if (t % 8 == 0) else []
                        if c == 0 and t == 0:
                            tdeps += [(lgst[0], lgst[0].n), (lgst[1], lgst[1].n)]
                        ttok = P.op("tensor", "transpose",
                                    self.psT16[:, (t % 8) * 128:(t % 8 + 1) * 128],
                                    self.xnT[:, c * T + t * 128: c * T + t * 128 + 128],
                                    self.ident[:, :],
                                    sem=(st if t % 8 == 7 else None),
                                    deps=tdeps)
                        if t % 8 == 7:
                            dst = xn_tm[:, :].rearrange("p (u j) -> p u j", j=1024)[
                                :, t - 7:t + 1, c * 128:(c + 1) * 128]
                            src8 = self.psT16[:, :].rearrange("p (u j) -> p u j", j=128)
                            etok = P.op("vector", "tensor_copy", dst, src8, sem=sv,
                                        deps=[ttok])
                            self._l1_ttfree = etok
                Ebuf = scB("l1_E", [128, NS], BF16)
                lrow = scB("l1_lrow", [128, NS], BF16)
                sds = scB("l1_sds", [128, 64 + 64], F32)
                scs = scB("l1_scs", [128, 32 + 32], F32)
                dcst = [self.stream("l1dc0"), self.stream("l1dc1")]
                ldst = [self.stream("l1ld0"), self.stream("l1ld1")]
                dfree = {}
                tok_dc = None
                prev_std = None
                prev_stc = None
                for tt in range(TT):
                    lddeps = [prev_std]
                    if tt == 0:
                        lddeps += [(lgst[0], lgst[0].n), (lgst[1], lgst[1].n)]
                    ld = P.dma("sync", lrow[:, :],
                               lg_dram[tt * 128:(tt + 1) * 128, :], ldst[0],
                               deps=lddeps)
                    ex1 = P.op("scalar", "activation", Ebuf[:, :], lrow[:, :], AF.Exp,
                               sem=sa, deps=[ld, prev_stc])
                    eV = Ebuf[:, :].rearrange("p (n q) -> p n q", q=32)
                    sD = sds[:, 0:64]
                    rD = sds[:, 64:128]
                    s1 = P.op("vector", "tensor_reduce", sD, eV, AX.X, ALU.add, sem=sv,
                              deps=[ex1])
                    r1 = P.op("vector", "reciprocal", rD, sD, sem=sv, deps=[s1])
                    eVq = Ebuf[:, :].rearrange("p (n q) -> p q n", q=32)
                    sC = scs[:, 0:32]
                    rC = scs[:, 32:64]
                    s2 = P.op("vector", "tensor_reduce", sC, eVq, AX.X, ALU.add, sem=sv,
                              deps=[ex1])
                    r2 = P.op("vector", "reciprocal", rC, sC, sem=sv, deps=[s2])
                    from concourse.bass import broadcast_tensor_aps as _bc
                    dV = lrow[:, :].rearrange("p (n q) -> p n q", q=32)
                    a2, b2 = _bc(eV, rD.rearrange("p (n o) -> p n o", o=1))
                    d1 = P.op("vector", "tensor_tensor", dV, a2, b2, ALU.mult, sem=sv,
                              deps=[r1])
                    std = P.dma("sync", d_dram[tt * 128:(tt + 1) * 128, :], lrow[:, :],
                                dcst[0], deps=[d1])
                    a3, b3 = _bc(eV, rC.rearrange("p (o q) -> p o q", o=1))
                    c1 = P.op("vector", "tensor_tensor", eV, a3, b3, ALU.mult, sem=sv,
                              deps=[d1, r2])
                    stc = P.dma("sync", c_dram[tt * 128:(tt + 1) * 128, :], Ebuf[:, :],
                                dcst[1], deps=[c1])
                    prev_std = std
                    prev_stc = stc
                    tok_dc = stc
                dtl = scB("l1_dtl", [128, 2 * 2048], BF16)
                xps = scB("l1_xps", [128, 2 * 1024], F32)
                dstr = [self.stream("l1dt0"), self.stream("l1dt1")]
                xpst = [self.stream("l1xp0"), self.stream("l1xp1")]
                dldf = {}
                xpf = {}
                tok_xp = None
                for sl in range(16):
                    lp2 = sl % 2
                    dts = dtl[:, lp2 * 2048:(lp2 + 1) * 2048]
                    tdl = None
                    for tt in range(TT):
                        tdl = P.dma("sync", dts[:, tt * 128:(tt + 1) * 128],
                                    d_dram[tt * 128:(tt + 1) * 128, sl * 128:(sl + 1) * 128],
                                    dstr[lp2],
                                    deps=(([dldf.get(sl - 2)] +
                                           ([(dcst[0], dcst[0].n), (dcst[1], dcst[1].n)]
                                            if sl == 0 else [])) if tt == 0 else ()))
                    mm = None
                    for tt in range(TT):
                        for c2 in range(2):
                            mm = P.op("tensor", "matmul",
                                      self.psA[:, c2 * 512:c2 * 512 + 512],
                                      dts[:, tt * 128:(tt + 1) * 128],
                                      xn_tm[:, tt * 1024 + c2 * 512: tt * 1024 + c2 * 512 + 512],
                                      start=(tt == 0), stop=(tt == TT - 1),
                                      sem=(st if (tt == TT - 1 and c2 == 1) else None),
                                      deps=(([tdl, etok, self._l1xp_psfree]
                                             + (list(deps) if sl == 0 else []))
                                            if (tt == 0 and c2 == 0) else ()),
                                      skip_group_check=True)
                    dldf[sl] = mm
                    xp = xps[:, lp2 * 1024:(lp2 + 1) * 1024]
                    ev = P.op("scalar", "activation", xp, self.psA[:, 0:1024], AF.Identity,
                              sem=sa, deps=[mm, xpf.get(sl - 2)])
                    self._l1xp_psfree = ev
                    stx = P.dma("sync", xp_dram[sl * 128:(sl + 1) * 128, :], xp,
                                xpst[lp2], deps=[ev])
                    xpf[sl] = stx
                    tok_xp = stx
            P.wtok("gpsimd", tok_xp)
            for s_ in [self.stream("l1xp0"), self.stream("l1xp1")]:
                P.wait("gpsimd", s_)
            tcc = P.op("gpsimd", "collective_compute", "AllReduce", ALU.add,
                       replica_groups, [xp_dram[:, :]], [xr_dram[:, :]], sem=self.cc)
        return tcc

    def l1_ffn_post(self, tcc, sT_ready=None):
        P = self.P
        st, sv, sa = self.st, self.sv, self.sa
        NS = 2048
        es2 = ExitStack()
        with es2:
            sb2 = lambda n, sh, dt: es2.enter_context(self.nc.sbuf_tensor(n, sh, dt))
            Ytl = sb2("l1_Ytl", [128, TT * 1024], BF16)
            ytfree = None
            with ExitStack() as sC:
                scC = lambda n, sh, dt: sC.enter_context(self.nc.sbuf_tensor(n, sh, dt))
                xpT = scC("l1_xpT", [128, 8 * NS], BF16)
                xrs = scC("l1_xrs", [128, 2 * 1024], F32)
                xts = scC("l1_xts", [128, 2 * 1024], BF16)
                stt_ = scC("l1_stats", [128, 16], F32)
                xrst = [self.stream("l1xr0"), self.stream("l1xr1")]
                rfree = {}
                ttfree = None
                for sl in range(16):
                    lp = sl % 2
                    xr = xrs[:, lp * 1024:(lp + 1) * 1024]
                    xt = xts[:, lp * 1024:(lp + 1) * 1024]
                    ld = P.dma("sync", xr, self._l1_xr[sl * 128:(sl + 1) * 128, :],
                               xrst[lp], deps=[tcc if sl == 0 else None, rfree.get(sl - 2)])
                    c1 = P.op("vector", "tensor_scalar", xr, xr, -33000.0, None,
                              ALU.max, sem=sv, deps=[ld])
                    c2 = P.op("vector", "tensor_scalar", xr, xr, 65000.0, None,
                              ALU.min, sem=sv, deps=[c1])
                    so = lp * 8
                    s1 = stt_[:, so:so + 1]; s2 = stt_[:, so + 1:so + 2]
                    mean = stt_[:, so + 2:so + 3]; msq = stt_[:, so + 3:so + 4]
                    var = stt_[:, so + 4:so + 5]; rstd = stt_[:, so + 5:so + 6]
                    nmr = stt_[:, so + 6:so + 7]
                    t1 = P.op("vector", "tensor_reduce", s1, xr, AX.X, ALU.add, sem=sv, deps=[c2])
                    t2 = P.op("vector", "tensor_tensor", xt, xr, xr, ALU.mult, sem=sv,
                              deps=[rfree.get(sl - 2)])
                    t3 = P.op("vector", "tensor_reduce", s2, xt, AX.X, ALU.add, sem=sv, deps=[t2])
                    t4 = P.op("vector", "tensor_scalar", mean, s1, 1.0 / D, None, ALU.mult, sem=sv, deps=[t1])
                    t5 = P.op("vector", "tensor_tensor", msq, mean, mean, ALU.mult, sem=sv, deps=[t4])
                    t6 = P.op("vector", "scalar_tensor_tensor", var, s2, 1.0 / D, msq,
                              ALU.mult, ALU.subtract, sem=sv, deps=[t3, t5])
                    t6b = P.op("vector", "tensor_scalar", var, var, 1e-5, None, ALU.add,
                               sem=sv, deps=[t6])
                    t7 = P.op("scalar", "activation", rstd, var, AF.Sqrt, sem=sa, deps=[t6b])
                    t8 = P.op("vector", "reciprocal", rstd, rstd, sem=sv, deps=[t7])
                    t9 = P.op("vector", "scalar_tensor_tensor", nmr, mean, -1.0, rstd,
                              ALU.mult, ALU.mult, sem=sv, deps=[t8])
                    t10 = P.op("scalar", "activation", xt, xr, AF.Identity, bias=nmr,
                               scale=rstd, sem=sa, deps=[t9])
                    t11 = P.op("scalar", "activation", xt, xt, AF.Tanh, sem=sa, deps=[t10])
                    rfree[sl] = t11
                    last = None
                    for c in range(DC):
                        last = P.op("tensor", "transpose",
                                    self.psT16[:, c * 128:(c + 1) * 128],
                                    xt[:, c * 128:(c + 1) * 128], self.ident[:, :],
                                    sem=(st if c == DC - 1 else None),
                                    deps=([t11, ttfree,
                                           sT_ready if sl == 0 else None] if c == 0 else ()))
                    dst = xpT[:, :].rearrange("p (c j) -> p c j", j=NS)[:, :, sl * 128:(sl + 1) * 128]
                    srcv = self.psT16[:, :].rearrange("p (c j) -> p c j", j=128)
                    ttfree = P.op("vector", "tensor_copy", dst, srcv, sem=sv, deps=[last])
                eb = scC("l1_eb", [64, 1024], BF16)
                ebT = scC("l1_ebT", [128, 8 * 64], F32)
                ystg = scC("l1_ystg", [128, 32], BF16)
                tb1 = P.dma("gpsimd", eb[:, :], self.w["L1_ffn_expert_b"][:, :],
                            self.stream("l1eb"), deps=[tcc])
                tbT = None
                for c in range(DC):
                    tt2 = P.op("tensor", "transpose", self.psT16[:, 0:64],
                               eb[:, c * 128:(c + 1) * 128], self.ident[0:64, 0:64],
                               sem=st, deps=[tb1, tbT, ttfree if c == 0 else None])
                    tbT = P.op("vector", "tensor_copy", ebT[:, c * 64:(c + 1) * 64],
                               self.psT16[:, 0:64], sem=sv, deps=[tt2])
                wes = scC("l1_wes", [128, 2 * 8 * 1024], BF16)
                wst_ = [self.stream("l1we0"), self.stream("l1we1")]
                wefree = {}
                ps_free = None
                ystg_free = None
                tr_free = None
                for n in range(64):
                    lp = n % 2
                    we = wes[:, lp * 8192:(lp + 1) * 8192]
                    tw = None
                    for k in range(DC):
                        tw = P.dma("gpsimd", we[:, k * 1024:(k + 1) * 1024],
                                   self.w["L1_ffn_expert_w"][n, k * 128:(k + 1) * 128, :],
                                   wst_[lp],
                                   deps=([wefree.get(n - 2), tbT if n == 0 else None]
                                         if k == 0 else ()))
                    mm = None
                    for hm in range(8):
                        prs = [(we[:, k * 1024 + hm * 128: k * 1024 + hm * 128 + 128],
                                xpT[:, k * NS + n * 32: k * NS + n * 32 + 32])
                               for k in range(DC)]
                        mm = self.mm_acc(self.psB[:, 0:32], prs, sem=st,
                                         deps=[tw, ttfree, ps_free])
                        ev = P.op("scalar", "activation", ystg[:, 0:32],
                                  self.psB[:, 0:32], AF.Identity,
                                  bias=ebT[:, hm * 64 + n:hm * 64 + n + 1],
                                  sem=sa, deps=[mm, ystg_free])
                        ps_free = ev
                        half = n % 2
                        tt3 = P.op("tensor", "transpose",
                                   self.psT16[half * 32:half * 32 + 32, hm * 128:(hm + 1) * 128],
                                   ystg[:, 0:32], self.ident[:, :],
                                   sem=st, deps=[ev, tr_free])
                        ystg_free = tt3
                        tr_free = tt3
                    if n % 2 == 1:
                        npair = n // 2
                        sl2 = (npair * 64) // 128
                        off = (npair * 64) % 128
                        ytfree = P.op("vector", "tensor_copy",
                                      Ytl[:, sl2 * 1024:(sl2 + 1) * 1024][off:off + 64, :],
                                      self.psT16[0:64, :], sem=sv, deps=[tt3])
                        tr_free = ytfree
                    wefree[n] = mm
            with ExitStack() as sD:
                scD = lambda n, sh, dt: sD.enter_context(self.nc.sbuf_tensor(n, sh, dt))
                cts = scD("l1_cts", [128, 2 * T], BF16)
                gstg2 = scD("l1_g2", [128, 2 * 512], BF16)
                ctst = [self.stream("l1ct0"), self.stream("l1ct1")]
                ctfree = {}
                gfree = {}
                mv = None
                cb_psfree = None
                gidx = 0
                for hm in range(8):
                    mm2 = None
                    for sl in range(16):
                        ci = hm * 16 + sl
                        lp = sl % 2
                        ct = cts[:, lp * T:(lp + 1) * T]
                        ld = P.dma("sync", ct, self._l1_c[:, sl * 128:(sl + 1) * 128],
                                   ctst[lp], transpose=True,
                                   deps=[ctfree.get(ci - 2), ytfree if ci == 0 else None])
                        for tc in range(4):
                            dd = []
                            if tc == 0:
                                dd = [ld]
                                if sl == 0:
                                    dd += [ytfree, cb_psfree]
                            mm2 = P.op("tensor", "matmul",
                                       self.psA[:, tc * 512:(tc + 1) * 512],
                                       Ytl[:, sl * 1024 + hm * 128: sl * 1024 + hm * 128 + 128],
                                       ct[:, tc * 512:(tc + 1) * 512],
                                       start=(sl == 0), stop=(sl == 15),
                                       sem=(st if tc == 3 else None),
                                       deps=dd, skip_group_check=True)
                        ctfree[ci] = mm2
                    for tc in range(4):
                        gp = gidx % 2
                        gsl = gstg2[:, gp * 512:(gp + 1) * 512]
                        sil = P.op("scalar", "activation", gsl,
                                   self.psA[:, tc * 512:(tc + 1) * 512], AF.Silu,
                                   sem=sa, deps=[mm2, gfree.get(gidx - 2)])
                        ssl = self.sT[:, hm * T + tc * 512: hm * T + tc * 512 + 512]
                        mdeps = [sil]
                        if hm == 0 and tc == 0 and sT_ready is not None:
                            mdeps.append(sT_ready)
                        mv = P.op("vector", "tensor_tensor", ssl, ssl, gsl, ALU.mult,
                                  sem=sv, deps=mdeps)
                        gfree[gidx] = mv
                        gidx += 1
                    cb_psfree = mv
        self.tok_sT = mv
        return mv

    def layer1(self, x_src, x_dst, f_cos, f_sin, replica_groups, deps=()):
        self.ln_phase(x_src, deps=deps)
        ta = self.l1_attn(f_cos, f_sin, deps=deps)
        tcc = self.l1_ffn(replica_groups, sT_ready=ta, deps=[ta])
        tf = self.l1_ffn_post(tcc, sT_ready=ta)
        return self.residual_phase(x_src, x_dst, deps=[tf])

    def l3_attn(self, deps=()):
        P = self.P
        st, sv, sa = self.st, self.sv, self.sa
        LAMBDA_INIT = 0.8 - 0.6 * math.exp(-0.3)
        qkv_dram = self.nc.dram_tensor("l3_qkv", [T, 3072], BF16)
        es2 = ExitStack()
        with es2:
            sb2 = lambda n, sh, dt: es2.enter_context(self.nc.sbuf_tensor(n, sh, dt))
            qst = [self.stream("l3q0"), self.stream("l3q1")]
            self._l3ps_free = None
            with ExitStack() as s1:
                sc1 = lambda n, sh, dt: s1.enter_context(self.nc.sbuf_tensor(n, sh, dt))
                qstg = sc1("l3qstg", [128, 2 * 512], BF16)
                sfree = {}
                qi = 0
                part_mm = None
                for part in range(3):
                    with ExitStack() as s1a:
                        sc1a = lambda n, sh, dt: s1a.enter_context(self.nc.sbuf_tensor(n, sh, dt))
                        wp = sc1a(f"l3w{part}", [128, 8 * 1024], BF16)
                        strm = self.stream(f"l3w{part}")
                        tw = None
                        for k in range(8):
                            tw = P.dma("gpsimd", wp[:, k * 1024:(k + 1) * 1024],
                                       self.w["L3_attn_Wqkv"][k * 128:(k + 1) * 128,
                                                              part * 1024:(part + 1) * 1024],
                                       strm, deps=(([self.tok_xnT, part_mm] + list(deps))
                                                   if k == 0 else ()))
                        for tt in range(TT):
                            for c2 in range(2):
                                gp = qi % 2
                                qs = qstg[:, gp * 512:(gp + 1) * 512]
                                prs = [(self.xnT[:, k * T + tt * 128: k * T + tt * 128 + 128],
                                        wp[:, k * 1024 + c2 * 512: k * 1024 + c2 * 512 + 512])
                                       for k in range(8)]
                                mm = self.mm_acc(self.psB[:, :], prs, sem=st,
                                                 deps=[tw, self.tok_xnT, self._l3ps_free])
                                ev = P.op("scalar", "activation", qs, self.psB[:, :],
                                          AF.Identity, scale=(0.125 if part <= 1 else 1.0),
                                          sem=sa, deps=[mm, sfree.get(qi - 2)])
                                self._l3ps_free = ev
                                stw = P.dma("sync", qkv_dram[tt * 128:(tt + 1) * 128,
                                                             part * 1024 + c2 * 512:
                                                             part * 1024 + c2 * 512 + 512],
                                            qs, qst[gp], deps=[ev])
                                sfree[qi] = stw
                                qi += 1
                        part_mm = mm
            import os as _os
            if _os.environ.get("L3STOP") == "qkv":
                self.tok_sT_attn = (qst[0], qst[0].n)
                P.wait("sync", qst[0])
                P.wait("sync", qst[1])
                return (self.sv, self.sv.n)
            lamb = sb2("l3lam", [1, 4 * 512], F32)
            lsc = sb2("l3lsc", [1, 8], F32)
            lam128 = sb2("l3lam128", [128, 1], F32)
            lambf = sb2("l3lambf", [1, 1], BF16)
            lst = self.stream("l3lam")
            tl = None
            for i, nm in enumerate(["lambda_q1", "lambda_k1", "lambda_q2", "lambda_k2"]):
                tl = P.dma("sync", lamb[0:1, i * 512:(i + 1) * 512],
                           self.w["L3_attn_" + nm].ap().rearrange("(o n) -> o n", o=1),
                           lst, deps=([(qst[0], qst[0].n), (qst[1], qst[1].n)]
                                      if i == 0 else ()))
            u1 = P.op("vector", "tensor_tensor", lamb[0:1, 0:512], lamb[0:1, 0:512],
                      lamb[0:1, 512:1024], ALU.mult, sem=sv, deps=[tl])
            u2 = P.op("vector", "tensor_tensor", lamb[0:1, 1024:1536], lamb[0:1, 1024:1536],
                      lamb[0:1, 1536:2048], ALU.mult, sem=sv, deps=[u1])
            r1 = P.op("vector", "tensor_reduce", lsc[0:1, 0:1], lamb[0:1, 0:512], AX.X,
                      ALU.add, sem=sv, deps=[u2])
            r2 = P.op("vector", "tensor_reduce", lsc[0:1, 1:2], lamb[0:1, 1024:1536], AX.X,
                      ALU.add, sem=sv, deps=[r1])
            e1 = P.op("scalar", "activation", lsc[0:1, 2:3], lsc[0:1, 0:1], AF.Exp,
                      sem=sa, deps=[r2])
            e2 = P.op("scalar", "activation", lsc[0:1, 3:4], lsc[0:1, 1:2], AF.Exp,
                      sem=sa, deps=[e1])
            l1_ = P.op("vector", "tensor_tensor", lsc[0:1, 4:5], lsc[0:1, 3:4],
                       lsc[0:1, 2:3], ALU.subtract, sem=sv, deps=[e2])
            l2_ = P.op("vector", "tensor_scalar", lsc[0:1, 4:5], lsc[0:1, 4:5],
                       -LAMBDA_INIT, None, ALU.add, sem=sv, deps=[l1_])
            l3_ = P.op("vector", "tensor_scalar", lsc[0:1, 4:5], lsc[0:1, 4:5],
                       1.0 - LAMBDA_INIT, None, ALU.mult, sem=sv, deps=[l2_])
            lcp = P.op("vector", "tensor_copy", lambf[0:1, :], lsc[0:1, 4:5], sem=sv,
                       deps=[l3_])
            bcl = P.op("tensor", "matmul", self.psD[:, 0:1], self.onesrow[0:1, 0:128],
                       lambf[0:1, :], start=True, stop=True, sem=st,
                       deps=[lcp, self._l3ps_free], skip_group_check=True)
            lcv = P.op("vector", "tensor_copy", lam128[:, :], self.psD[:, 0:1], sem=sv,
                       deps=[bcl])
            ybuf = sb2("l3y", [128, DC * T], BF16)
            with ExitStack() as s3:
                sc3 = lambda n, sh, dt: s3.enter_context(self.nc.sbuf_tensor(n, sh, dt))
                attnb = sc3("l3attnb", [128, DC * T], BF16)
                s3b = ExitStack()
                sc3b = lambda n, sh, dt: s3b.enter_context(self.nc.sbuf_tensor(n, sh, dt))
                Ka = sc3b("l3Ka", [65, T], BF16)
                Qa = sc3b("l3Qa", [65, 2 * T], BF16)
                Vh = sc3b("l3Vh", [128, 16 * 65], BF16)
                kbl = sc3b("l3kbl", [128, 1024], BF16)
                vbl = sc3b("l3vbl", [128, 1024], BF16)
                vT = sc3b("l3vT", [64, T], BF16)
                qbl = sc3b("l3qbl", [64, T], BF16)
                PTa = sc3b("l3PTa", [128, 8 * 512], BF16)
                PTx = sc3b("l3PTb", [128, 8 * 512], BF16)
                rstg = sc3b("l3rstg", [128, 2 * 512], F32)
                rabs = sc3b("l3rabs", [64, 2 * 512], BF16)
                sst = sc3b("l3sst", [128, 4], F32)
                mnbf = sc3b("l3mnbf", [128, 1], BF16)
                self._l3rbf16 = sc3b("l3rb16", [1, 1024], BF16)
                vstr = self.stream("l3v")
                kstr = self.stream("l3k")
                qstr = self.stream("l3qh")
                tok_attn = None
                tok_red = None
                tok_exp = {0: None, 1: None}
                tok_pv = {0: None, 1: None}
                tok_stat = None
                tr_free = None
                q_free = None
                pv_hist = {}
                for kvh in range(8):
                    ldk = []
                    for at in range(2):
                        ldk.append(P.dma("sync", kbl[:, at * 512:(at + 1) * 512],
                                         qkv_dram[256 * kvh + at * 128:256 * kvh + at * 128 + 128,
                                                  2048:2560], kstr,
                                         deps=([tr_free, tok_exp[0], tok_exp[1],
                                                (qst[0], qst[0].n) if kvh == 0 else None,
                                                (qst[1], qst[1].n) if kvh == 0 else None]
                                               if at == 0 else ())))
                        ldk.append(P.dma("sync", vbl[:, at * 512:(at + 1) * 512],
                                         qkv_dram[256 * kvh + at * 128:256 * kvh + at * 128 + 128,
                                                  2560:3072], kstr))
                    ko = P.op("vector", "memset", Ka[64:65, :], 1.0, sem=sv,
                              deps=[lcv if kvh == 0 else None, tok_attn])
                    for which, (blk, dstb) in enumerate([(kbl, Ka), (vbl, vT)]):
                        for at in range(2):
                            last4 = None
                            for c in range(4):
                                last4 = P.op("tensor", "transpose",
                                             self.psT16[:, c * 128:(c + 1) * 128],
                                             blk[:, at * 512 + c * 128: at * 512 + c * 128 + 128],
                                             self.ident[:, :],
                                             sem=(st if c == 3 else None),
                                             deps=([ldk[-1], tr_free, tok_stat] if c == 0 else ()))
                            dv = dstb[0:64, :].rearrange("p (a f) -> p a f", f=8)
                            for c in range(4):
                                for hh in range(2):
                                    f = 2 * c + hh
                                    tr_free = P.op("vector", "tensor_copy",
                                                   dv[:, at * 128:(at + 1) * 128, f],
                                                   self.psT16[64 * hh:64 * hh + 64,
                                                              c * 128:(c + 1) * 128],
                                                   sem=sv, deps=[last4])
                    vo = P.op("vector", "memset",
                              Vh[:, :].rearrange("p (u c) -> p u c", c=65)[:, :, 64:65],
                              1.0, sem=sv, deps=[tok_pv[0], tok_pv[1]])
                    for u in range(TT):
                        lv = P.op("tensor", "transpose",
                                  self.psT16[:, (u % 8) * 64:(u % 8) * 64 + 64],
                                  vT[0:64, u * 128:(u + 1) * 128], self.ident[0:64, 0:64],
                                  sem=(st if u % 8 == 7 else None),
                                  deps=([tr_free, vo, tok_stat] if u == 0 else
                                        ([tr_free] if u % 8 == 0 else ())))
                        if u % 8 == 7:
                            vdv = Vh[:, :].rearrange("p (u2 c) -> p u2 c", c=65)[
                                :, u - 7:u + 1, 0:64]
                            svv = self.psT16[:, 0:512].rearrange("p (u2 c) -> p u2 c", c=64)
                            tr_free = P.op("vector", "tensor_copy", vdv, svv, sem=sv,
                                           deps=[lv])
                    if kvh == 0:
                        self.dbg("l3Ka", Ka[0:64, :], tr_free)
                        self.dbg("l3Vh", Vh[:, :], tr_free)
                        if self.dbg_on:
                            tr_free = self._dbg_tok
                    if _os.environ.get("L3STOP") == "kv":
                        continue
                    for ip in range(2):
                        effh = 2 * kvh + ip
                        if _os.environ.get("L3STOP") == "q1" and (kvh > 0 or ip > 0):
                            continue
                        for sub in range(2):
                            qh = 4 * kvh + 2 * ip + sub
                            if _os.environ.get("L3STOP") == "q1" and sub > 0:
                                continue
                            ldq = P.dma("sync", qbl[0:64, :],
                                        qkv_dram[64 * qh:64 * (qh + 1), 0:2048], qstr,
                                        deps=[tr_free, q_free, tok_stat])
                            qdv = Qa[0:64, sub * T:(sub + 1) * T].rearrange(
                                "p (a f) -> p a f", f=32)
                            for g in range(2):
                                lastq = None
                                for cl in range(8):
                                    c = g * 8 + cl
                                    lastq = P.op("tensor", "transpose",
                                                 self.psT16[:, cl * 64:cl * 64 + 64],
                                                 qbl[0:64, c * 128:(c + 1) * 128],
                                                 self.ident[0:64, 0:64],
                                                 sem=(st if cl == 7 else None),
                                                 deps=([ldq if g == 0 else None, tr_free,
                                                        tok_stat] if cl == 0 else ()))
                                for cl in range(8):
                                    c = g * 8 + cl
                                    for hh in range(2):
                                        m = 2 * c + hh
                                        tr_free = P.op("vector", "tensor_copy",
                                                       qdv[:, :, m],
                                                       self.psT16[64 * hh:64 * hh + 64,
                                                                  cl * 64:cl * 64 + 64],
                                                       sem=sv,
                                                       deps=([lastq] if (cl == 0 and hh == 0) else ()))
                        q_free = tr_free
                        if _os.environ.get("L3STOP") in ("q", "q1"):
                            continue
                        if kvh == 0 and ip == 0:
                            self.dbg(f"l3Qa", Qa[0:64, :], tr_free)
                            if self.dbg_on:
                                tr_free = self._dbg_tok
                                tok_stat = self._dbg_tok
                        for qg in range(4):
                            qts = [4 * qg + i for i in range(4)]
                            ktn = cdiv((qts[-1] + 1) * 128, 128)
                            for sub in range(2):
                                PT = PTa if sub == 0 else PTx
                                for qt in qts:
                                    kmax = (qt + 1) * 128
                                    nch = cdiv(kmax, 512)
                                    tok_qk = None
                                    for c in range(nch):
                                        k1 = min(512, kmax - c * 512)
                                        tok_qk = P.op(
                                            "tensor", "matmul",
                                            self.psA[:, c * 512:c * 512 + k1],
                                            Qa[0:64, sub * T + qt * 128: sub * T + qt * 128 + 128],
                                            Ka[0:64, c * 512:c * 512 + k1],
                                            start=True, stop=True, sem=st,
                                            deps=([tr_free, tok_red, tok_attn] if c == 0 else ()),
                                            skip_group_check=True)
                                    tqm = P.op("vector", "tensor_tensor",
                                               self.psA[:, qt * 128:qt * 128 + 128],
                                               self.psA[:, qt * 128:qt * 128 + 128],
                                               self.diagmask[:, :], ALU.add, sem=sv,
                                               deps=[tok_qk])
                                    tok_red = P.op("vector", "tensor_reduce", sst[:, 0:1],
                                                   self.psA[:, 0:kmax], AX.X, ALU.max,
                                                   sem=sv, deps=[tqm, tok_stat])
                                    tneg = P.op("scalar", "activation", mnbf[:, :],
                                                sst[:, 0:1], AF.Identity, scale=-1.0,
                                                sem=sa, deps=[tok_red, tok_stat])
                                    ttr = P.op("tensor", "transpose", self.psT16[0:1, 0:128],
                                               mnbf[:, :], self.ident[:, :], sem=st,
                                               deps=[tneg, tok_stat, tr_free])
                                    tok_stat = P.op("scalar", "copy",
                                                    Qa[64:65, sub * T + qt * 128:
                                                       sub * T + qt * 128 + 128],
                                                    self.psT16[0:1, 0:128], sem=sa,
                                                    deps=[ttr])
                                for kt in (range(ktn) if _os.environ.get("L3STOP") != "qk" else []):
                                    ks = kt % 8
                                    tok_sT = P.op("tensor", "matmul", self.psB[:, :],
                                                  Ka[0:65, kt * 128:(kt + 1) * 128],
                                                  Qa[0:65, sub * T + qg * 512:
                                                     sub * T + qg * 512 + 512],
                                                  start=True, stop=True, sem=st,
                                                  deps=[tok_stat, tok_exp[sub]],
                                                  skip_group_check=True)
                                    mt = self._mask_subtiles(qg, kt, lambda q: (q + 1) * 128)
                                    tok_m = tok_sT
                                    for (sub_i, mask) in mt:
                                        tok_m = P.op("vector", "tensor_tensor",
                                                     self.psB[:, sub_i * 128:(sub_i + 1) * 128],
                                                     self.psB[:, sub_i * 128:(sub_i + 1) * 128],
                                                     mask, ALU.add, sem=sv, deps=[tok_m])
                                    tok_exp[sub] = P.op("scalar", "activation",
                                                        PT[:, ks * 512:(ks + 1) * 512],
                                                        self.psB[:, :], AF.Exp, sem=sa,
                                                        deps=[tok_m, pv_hist.get((sub, kt - 8))])
                                    ps_pv = self.psC if sub == 0 else self.psD
                                    tok_pv[sub] = P.op("tensor", "matmul", ps_pv[0:65, :],
                                                       Vh[:, kt * 65:(kt + 1) * 65],
                                                       PT[:, ks * 512:(ks + 1) * 512],
                                                       start=(kt == 0), stop=(kt == ktn - 1),
                                                       sem=st,
                                                       deps=[tok_exp[sub], vo,
                                                             tok_attn if kt == 0 else None],
                                                       skip_group_check=True)
                                    pv_hist[(sub, kt)] = tok_pv[sub]
                            if kvh == 0 and ip == 0 and qg == 0 and self.dbg_on:
                                self.dbg("l3PTa0", PTa[:, 0:2048], tok_pv[0])
                                self.dbg("l3PTx0", PTx[:, 0:2048], tok_pv[1])
                                tok_exp[0] = self._dbg_tok
                                tok_exp[1] = self._dbg_tok
                            if _os.environ.get("L3STOP") == "qk":
                                continue
                            rla = P.op("vector", "reciprocal", rstg[0:1, 0:512],
                                       self.psC[64:65, :], sem=sv, deps=[tok_pv[0], tok_attn])
                            rlb = P.op("vector", "reciprocal", rstg[0:1, 512:1024],
                                       self.psD[64:65, :], sem=sv, deps=[tok_pv[1]])
                            rbf = P.op("vector", "tensor_copy", self._l3rbf16[0:1, 0:1024],
                                       rstg[0:1, 0:1024], sem=sv, deps=[rla, rlb])
                            ba = P.op("tensor", "matmul", self.psA[0:64, 0:512],
                                      self.onesrow[0:1, 0:64], self._l3rbf16[0:1, 0:512],
                                      start=True, stop=True, sem=st,
                                      deps=[rbf, tok_red], skip_group_check=True)
                            bb = P.op("tensor", "matmul", self.psA[0:64, 512:1024],
                                      self.onesrow[0:1, 0:64], self._l3rbf16[0:1, 512:1024],
                                      start=True, stop=True, sem=st,
                                      skip_group_check=True)
                            eva = P.op("scalar", "activation", rabs[:, 0:512],
                                       self.psA[0:64, 0:512], AF.Identity, sem=sa,
                                       deps=[ba, bb])
                            evb = P.op("scalar", "activation", rabs[:, 512:1024],
                                       self.psA[0:64, 512:1024], AF.Identity, sem=sa,
                                       deps=[eva])
                            v1 = P.op("vector", "tensor_tensor", rstg[0:64, 0:512],
                                      self.psC[0:64, :], rabs[0:64, 0:512], ALU.mult,
                                      sem=sv, deps=[evb])
                            v2 = P.op("vector", "scalar_tensor_tensor", rstg[0:64, 512:1024],
                                      self.psD[0:64, :], lam128[0:64, 0:1], rabs[0:64, 512:1024],
                                      ALU.mult, ALU.mult, sem=sv, deps=[v1, lcv])
                            dsta = attnb[64 * (effh % 2):64 * (effh % 2) + 64,
                                         (effh // 2) * T + qg * 512:
                                         (effh // 2) * T + qg * 512 + 512]
                            tok_attn = P.op("vector", "scalar_tensor_tensor", dsta,
                                            rstg[0:64, 0:512], 1.0 - LAMBDA_INIT,
                                            rstg[0:64, 512:1024], ALU.mult, ALU.add,
                                            sem=sv, deps=[v2])
                self.dbg("l3attnb", attnb[:, :], tok_attn)
                if _os.environ.get("L3STOP") in ("heads", "kv", "q", "qk", "q1"):
                    cpy = P.op("vector", "tensor_copy", self.sT[:, :], attnb[:, :],
                               sem=self.sv, deps=[tok_attn])
                    s3b.close()
                    self.tok_sT_attn = cpy
                    return cpy
                s3b.close()
                with ExitStack() as s4:
                    sc4 = lambda n, sh, dt: s4.enter_context(self.nc.sbuf_tensor(n, sh, dt))
                    wpj, t_wpj = self.load_w_nat("L3_attn_Wproj", 1024, 1024, sc4,
                                                 deps=[tok_attn])
                    t_y = self.proj_T(wpj, t_wpj, 8, ybuf,
                                      lambda m, c: ybuf[:, m * T + c * 512: m * T + c * 512 + 512],
                                      8, None, attnb, 0, extra_dep=tok_attn)
                    y2b = sc4("l3y2", [128, DC * T], BF16)
                    prfree = None
                    for t2 in range(2):
                        for ct in range(DC):
                            for cc in range(DC):
                                tp = P.op("tensor", "transpose",
                                          self.psT16[:, (cc % 8) * 128:(cc % 8) * 128 + 128],
                                          ybuf[:, ct * T + t2 * 1024 + cc * 128:
                                               ct * T + t2 * 1024 + cc * 128 + 128],
                                          self.ident[:, :],
                                          sem=(st if cc == DC - 1 else None),
                                          deps=([t_y, prfree] if cc == 0 else ()))
                            dv2 = y2b[:, :].rearrange("p (cc g m2) -> p cc g m2", g=T // 256,
                                                      m2=256)[:, :, ct, :].rearrange(
                                "p cc (m two) -> p cc m two", two=2)[:, :, :, t2]
                            srcv = self.psT16[:, :].rearrange("p (cc j) -> p cc j", j=128)
                            prfree = P.op("vector", "tensor_copy", dv2, srcv, sem=sv,
                                          deps=[tp])
                    self.dbg("l3ybuf", ybuf[:, :], t_y)
                    self.dbg("l3y2b", y2b[:, :], prfree)
                    tok_s = self.proj_T(wpj, t_wpj, 8, self.sT,
                                        lambda m, c: self.sT[:, m * T + c * 512: m * T + c * 512 + 512],
                                        8, None, y2b, 0, extra_dep=prfree)
        self.tok_sT_attn = tok_s
        return tok_s

    def l3_ffn(self, sT_ready=None, deps=()):
        P = self.P
        st, sv, sa = self.st, self.sv, self.sa
        from concourse.bass import broadcast_tensor_aps as _bc
        acc_dram = self.nc.dram_tensor("l3_acc", [D, T], F32)
        cw_dram = self.nc.dram_tensor("l3_cw", [48, T], BF16)
        es2 = ExitStack()
        with es2:
            sb2 = lambda n, sh, dt: es2.enter_context(self.nc.sbuf_tensor(n, sh, dt))
            with ExitStack() as sr:
                sbr = lambda n, sh, dt: sr.enter_context(self.nc.sbuf_tensor(n, sh, dt))
                gwb = sbr("l3_gw", [32, 1024], BF16)
                t_gw = P.dma("gpsimd", gwb[:, :], self.w["L3_ffn_gate_weight"][:, :],
                             self.stream("l3gw"), deps=list(deps))
                gbb = sbr("l3_gb", [1, 32], BF16)
                t_gb = P.dma("gpsimd", gbb[0:1, :],
                             self.w["L3_ffn_gate_bias"].ap().rearrange("(o n) -> o n", o=1),
                             self.stream("l3gb"), deps=list(deps))
                gwT = sbr("l3_gwT", [128, 8 * 32], BF16)
                gbx = sbr("l3_gbx", [128, 32], F32)
                tw = None
                for k in range(8):
                    tt1 = P.op("tensor", "transpose", self.psT16[:, 0:32],
                               gwb[0:32, k * 128:(k + 1) * 128], self.ident[0:32, 0:32],
                               sem=st, deps=[t_gw, tw, self.tok_consts] + (list(deps) if k == 0 else []))
                    tw = P.op("vector", "tensor_copy", gwT[:, k * 32:(k + 1) * 32],
                              self.psT16[:, 0:32], sem=sv, deps=[tt1])
                bcb = P.op("tensor", "matmul", self.psB[:, 0:32], self.onesrow[0:1, 0:128],
                           gbb[0:1, :], start=True, stop=True, sem=st,
                           deps=[t_gb, self.tok_ones, tw], skip_group_check=True)
                bcv = P.op("vector", "tensor_copy", gbx[:, :], self.psB[:, 0:32], sem=sv,
                           deps=[bcb])
                lgs = sbr("l3_lg", [128, 32], F32)
                scb = sbr("l3_scb", [128, 32], F32)
                scm = sbr("l3_scm", [128, 32], F32)
                mall = sbr("l3_mall", [128, 32], F32)
                grp = sbr("l3_grp", [128, 16], F32)
                stt_ = sbr("l3_st", [128, 16], F32)
                cwa = sbr("l3_cwa", [128, 48], BF16)
                self._l3_mtmp = sbr("l3_mtmp", [128, 32], F32)
                cwT_stage = sbr("l3_cwts", [48, T], BF16)
                tok_lg = None
                tok_cws = None
                for tt in range(TT):
                    prs = [(self.xnT[:, k * T + tt * 128: k * T + tt * 128 + 128],
                            gwT[:, k * 32:(k + 1) * 32]) for k in range(8)]
                    mm = self.mm_acc(self.psB[:, 0:32], prs, sem=st,
                                     deps=[tw, bcv, self.tok_xnT, tok_lg])
                    sig = P.op("scalar", "activation", lgs[:, :], self.psB[:, 0:32],
                               AF.Sigmoid, sem=sa, deps=[mm, tok_cws])
                    tok_lg = sig
                    a1 = P.op("vector", "tensor_tensor", scb[:, :], lgs[:, :], gbx[:, :],
                              ALU.add, sem=sv, deps=[sig])
                    scv = scb[:, :].rearrange("p (g e) -> p g e", e=8)
                    g1 = stt_[:, 0:4]; g2 = stt_[:, 4:8]; gs = stt_[:, 8:12]
                    gm1 = stt_[:, 12:13]; gm2 = stt_[:, 13:14]
                    t1 = P.op("vector", "tensor_reduce", g1, scv, AX.X, ALU.max, sem=sv,
                              deps=[a1])
                    m1v, m1b = _bc(scv, g1.rearrange("p (g o) -> p g o", o=1))
                    t2 = P.op("vector", "tensor_tensor", grp[:, :].rearrange(
                        "p (g e) -> p g e", e=8) if False else scm[:, :].rearrange(
                        "p (g e) -> p g e", e=8), m1v, m1b, ALU.is_equal, sem=sv, deps=[t1])
                    t3 = P.op("vector", "scalar_tensor_tensor", scm[:, :], scm[:, :], NEG,
                              scb[:, :], ALU.mult, ALU.add, sem=sv, deps=[t2])
                    t4 = P.op("vector", "tensor_reduce", g2, scm[:, :].rearrange(
                        "p (g e) -> p g e", e=8), AX.X, ALU.max, sem=sv, deps=[t3])
                    t5 = P.op("vector", "tensor_tensor", gs, g1, g2, ALU.add, sem=sv,
                              deps=[t4])
                    t6 = P.op("vector", "tensor_reduce", gm1, gs, AX.X, ALU.max, sem=sv,
                              deps=[t5])
                    t7 = P.op("vector", "tensor_scalar", grp[:, 0:4], gs, gm1, None,
                              ALU.is_equal, sem=sv, deps=[t6])
                    t8 = P.op("vector", "scalar_tensor_tensor", grp[:, 4:8], grp[:, 0:4],
                              NEG, gs, ALU.mult, ALU.add, sem=sv, deps=[t7])
                    t9 = P.op("vector", "tensor_reduce", gm2, grp[:, 4:8], AX.X, ALU.max,
                              sem=sv, deps=[t8])
                    t10 = P.op("vector", "tensor_scalar", grp[:, 8:12], grp[:, 4:8], gm2,
                               None, ALU.is_equal, sem=sv, deps=[t9])
                    t11 = P.op("vector", "tensor_tensor", grp[:, 0:4], grp[:, 0:4],
                               grp[:, 8:12], ALU.add, sem=sv, deps=[t10])
                    ev8, eb8 = _bc(scb[:, :].rearrange("p (g e) -> p g e", e=8),
                                   grp[:, 0:4].rearrange("p (g o) -> p g o", o=1))
                    t12 = P.op("vector", "scalar_tensor_tensor",
                               scm[:, :].rearrange("p (g e) -> p g e", e=8),
                               eb8, -NEG, ev8, ALU.mult, ALU.add, sem=sv, deps=[t11])
                    zz = P.op("vector", "memset", mall[:, :], 0.0, sem=sv, deps=[t12])
                    tprev = zz
                    for it in range(6):
                        mi = stt_[:, 14:15]
                        tca = P.op("vector", "tensor_reduce", mi, scm[:, :], AX.X, ALU.max,
                                   sem=sv, deps=[tprev])
                        tcc2 = P.op("vector", "tensor_scalar", self._l3_mtmp[:, :],
                                    scm[:, :], mi, None, ALU.is_equal, sem=sv, deps=[tca])
                        if it == 0:
                            acw = cwa[:, 32:48]
                            pr = self._l3_mtmp[:, :].rearrange("p (l two) -> p l two", two=2)
                            tpr = P.op("vector", "tensor_tensor", grp[:, 0:16],
                                       pr[:, :, 0], pr[:, :, 1], ALU.add, sem=sv, deps=[tcc2])
                            tpr2 = P.op("vector", "tensor_scalar", acw, grp[:, 0:16],
                                        0.1, None, ALU.mult, sem=sv, deps=[tpr])
                        tm2 = P.op("vector", "tensor_tensor", mall[:, :], mall[:, :],
                                   self._l3_mtmp[:, :], ALU.add, sem=sv, deps=[tcc2])
                        tprev = P.op("vector", "scalar_tensor_tensor", scm[:, :],
                                     self._l3_mtmp[:, :], NEG, scm[:, :], ALU.mult,
                                     ALU.add, sem=sv, deps=[tm2])
                    tt1_ = P.op("vector", "tensor_tensor", scm[:, :], mall[:, :], lgs[:, :],
                                ALU.mult, sem=sv, deps=[tprev])
                    ts1 = P.op("vector", "tensor_reduce", stt_[:, 14:15], scm[:, :], AX.X,
                               ALU.add, sem=sv, deps=[tt1_])
                    ts2 = P.op("vector", "tensor_scalar", stt_[:, 14:15], stt_[:, 14:15],
                               1e-20, None, ALU.add, sem=sv, deps=[ts1])
                    ts3 = P.op("vector", "reciprocal", stt_[:, 15:16], stt_[:, 14:15],
                               sem=sv, deps=[ts2])
                    ts4 = P.op("vector", "tensor_scalar", stt_[:, 15:16], stt_[:, 15:16],
                               0.2, None, ALU.mult, sem=sv, deps=[ts3])
                    ts5 = P.op("vector", "tensor_scalar", cwa[:, 0:32], scm[:, :],
                               stt_[:, 15:16], None, ALU.mult, sem=sv, deps=[ts4])
                    ttr = P.op("tensor", "transpose", self.psT16[0:48, 0:128],
                               cwa[:, :], self.ident[:, :], sem=st,
                               deps=[ts5, tok_cws])
                    tok_cws = P.op("vector", "tensor_copy",
                                   cwT_stage[:, tt * 128:(tt + 1) * 128],
                                   self.psT16[0:48, 0:128], sem=sv, deps=[ttr])
                tok_cwd = P.dma("sync", cw_dram[:, :], cwT_stage[:, :],
                                self.stream("l3cwd"), deps=[tok_cws])
            wgb = sb2("l3_ew1", [128, 2 * 8 * 512], BF16)
            wub = sb2("l3_ew2", [128, 2 * 8 * 512], BF16)
            wdb = sb2("l3_ew3", [128, 2 * 4 * 1024], BF16)
            hbuf = sb2("l3_h", [128, 4 * 512], BF16)
            gstg = sb2("l3_gs", [128, 2 * 512], BF16)
            dstg = sb2("l3_ds", [128, 2 * 512], F32)
            cwbc = sb2("l3_cwbc", [128, T], BF16)
            cwrow = sb2("l3_cwrow", [1, 2 * T], BF16)
            accst2 = [self.stream("l3acc0"), self.stream("l3acc1")]
            wstr = [self.stream("l3ew0"), self.stream("l3ew1")]
            cwst2 = [self.stream("l3cwr0"), self.stream("l3cwr1")]
            experts = [(0, 0)] + [(1, e) for e in range(32)] + [(2, l) for l in range(16)]
            psB_free = psC_free = psD_free = None
            gsl_free = {}
            dsl_free = {}
            wfree = {}
            tok_bc = None
            cwr_free = {}
            gidx = didx = 0
            down_done = None
            for ei, (kind, e) in enumerate(experts):
                lp = ei % 2
                NI = 256 if kind == 2 else 512
                nki = NI // 128
                wg_sl = wgb[:, lp * 4096: lp * 4096 + 8 * NI]
                wu_sl = wub[:, lp * 4096: lp * 4096 + 8 * NI]
                wd_sl = wdb[:, lp * 4096: lp * 4096 + nki * 1024]
                if kind == 0:
                    srcs = [self.w["L3_ffn_shared_w_gate"], self.w["L3_ffn_shared_w_up"],
                            self.w["L3_ffn_shared_w_down"]]
                elif kind == 1:
                    srcs = [self.w["L3_ffn_w_gate"][e], self.w["L3_ffn_w_up"][e],
                            self.w["L3_ffn_w_down"][e]]
                else:
                    srcs = [self.w["L3_ffn_anchor_w_gate"][e], self.w["L3_ffn_anchor_w_up"][e],
                            self.w["L3_ffn_anchor_w_down"][e]]
                ldp = [wfree.get(ei - 2), tok_cwd if ei == 0 else None] + \
                    (list(deps) if ei == 0 else [])
                tw = None
                for k in range(8):
                    tw = P.dma("gpsimd", wg_sl[:, k * NI:(k + 1) * NI],
                               srcs[0][k * 128:(k + 1) * 128, :], wstr[lp],
                               deps=(ldp if k == 0 else ()))
                for k in range(8):
                    tw = P.dma("gpsimd", wu_sl[:, k * NI:(k + 1) * NI],
                               srcs[1][k * 128:(k + 1) * 128, :], wstr[lp])
                for k in range(nki):
                    tw = P.dma("gpsimd", wd_sl[:, k * 1024:(k + 1) * 1024],
                               srcs[2][k * 128:(k + 1) * 128, :], wstr[lp])
                if kind != 0:
                    crow = cwrow[0:1, lp * T:(lp + 1) * T]
                    ri = e if kind == 1 else 32 + e
                    tcr = P.dma("sync", crow, cw_dram[ri:ri + 1, :], cwst2[lp],
                                deps=[cwr_free.get(ei - 2), tok_cwd if ei <= 1 else None])
                    bcs = None
                    for c in range(4):
                        tb = P.op("tensor", "matmul", self.psA[:, c * 512:(c + 1) * 512],
                                  self.onesrow[0:1, 0:128],
                                  crow[0:1, c * 512:(c + 1) * 512],
                                  start=True, stop=True, sem=st,
                                  deps=([tcr, tok_bc] if c == 0 else ()),
                                  skip_group_check=True)
                        bcs = P.op("scalar", "activation", cwbc[:, c * 512:(c + 1) * 512],
                                   self.psA[:, c * 512:(c + 1) * 512], AF.Identity,
                                   sem=sa, deps=[tb])
                    tok_bc = bcs
                    cwr_free[ei] = bcs
                for nc4 in range(4):
                    hv = None
                    for mt in range(nki):
                        gp = gidx % 2
                        gsl = gstg[:, gp * 512:(gp + 1) * 512]
                        xsl = lambda dk: self.xnT[:, dk * T + nc4 * 512: dk * T + nc4 * 512 + 512]
                        prsg = [(wg_sl[:, dk * NI + mt * 128: dk * NI + mt * 128 + 128], xsl(dk))
                                for dk in range(8)]
                        g_tok = self.mm_acc(self.psB[:, :], prsg, sem=st,
                                            deps=[tw, self.tok_xnT, psB_free]
                                            + (list(deps) if gidx == 0 else []))
                        sil = P.op("scalar", "activation", gsl, self.psB[:, :], AF.Silu,
                                   sem=sa, deps=[g_tok, gsl_free.get(gidx - 2)])
                        psB_free = sil
                        if kind != 0:
                            sil = P.op("vector", "tensor_tensor", gsl, gsl,
                                       cwbc[:, nc4 * 512:(nc4 + 1) * 512], ALU.mult,
                                       sem=sv, deps=[sil, tok_bc])
                        prsu = [(wu_sl[:, dk * NI + mt * 128: dk * NI + mt * 128 + 128], xsl(dk))
                                for dk in range(8)]
                        u_tok = self.mm_acc(self.psC[:, :], prsu, sem=st,
                                            deps=[psC_free] + (list(deps) if gidx == 0 else []))
                        hdeps = [u_tok, sil]
                        if mt == 0:
                            hdeps.append(down_done)
                        hv = P.op("vector", "tensor_tensor",
                                  hbuf[:, mt * 512:(mt + 1) * 512], self.psC[:, :], gsl,
                                  ALU.mult, sem=sv, deps=hdeps)
                        psC_free = hv
                        gsl_free[gidx] = hv
                        gidx += 1
                    for dm in range(8):
                        prd = [(wd_sl[:, kh * 1024 + dm * 128: kh * 1024 + dm * 128 + 128],
                                hbuf[:, kh * 512:(kh + 1) * 512]) for kh in range(nki)]
                        d_tok = self.mm_acc(self.psD[:, :], prd, sem=st,
                                            deps=[hv, tw, psD_free]
                                            + (list(deps) if didx == 0 else []))
                        dp = didx % 2
                        dsl = dstg[:, dp * 512:(dp + 1) * 512]
                        ev = P.op("scalar", "activation", dsl, self.psD[:, :], AF.Identity,
                                  sem=sa, deps=[d_tok, dsl_free.get(didx - 2)])
                        psD_free = ev
                        stw = P.dma("gpsimd", acc_dram[dm * 128:(dm + 1) * 128,
                                                       nc4 * 512:(nc4 + 1) * 512], dsl,
                                    accst2[dp], deps=[ev],
                                    accum_op=(ALU.add if ei > 0 else ALU.bypass))
                        dsl_free[didx] = stw
                        didx += 1
                    down_done = d_tok
                wfree[ei] = down_done
            accrd = sb2("l3_accrd", [128, 2 * 2048], F32)
            gbf = sb2("l3_gbf", [128, 2 * 2048], BF16)
            rd_str2 = [self.stream("l3ar0"), self.stream("l3ar1")]
            rfree = {}
            mv = None
            for dm in range(8):
                par = dm % 2
                arr = accrd[:, par * 2048:(par + 1) * 2048]
                gb = gbf[:, par * 2048:(par + 1) * 2048]
                ld = P.dma("sync", arr, acc_dram[dm * 128:(dm + 1) * 128, :], rd_str2[par],
                           deps=[rfree.get(dm - 2), (accst2[0], accst2[0].n),
                                 (accst2[1], accst2[1].n)])
                sg2 = P.op("scalar", "activation", gb, arr, AF.Silu, sem=sa,
                           deps=[ld, rfree.get(dm - 2)])
                mdeps = [sg2]
                if dm == 0 and sT_ready is not None:
                    mdeps.append(sT_ready)
                mv = P.op("vector", "tensor_tensor", self.sT[:, dm * T:(dm + 1) * T],
                          self.sT[:, dm * T:(dm + 1) * T], gb, ALU.mult, sem=sv,
                          deps=mdeps)
                rfree[dm] = mv
        self.tok_sT = mv
        return mv

    def layer3(self, x_src, x_dst, deps=()):
        self.ln_phase(x_src, deps=deps)
        ta = self.l3_attn(deps=deps)
        tf = self.l3_ffn(sT_ready=ta, deps=[ta])
        return self.residual_phase(x_src, x_dst, deps=[tf])


WSHAPES = {
    "L0_attn_Wd_q_w": ([1024, 512], "f"), "L0_attn_Wd_q_b": ([512], "f"),
    "L0_attn_Wu_q_w": ([512, 1024], "f"), "L0_attn_Wu_q_b": ([1024], "f"),
    "L0_attn_Wq_r_w": ([512, 1024], "f"), "L0_attn_Wq_r_b": ([1024], "f"),
    "L0_attn_Wk_r_w": ([1024, 1024], "f"), "L0_attn_Wk_r_b": ([1024], "f"),
    "L0_attn_Wd_kv_w": ([1024, 512], "f"), "L0_attn_Wd_kv_b": ([512], "f"),
    "L0_attn_Wu_k_w": ([512, 1024], "f"), "L0_attn_Wu_k_b": ([1024], "f"),
    "L0_attn_Wu_v_w": ([512, 1024], "f"), "L0_attn_Wu_v_b": ([1024], "f"),
    "L0_attn_Wo_w": ([1024, 1024], "f"), "L0_attn_Wo_b": ([1024], "f"),
    "L0_ffn_w_gate": ([3072, 1024], "i"), "L0_ffn_w_up": ([3072, 1024], "i"),
    "L0_ffn_w_down": ([1024, 3072], "i"),
    "L1_attn_Wqkv": ([3072, 1024], "f"), "L1_attn_Wout": ([1024, 1024], "f"),
    "L1_attn_alpha": ([16, 32], "f"), "L1_attn_beta": ([16, 32], "f"),
    "L1_ffn_phi_w": ([2048, 1024], "f"), "L1_ffn_phi_b": ([2048], "f"),
    "L1_ffn_expert_w": ([64, 1024, 1024], "f"), "L1_ffn_expert_b": ([64, 1024], "f"),
    "L2_attn_W_qkv": ([1024, 3072], "f"), "L2_attn_W_q_lsr": ([16, 64, 16], "f"),
    "L2_attn_W_k_lsr": ([16, 64, 16], "f"), "L2_attn_core": ([16, 16], "f"),
    "L2_attn_W_o": ([1024, 1024], "f"),
    "L2_ffn_sg": ([2816, 1024], "f"), "L2_ffn_su": ([2816, 1024], "f"),
    "L2_ffn_sd": ([1024, 2816], "f"),
    "L2_ffn_rg": ([7, 2816, 1024], "f"), "L2_ffn_ru": ([7, 2816, 1024], "f"),
    "L2_ffn_rd": ([7, 1024, 2816], "f"),
    "L2_ffn_wr": ([7, 1024], "f"), "L2_ffn_rb": ([7], "f"),
    "L3_attn_Wqkv": ([1024, 3072], "f"), "L3_attn_Wproj": ([1024, 1024], "f"),
    "L3_attn_lambda_q1": ([512], "f"), "L3_attn_lambda_k1": ([512], "f"),
    "L3_attn_lambda_q2": ([512], "f"), "L3_attn_lambda_k2": ([512], "f"),
    "L3_ffn_gate_weight": ([32, 1024], "f"), "L3_ffn_gate_bias": ([32], "f"),
    "L3_ffn_w_gate": ([32, 1024, 512], "f"), "L3_ffn_w_up": ([32, 1024, 512], "f"),
    "L3_ffn_w_down": ([32, 512, 1024], "f"),
    "L3_ffn_shared_w_gate": ([1024, 512], "f"), "L3_ffn_shared_w_up": ([1024, 512], "f"),
    "L3_ffn_shared_w_down": ([512, 1024], "f"),
    "L3_ffn_anchor_w_gate": ([16, 1024, 256], "f"), "L3_ffn_anchor_w_up": ([16, 1024, 256], "f"),
    "L3_ffn_anchor_w_down": ([16, 256, 1024], "f"),
}


def _rope_tables():
    pos = np.arange(T, dtype=np.float64)[:, None]
    ids = np.arange(512, dtype=np.float64)
    theta = 10000.0 ** (-2.0 * ids) / 512.0
    r = pos * theta[None, :]
    return np.cos(r).astype(np.float32), np.sin(r).astype(np.float32)


def _fourier_tables():
    r = np.arange(32, dtype=np.float64)
    om = 10000.0 ** (-2.0 * r / 64.0)
    k = np.arange(T, dtype=np.float64)
    ang = om[:, None] * k[None, :]
    return np.cos(ang).astype(np.float32), np.sin(ang).astype(np.float32)


def _build(ncores):
    nc = bass.Bass("TRN2", target_bir_lowering=False)
    es = ExitStack()
    with es:
        w = {}
        x_in = nc.dram_tensor("x_in", [T, D], F32, kind="ExternalInput")
        x_out = nc.dram_tensor("x_out", [T, D], F32, kind="ExternalOutput")
        for nm, (shp, k) in WSHAPES.items():
            w[nm] = nc.dram_tensor(nm, shp, I32 if k == "i" else F32, kind="ExternalInput")
        rc = nc.dram_tensor("rope_cos", [T, 512], F32, kind="ExternalInput")
        rs = nc.dram_tensor("rope_sin", [T, 512], F32, kind="ExternalInput")
        fc = nc.dram_tensor("f_cos", [32, T], F32, kind="ExternalInput")
        fs = nc.dram_tensor("f_sin", [32, T], F32, kind="ExternalInput")
        xb1 = nc.dram_tensor("xb1", [T, D], F32)
        xb2 = nc.dram_tensor("xb2", [T, D], F32)
        xb3 = nc.dram_tensor("xb3", [T, D], F32)
        ctx = Ctx(nc, es, w)
        r0 = ctx.layer0(x_in, xb1, rc, rs)
        r1 = ctx.layer1(xb1, xb2, fc, fs, [list(range(ncores))], deps=[r0])
        r2 = ctx.layer2(xb2, xb3, deps=[r1])
        r3 = ctx.layer3(xb3, x_out, deps=[r2])
        ctx.finish()
        ctx.P.run()
    return nc


def kernel(**inputs):
    NC = 8
    x = np.ascontiguousarray(inputs["L0_attn_x"]).astype(np.float32)
    rc, rs = _rope_tables()
    fc, fs = _fourier_tables()
    base = {}
    for nm, (shp, k) in WSHAPES.items():
        base[nm] = np.ascontiguousarray(inputs[nm]).astype(np.int32 if k == "i" else np.float32)
    base["rope_cos"] = rc
    base["rope_sin"] = rs
    base["f_cos"] = fc
    base["f_sin"] = fs
    in_maps = []
    for b in range(NC):
        m = dict(base)
        m["x_in"] = np.ascontiguousarray(x[b])
        in_maps.append(m)
    nc = _build(NC)
    res = run_bass_kernel_spmd(nc, in_maps, core_ids=list(range(NC)))
    out = np.stack([res.results[b]["x_out"] for b in range(NC)], axis=0)
    return out.astype(np.float32)


# version 04
# speedup vs baseline: 1.1184x; 1.1184x over previous
from concourse.bass_utils import run_bass_kernel_spmd

import math
from contextlib import ExitStack

import numpy as np
import ml_dtypes
import concourse.bass as bass
import concourse.mybir as mybir

F32 = mybir.dt.float32
BF16 = mybir.dt.bfloat16
I32 = mybir.dt.int32
AF = mybir.ActivationFunctionType
ALU = mybir.AluOpType
AX = mybir.AxisListType
bfdt = ml_dtypes.bfloat16

ENGINES = ("sync", "gpsimd", "tensor", "vector", "scalar")

T = 2048
D = 1024
TT = T // 128
DC = D // 128
NEG = -30000.0


class Sem:
    __slots__ = ("h", "n", "name")

    def __init__(self, h, name):
        self.h = h
        self.n = 0
        self.name = name


class Prog:
    def __init__(self, nc, es):
        self.nc = nc
        self.es = es
        self.ops = {e: [] for e in ENGINES}
        self._lastwait = {}

    def sem(self, name):
        return Sem(self.es.enter_context(self.nc.semaphore(name)), name)

    def emit(self, eng, fn):
        self.ops[eng].append(fn)

    def dma(self, queue, out, in_, sem, deps=(), **kw):
        for d in deps:
            if d is not None:
                self.wait(queue, d[0], d[1])
        sem.n += 16

        def f(e, out=out, in_=in_, s=sem.h, kw=kw):
            e.dma_start(out=out, in_=in_, **kw).then_inc(s, 16)

        self.emit(queue, f)
        return (sem, sem.n)

    def wait(self, eng, sem, v=None):
        v = sem.n if v is None else v
        key = (eng, sem.name)
        if self._lastwait.get(key, -1) >= v:
            return
        self._lastwait[key] = v

        def f(e, s=sem.h, v=v):
            e.wait_ge(s, v)

        self.emit(eng, f)

    def wtok(self, eng, *toks):
        for d in toks:
            if d is not None:
                self.wait(eng, d[0], d[1])

    def op(self, eng, method, *args, sem=None, deps=(), inc=1, **kw):
        self.wtok(eng, *deps)
        if sem is not None:
            sem.n += inc

        def f(e, args=args, kw=kw, sem=sem, inc=inc):
            inst = getattr(e, method)(*args, **kw)
            if sem is not None:
                inst.then_inc(sem.h, inc)

        self.emit(eng, f)
        return (sem, sem.n) if sem is not None else None

    def run(self):
        with self.nc.Block() as block:
            for eng in ENGINES:
                ops = self.ops[eng]
                if not ops:
                    continue

                def body(e, ops=ops):
                    for fn in ops:
                        fn(e)

                getattr(block, eng)(body)




class _PlainAug:

    def __init__(self, plain, aug):
        self.plain = plain
        self.aug = aug

    def __getitem__(self, idx):
        rows, cols = idx
        if rows == slice(0, 64):
            return self.plain[:, cols]
        if rows == slice(64, 65):
            return self.aug[64:65, cols]
        raise KeyError(idx)


def cdiv(a, b):
    return (a + b - 1) // b


class Ctx:
    def __init__(self, nc, es, wdram):
        self.nc = nc
        self.es = es
        self.w = wdram
        self.P = Prog(nc, es)
        P = self.P
        self._streams = {}
        self.st = P.sem("st")
        self.sv = P.sem("sv")
        self.sa = P.sem("sa")
        self.sg = P.sem("sg")
        self.cc = P.sem("cc")
        sb = lambda name, shape, dt: es.enter_context(nc.sbuf_tensor(name, shape, dt))
        self.sb = sb
        self.xnT = sb("xnT", [128, DC * T], BF16)
        self.sT = sb("sT", [128, DC * T], BF16)
        ps = lambda name, shape, dt: es.enter_context(nc.psum_tensor(name, shape, dt))
        self.psA = ps("psA", [128, 2048], F32)
        self.psB = ps("psB", [128, 512], F32)
        self.psC = ps("psC", [128, 512], F32)
        self.psD = ps("psD", [128, 512], F32)
        self.psT16 = ps("psT16", [128, 1024], BF16)
        self.ident = sb("ident", [128, 128], BF16)
        identd = nc.inline_tensor(np.eye(128, dtype=bfdt), "identconst")
        cst = self.stream("const")
        P.dma("sync", self.ident[:, :], identd[:, :], cst)
        self.onescol = sb("onescol", [128, 1], BF16)
        self.onesrow = sb("onesrow", [1, 512], BF16)
        P.op("vector", "memset", self.onescol[:, :], 1.0)
        self.tok_ones = P.op("vector", "memset", self.onesrow[:, :], 1.0, sem=self.sv)
        mk = np.where(np.arange(128)[:, None] >= np.arange(128)[None, :], 0.0, NEG)
        self.diagmask = sb("diagmask", [128, 128], F32)
        md = nc.inline_tensor(mk.astype(np.float32), "diagmaskc")
        P.dma("sync", self.diagmask[:, :], md[:, :], cst)
        self.diagmaskT = sb("diagmaskT", [128, 128], F32)
        mdT = nc.inline_tensor(mk.T.copy().astype(np.float32), "diagmaskTc")
        self.negmask = sb("negmask", [128, 128], F32)
        nm = nc.inline_tensor(np.full((128, 128), NEG, np.float32), "negmaskc")
        P.dma("sync", self.negmask[:, :], nm[:, :], cst)
        self.tok_consts = P.dma("sync", self.diagmaskT[:, :], mdT[:, :], cst)
        self.tok_ident = self.tok_diag = self.tok_diagT = self.tok_consts
        self.x_dram = nc.dram_tensor("x_stream", [T, D], F32)
        self._proj_ps_free = {}
        self._uid = 0
        self.dbg_on = False
        self.dbg_names = []
        self._dbg_tok = None

    def dbg(self, name, ap, tok, dt=BF16):
        if not self.dbg_on:
            return
        shape = [ap.shape[0], ap.shape[1]]
        d = self.nc.dram_tensor("dbg_" + name, shape, dt, kind="ExternalOutput")
        self._dbg_tok = self.P.dma("sync", d[:, :], ap, self.stream("dbg"), deps=[tok])
        self.dbg_names.append(("dbg_" + name, dt))

    def uname(self, base):
        self._uid += 1
        return f"{base}_u{self._uid}"

    def stream(self, name):
        s = self._streams.get(name)
        if s is None:
            s = self.P.sem("D_" + name)
            self._streams[name] = s
        return s

    def finish(self):
        for s in self._streams.values():
            self.P.wait("sync", s)

    def ln_phase(self, x_src_dram, xn_tm_dst=None, deps=()):
        P, st, sv, sa = self.P, self.st, self.sv, self.sa
        lnes = ExitStack()
        xs_streams = [self.stream(f"xs{p}") for p in range(3)]
        cons = {}
        evac = {}
        nst = {}
        with lnes:
            S = lnes.enter_context(self.nc.sbuf_tensor(self.uname("lnstats"), [128, 64], F32))
            self.xstg = lnes.enter_context(self.nc.sbuf_tensor(self.uname("xstg"), [128, 3 * 1024], F32))
            self.xsqbuf = lnes.enter_context(self.nc.sbuf_tensor(self.uname("xsqbuf"), [128, 2 * 1024], F32))
            self.nstg = lnes.enter_context(self.nc.sbuf_tensor(self.uname("nstg"), [128, 2 * 1024], BF16))
            self._ln_loop(P, st, sv, sa, S, x_src_dram, xn_tm_dst, xs_streams, cons, evac, nst, deps)
        return self.tok_xnT

    def _ln_loop(self, P, st, sv, sa, S, x_src_dram, xn_tm_dst, xs_streams, cons, evac, nst, ldeps0=()):
        for t in range(TT):
            p3 = t % 3
            p2 = t % 2
            xs = self.xstg[:, p3 * 1024:(p3 + 1) * 1024]
            xq = self.xsqbuf[:, p2 * 1024:(p2 + 1) * 1024]
            ns = self.nstg[:, p2 * 1024:(p2 + 1) * 1024]
            so = p2 * 8
            s1, s2, mean, msq, var, rstd, nmr = (S[:, so + i:so + i + 1] for i in range(7))
            deps = [cons.get(t - 3, [None, None])[0], cons.get(t - 3, [None, None])[1]]
            if t < 3:
                deps += list(ldeps0) + [(xs_streams[p3], xs_streams[p3].n)]
            ld = P.dma("sync", xs, x_src_dram[t * 128:(t + 1) * 128, :], xs_streams[p3], deps=deps)
            t1 = P.op("vector", "tensor_reduce", s1, xs, AX.X, ALU.add, sem=sv, deps=[ld])
            t2 = P.op("vector", "tensor_tensor", xq, xs, xs, ALU.mult, sem=sv)
            t3 = P.op("vector", "tensor_reduce", s2, xq, AX.X, ALU.add, sem=sv, deps=[t2])
            t4 = P.op("vector", "tensor_scalar", mean, s1, 1.0 / D, None, ALU.mult, sem=sv, deps=[t1])
            t5 = P.op("vector", "tensor_tensor", msq, mean, mean, ALU.mult, sem=sv, deps=[t4])
            t6 = P.op("vector", "scalar_tensor_tensor", var, s2, 1.0 / D, msq, ALU.mult,
                      ALU.subtract, sem=sv, deps=[t3, t5])
            t6b = P.op("vector", "tensor_scalar", var, var, 1e-5, None, ALU.add, sem=sv, deps=[t6])
            t7 = P.op("scalar", "activation", rstd, var, AF.Sqrt, sem=sa, deps=[t6b])
            t8 = P.op("vector", "reciprocal", rstd, rstd, sem=sv, deps=[t7])
            t9 = P.op("vector", "scalar_tensor_tensor", nmr, mean, -1.0, rstd, ALU.mult,
                      ALU.mult, sem=sv, deps=[t8])
            ndeps = [t9, nst.get(t - 2)]
            t10 = P.op("scalar", "activation", ns, xs, AF.Identity, bias=nmr, scale=rstd,
                       sem=sa, deps=ndeps)
            cons[t] = [t10, t6]
            if xn_tm_dst is not None:
                P.op("vector", "tensor_copy", xn_tm_dst[:, t * 1024:(t + 1) * 1024], ns,
                     sem=sv, deps=[t10])
            tdeps = [t10, evac.get(t - 1), self.tok_ident] + (list(ldeps0) if t == 0 else [])
            last = None
            for c in range(DC):
                last = P.op("tensor", "transpose", self.psT16[:, c * 128:(c + 1) * 128],
                            ns[:, c * 128:(c + 1) * 128], self.ident[:, :],
                            sem=(st if c == DC - 1 else None),
                            deps=(tdeps if c == 0 else ()))
            dst = self.xnT[:, :].rearrange("p (c j) -> p c j", j=T)[:, :, t * 128:(t + 1) * 128]
            src = self.psT16[:, :].rearrange("p (c j) -> p c j", j=128)
            ev = P.op("vector", "tensor_copy", dst, src, sem=sv, deps=[last])
            evac[t] = ev
            nst[t] = last
        self.tok_xnT = ev

    def mm_acc(self, out_ap, pairs, sem=None, deps=(), extra=()):
        P = self.P
        n = len(pairs) + len(extra)
        tok = None
        for i, (l, r) in enumerate(list(pairs) + list(extra)):
            tok = P.op("tensor", "matmul", out_ap, l, r,
                       start=(i == 0), stop=(i == n - 1),
                       sem=(sem if i == n - 1 else None),
                       deps=(deps if i == 0 else ()))
        return tok

    def cast_weight_to_dram(self, name, dst_shape=None):
        src = self.w[name]
        shp = list(src.shape) if dst_shape is None else dst_shape
        dst = self.nc.dram_tensor(name + "_bf", shp, BF16)
        strm = self.stream("castall")
        self.P.dma("gpsimd", dst[:, :] if len(shp) == 2 else dst[:, :, :],
                   src[:, :] if len(src.shape) == 2 else src[:, :, :], strm)
        return dst, (strm, None)

    def residual_phase(self, x_src_dram, x_dst_dram, deps=()):
        P, st, sv = self.P, self.st, self.sv
        xs_streams = [self.stream(f"xs{p}") for p in range(3)]
        evtok = {}
        sttok = {}
        res_es = ExitStack()
        with res_es:
            self.xstg = res_es.enter_context(self.nc.sbuf_tensor(self.uname("xstg_r"), [128, 3 * 1024], F32))
            stw = self._res_loop(P, st, sv, x_src_dram, x_dst_dram, deps, xs_streams, evtok, sttok)
        return stw

    def _res_loop(self, P, st, sv, x_src_dram, x_dst_dram, deps, xs_streams, evtok, sttok):
        for t in range(TT):
            p3 = t % 3
            xs = self.xstg[:, p3 * 1024:(p3 + 1) * 1024]
            lddeps = [sttok.get(t - 3)] + (list(deps) if t == 0 else [])
            if t < 3:
                lddeps += [(xs_streams[p3], xs_streams[p3].n)]
            ld = P.dma("sync", xs, x_src_dram[t * 128:(t + 1) * 128, :], xs_streams[p3], deps=lddeps)
            tdeps = [evtok.get(t - 1)] + (list(deps) if t == 0 else [])
            last = None
            sgv = self.sT[:, :].rearrange("p (c j) -> p c j", j=T)
            for c in range(DC):
                last = P.op("tensor", "transpose", self.psT16[:, c * 128:(c + 1) * 128],
                            sgv[:, c, t * 128:(t + 1) * 128], self.ident[:, :],
                            sem=(st if c == DC - 1 else None),
                            deps=(tdeps if c == 0 else ()))
            ad = P.op("vector", "tensor_tensor", xs, xs,
                      self.psT16[:, :], ALU.add, sem=sv, deps=[last, ld])
            evtok[t] = ad
            stw = P.dma("sync", x_dst_dram[t * 128:(t + 1) * 128, :], xs, xs_streams[p3], deps=[ad])
            sttok[t] = stw
        return stw

    def l0_ffn(self, sT_ready=None, deps=()):
        P = self.P
        st, sv, sa = self.st, self.sv, self.sa
        SC = 1.0 / 127.0
        wst = [self.stream("l0w0"), self.stream("l0w1")]
        wdst = self.stream("l0wd")
        wg_bf, tg = self.cast_weight_to_dram("L0_ffn_w_gate")
        wu_bf, tu = self.cast_weight_to_dram("L0_ffn_w_up")
        wd_bf, td = self.cast_weight_to_dram("L0_ffn_w_down")
        es0 = ExitStack()
        sb0 = lambda n, sh, dt: es0.enter_context(self.nc.sbuf_tensor(n, sh, dt))
        wgbuf = sb0("l0_wg", [128, 2 * 8 * 512], BF16)
        wubuf = sb0("l0_wu", [128, 2 * 8 * 512], BF16)
        wdbuf = sb0("l0_wd", [128, 24 * 1024], BF16)
        hbuf = sb0("l0_h", [128, 24 * 512], BF16)
        gstg = sb0("l0_gs", [128, 2 * 512], BF16)
        dstg = sb0("l0_ds", [128, 2 * 512], BF16)
        NCH, HC = 4, 6
        gfree = {1: sT_ready}
        ufree = {1: sT_ready}
        dfree = {1: sT_ready}
        gsl_free = {}
        dsl_free = {}
        wpar_free = {}
        down_done = None
        gidx = 0
        didx = 0
        mv = None
        for nc4 in range(NCH):
            wdeps = [td, down_done] + (list(deps) if nc4 == 0 else [])
            wd_ld = None
            for j in range(24):
                wd_ld = P.dma("sync", wdbuf[:, j * 1024:(j + 1) * 1024],
                              wd_bf[:, j * 128:(j + 1) * 128], wdst,
                              deps=(wdeps if j == 0 else ()), transpose=True)
            hv = None
            u_tok = None
            for hc in range(HC):
                par = hc % 2
                wg_sl = wgbuf[:, par * 4096:(par + 1) * 4096]
                wu_sl = wubuf[:, par * 4096:(par + 1) * 4096]
                h0 = hc * 512
                pi = nc4 * HC + hc
                ldeps = [tg, tu, wpar_free.get(pi - 2)] + (list(deps) if pi == 0 else [])
                lg = None
                for dk in range(8):
                    lg = P.dma("sync", wg_sl[:, dk * 512:(dk + 1) * 512],
                               wg_bf[h0:h0 + 512, dk * 128:(dk + 1) * 128], wst[par],
                               deps=(ldeps if dk == 0 else ()), transpose=True)
                lu = None
                for dk in range(8):
                    lu = P.dma("sync", wu_sl[:, dk * 512:(dk + 1) * 512],
                               wu_bf[h0:h0 + 512, dk * 128:(dk + 1) * 128], wst[par],
                               transpose=True)
                for mt in range(4):
                    gp = gidx % 2
                    gsl = gstg[:, gp * 512:(gp + 1) * 512]
                    pg = self.psB[:, 0:512] if gp == 0 else self.psA[:, 0:512]
                    pu = self.psC[:, 0:512] if gp == 0 else self.psA[:, 512:1024]
                    xsl = lambda dk: self.xnT[:, dk * T + nc4 * 512: dk * T + nc4 * 512 + 512]
                    prs = [(wg_sl[:, dk * 512 + mt * 128: dk * 512 + mt * 128 + 128], xsl(dk))
                           for dk in range(8)]
                    gdeps = [lu, self.tok_xnT, gfree.get(gp)] + (list(deps) if gidx == 0 else [])
                    g_tok = self.mm_acc(pg, prs, sem=st, deps=gdeps)
                    sil = P.op("scalar", "activation", gsl, pg, AF.Silu,
                               scale=SC, sem=sa, deps=[g_tok, gsl_free.get(gidx - 2)])
                    gfree[gp] = sil
                    pru = [(wu_sl[:, dk * 512 + mt * 128: dk * 512 + mt * 128 + 128], xsl(dk))
                           for dk in range(8)]
                    u_tok = self.mm_acc(pu, pru, sem=st,
                                        deps=[ufree.get(gp)] + (list(deps) if gidx == 1 else []))
                    kh = hc * 4 + mt
                    hdeps = [u_tok, sil]
                    if nc4 > 0 and hc == 0 and mt == 0:
                        hdeps.append(down_done)
                    hv = P.op("vector", "scalar_tensor_tensor",
                              hbuf[:, kh * 512:(kh + 1) * 512], pu, SC, gsl,
                              ALU.mult, ALU.mult, sem=sv, deps=hdeps)
                    ufree[gp] = hv
                    gsl_free[gidx] = hv
                    gidx += 1
                wpar_free[pi] = u_tok
            d_tok = None
            for dm in range(8):
                dp = didx % 2
                pd = self.psD[:, 0:512] if dp == 0 else self.psA[:, 1024:1536]
                prd = [(wdbuf[:, kh * 1024 + dm * 128: kh * 1024 + dm * 128 + 128],
                        hbuf[:, kh * 512:(kh + 1) * 512]) for kh in range(24)]
                d_tok = self.mm_acc(pd, prd, sem=st,
                                    deps=[hv, wd_ld, dfree.get(dp)] + (list(deps) if didx == 0 else []))
                dsl = dstg[:, dp * 512:(dp + 1) * 512]
                sil2 = P.op("scalar", "activation", dsl, pd, AF.Silu,
                            scale=SC, sem=sa, deps=[d_tok, dsl_free.get(didx - 2)])
                dfree[dp] = sil2
                ssl = self.sT[:, dm * T + nc4 * 512: dm * T + nc4 * 512 + 512]
                mdeps = [sil2]
                if nc4 == 0 and dm == 0 and sT_ready is not None:
                    mdeps.append(sT_ready)
                mv = P.op("vector", "tensor_tensor", ssl, ssl, dsl, ALU.mult,
                          sem=sv, deps=mdeps)
                dsl_free[didx] = mv
                didx += 1
            down_done = d_tok
        es0.close()
        self.tok_sT = mv
        return mv

    def load_bias_cols2(self, scope, name, m, deps=()):
        b = scope("b_" + name, [128, m], F32)
        ap = self.w[name].ap().rearrange("(m p) -> p m", p=128)
        strm = self.stream("biasS")
        self.P.dma("sync", b[:, :], ap, strm, deps=deps,
                   allow_slow_non_contiguous=True)
        return b, (strm, None)

    def load_bias_row2(self, scope, name, n, deps=()):
        b = scope("br_" + name, [1, n], BF16)
        strm = self.stream("biasG")
        self.P.dma("gpsimd", b[0:1, :], self.w[name].ap().rearrange("(o n) -> o n", o=1),
                   strm, deps=deps)
        return b, (strm, None)

    def load_w_nat(self, name, kin, nout, scope_sb, deps=(), tag=""):
        kt = kin // 128
        sb = scope_sb("w_" + name + tag, [128, kt * nout], BF16)
        strm = self.stream("wnat")
        for k in range(kt):
            self.P.dma("gpsimd", sb[:, k * nout:(k + 1) * nout],
                       self.w[name][k * 128:(k + 1) * 128, :], strm,
                       deps=((list(deps) + [self._dbg_tok]) if k == 0 else ()))
        return sb, (strm, None)

    def attn_heads(self, sbf, nheads, augq, augk, vcol, attn_dst, kmax_fn, causal,
                   sT_extra=None, qk_extra=None, deps=(), pst_free=None, pt_buf=None):
        P = self.P
        st, sv, sa = self.st, self.sv, self.sa
        PT = pt_buf if pt_buf is not None else sbf(self.uname("PTbuf"), [128, 8 * 512], BF16)
        pv_hist = {}
        self._ah_expfree = {}
        self._ah_psD_busy = None
        rcrow = sbf(self.uname("rcrow"), [1, 512], F32)
        rcbf = sbf(self.uname("rcbf"), [1, 512], BF16)
        rcsb = sbf(self.uname("rcsb"), [64, 512], BF16)
        mrow = sbf(self.uname("mrow"), [128, 1], F32)
        mneg = sbf(self.uname("mneg"), [128, 1], BF16)
        tok_qk = None
        tok_red = None
        tok_exp = None
        tok_sTmm = None
        tok_pv = None
        tok_attn = None
        tok_bc = None
        tok_rc = None
        tok_stat = pst_free
        for h in range(nheads):
            prev = {"attn": tok_attn, "red": tok_red, "exp": tok_exp, "pv": tok_pv}
            aq, tq = augq(h, prev)
            ak, tk = augk(h, prev)
            for qg in range(4):
                qts = [4 * qg + i for i in range(4)]
                qke = qk_extra(h) if qk_extra is not None else None
                for qt in qts:
                    kmax = kmax_fn(qt)
                    nch = cdiv(kmax, 512)
                    first = True
                    for c in range(nch):
                        k1 = min(512, kmax - c * 512)
                        dd = [tq, tk, tok_red] + list(deps) if first else []
                        tok_qk = P.op("tensor", "matmul", self.psA[:, c * 512:c * 512 + k1],
                                      aq[0:64, qt * 128:(qt + 1) * 128],
                                      ak[0:64, c * 512:c * 512 + k1],
                                      start=True, stop=(qke is None), sem=st, deps=dd,
                                      skip_group_check=True)
                        if qke is not None:
                            ql, qr = qke
                            tok_qk = P.op("tensor", "matmul",
                                          self.psA[:, c * 512:c * 512 + k1],
                                          ql(qt), qr(c, k1),
                                          start=False, stop=True, sem=st,
                                          skip_group_check=True)
                        first = False
                    if causal:
                        tok_qk = P.op("vector", "tensor_tensor",
                                      self.psA[:, qt * 128:qt * 128 + 128],
                                      self.psA[:, qt * 128:qt * 128 + 128],
                                      self.diagmask[:, :], ALU.add, sem=sv,
                                      deps=[tok_qk])
                    tok_red = P.op("vector", "tensor_reduce", mrow[:, :],
                                   self.psA[:, 0:kmax], AX.X, ALU.max, sem=sv,
                                   deps=[tok_qk, tok_stat])
                    tneg = P.op("scalar", "activation", mneg[:, :], mrow[:, :],
                                AF.Identity, scale=-1.0, sem=sa, deps=[tok_red])
                    ttr = P.op("tensor", "transpose", self.psT16[0:1, 0:128],
                               mneg[:, :], self.ident[:, :], sem=st,
                               deps=[tneg, tok_stat])
                    tok_stat = P.op("scalar", "copy", aq[64:65, qt * 128:(qt + 1) * 128],
                                    self.psT16[0:1, 0:128], sem=sa, deps=[ttr])
                ktn = cdiv(kmax_fn(qts[-1]), 128)
                extra = sT_extra(h) if sT_extra is not None else None
                exp_hist = {}
                for kt in range(ktn):
                    ks = kt % 8
                    pS = self.psB if kt % 2 == 0 else self.psD
                    dd = [tok_stat, self._ah_expfree.get((id(pS), 0)),
                          self._ah_psD_busy if (pS is self.psD) else None]
                    if extra is None:
                        tok_sTmm = P.op("tensor", "matmul", pS[:, :],
                                        ak[0:65, kt * 128:(kt + 1) * 128],
                                        aq[0:65, qg * 512:(qg + 1) * 512],
                                        start=True, stop=True, sem=st, deps=dd,
                                        skip_group_check=True)
                    else:
                        l2, r2 = extra
                        P.op("tensor", "matmul", pS[:, :],
                             ak[0:64, kt * 128:(kt + 1) * 128],
                             aq[0:64, qg * 512:(qg + 1) * 512],
                             start=True, stop=False, sem=None, deps=dd,
                             skip_group_check=True)
                        tok_sTmm = P.op("tensor", "matmul", pS[:, :],
                                        l2(kt), r2(qg),
                                        start=False, stop=True, sem=st,
                                        skip_group_check=True)
                    mt = self._mask_subtiles(qg, kt, kmax_fn) if causal else []
                    tok_m = tok_sTmm
                    for (sub, mask) in mt:
                        tok_m = P.op("vector", "tensor_tensor",
                                     pS[:, sub * 128:(sub + 1) * 128],
                                     pS[:, sub * 128:(sub + 1) * 128],
                                     mask, ALU.add, sem=sv, deps=[tok_m])
                    tok_exp = P.op("scalar", "activation", PT[:, ks * 512:(ks + 1) * 512],
                                   pS[:, :], AF.Exp, sem=sa,
                                   deps=[tok_m, pv_hist.get(kt - 8)])
                    self._ah_expfree[(id(pS), 0)] = tok_exp
                    exp_hist[kt] = tok_exp
                    if kt > 0:
                        pk = kt - 1
                        tok_pv = P.op("tensor", "matmul", self.psC[0:65, :],
                                      vcol(h, pk), PT[:, (pk % 8) * 512:(pk % 8) * 512 + 512],
                                      start=(pk == 0), stop=False, sem=st,
                                      deps=[exp_hist[pk], tok_attn if pk == 0 else None],
                                      skip_group_check=True)
                        pv_hist[pk] = tok_pv
                if ktn == 1:
                    tok_pv = P.op("tensor", "matmul", self.psC[0:65, :],
                                  vcol(h, 0), PT[:, 0:512],
                                  start=True, stop=True, sem=st,
                                  deps=[exp_hist[0], tok_attn],
                                  skip_group_check=True)
                    pv_hist[0] = tok_pv
                else:
                    pk = ktn - 1
                    tok_pv = P.op("tensor", "matmul", self.psC[0:65, :],
                                  vcol(h, pk), PT[:, (pk % 8) * 512:(pk % 8) * 512 + 512],
                                  start=False, stop=True, sem=st,
                                  deps=[exp_hist[pk]],
                                  skip_group_check=True)
                    pv_hist[pk] = tok_pv
                tok_rc = P.op("vector", "reciprocal", rcrow[0:1, :], self.psC[64:65, :],
                              sem=sv, deps=[tok_pv, tok_bc])
                trc2 = P.op("scalar", "copy", rcbf[0:1, :], rcrow[0:1, :], sem=sa,
                            deps=[tok_rc])
                tok_bc = P.op("tensor", "matmul", self.psD[0:64, :],
                              self.onesrow[0:1, 0:64], rcbf[0:1, :],
                              start=True, stop=True, sem=st,
                              deps=[trc2, self._ah_expfree.get((id(self.psD), 0))],
                              skip_group_check=True)
                trc3 = P.op("scalar", "copy", rcsb[:, :], self.psD[0:64, :], sem=sa,
                            deps=[tok_bc])
                self._ah_psD_busy = trc3
                tok_attn = P.op("vector", "tensor_tensor", attn_dst(h, qg),
                                self.psC[0:64, :], rcsb[:, :], ALU.mult, sem=sv,
                                deps=[trc3])
        return tok_attn

    def _mask_subtiles(self, qg, kt, kmax_fn):
        out = []
        for i in range(4):
            qt = 4 * qg + i
            kmax = kmax_fn(qt)
            if kt * 128 >= kmax:
                out.append((i, self.negmask[:, :]))
            elif kt == qt:
                out.append((i, self.diagmaskT[:, :]))
        return out

    def proj_T(self, wsb, wtok, kt, dst_sb, dst_cols, mout, bias_sb, rhs_sb, rhs_cols,
               extra_dep=None, scale=None, psum=None, dst_rowoff=0, evac_bf=True,
               nch=4, ncw=512, rhs_T=T):
        P = self.P
        ps = psum or self.psB
        tok_ev = None
        for m in range(mout):
            for c in range(nch):
                prs = [(wsb[:, k * (mout * 128) + m * 128: k * (mout * 128) + m * 128 + 128],
                        rhs_sb[:, k * rhs_T + rhs_cols + c * ncw: k * rhs_T + rhs_cols + c * ncw + ncw])
                       for k in range(kt)]
                dd = [wtok, self._proj_ps_free.get(id(ps)), extra_dep]
                tokmm = self.mm_acc(ps[:, 0:ncw], prs, sem=self.st, deps=dd)
                dst = dst_cols(m, c)
                kw = {}
                if bias_sb is not None:
                    kw["bias"] = bias_sb[:, m:m + 1]
                if scale is not None:
                    kw["scale"] = scale
                tok_ev = P.op("scalar", "activation", dst, ps[:, 0:ncw], AF.Identity,
                              sem=self.sa, deps=[tokmm], **kw)
                self._proj_ps_free[id(ps)] = tok_ev
        return tok_ev

    def _rope_stream(self, src_sb, src_kt, wsb, wt, brow, bt, dstb, prevdep,
                     rope_cos, rope_sin, base_ii, scope):
        P, st, sv = self.P, self.st, self.sv
        csbuf = scope(f"csbuf{base_ii}", [128, 2 * 1024], F32)
        rtmp = scope(f"rtmp{base_ii}", [128, 2 * 512], F32)
        ropestg = scope(f"ropestg{base_ii}", [128, 2 * 1024], BF16)
        cs_str = [self.stream(f"cs{base_ii}_0"), self.stream(f"cs{base_ii}_1")]
        ta = None
        for tt in range(TT):
            par = tt % 2
            ii = base_ii * TT + tt
            cs = csbuf[:, par * 1024:(par + 1) * 1024]
            lc = P.dma("sync", cs[:, 0:512], rope_cos[tt * 128:(tt + 1) * 128, :],
                       cs_str[par], deps=[self._rp_csfree.get(ii - 2)])
            ls = P.dma("sync", cs[:, 512:1024], rope_sin[tt * 128:(tt + 1) * 128, :],
                       cs_str[par])
            prs = [(src_sb[:, k * T + tt * 128: k * T + tt * 128 + 128],
                    wsb[:, k * 1024: k * 1024 + 512]) for k in range(src_kt)]
            prs2 = [(src_sb[:, k * T + tt * 128: k * T + tt * 128 + 128],
                     wsb[:, k * 1024 + 512: k * 1024 + 1024]) for k in range(src_kt)]
            exd = [wt, bt, prevdep, self._rp_psfree.get(ii - 1), self.tok_ones]
            m1 = self.mm_acc(self.psA[:, 0:512],
                             prs + [(self.onesrow[0:1, 0:128], brow[0:1, 0:512])],
                             sem=st, deps=exd)
            m2 = self.mm_acc(self.psA[:, 512:1024],
                             prs2 + [(self.onesrow[0:1, 0:128], brow[0:1, 512:1024])],
                             sem=st)
            pv = self.psA[:, 0:1024].rearrange("p (i two) -> p i two", two=2)
            ev = pv[:, :, 0]
            od = pv[:, :, 1]
            rs = ropestg[:, par * 1024:(par + 1) * 1024]
            rv = rs.rearrange("p (i two) -> p i two", two=2)
            t1 = P.op("vector", "tensor_tensor", rtmp[:, 0:512], ev, cs[:, 0:512],
                      ALU.mult, sem=sv, deps=[m2, ls, self._rp_rsfree.get(ii - 2)])
            t2 = P.op("vector", "scalar_tensor_tensor", rtmp[:, 512:1024], od, -1.0,
                      cs[:, 512:1024], ALU.mult, ALU.mult, sem=sv)
            t3 = P.op("vector", "tensor_tensor", rv[:, :, 0], rtmp[:, 0:512],
                      rtmp[:, 512:1024], ALU.add, sem=sv, deps=[t1, t2])
            t4 = P.op("vector", "tensor_tensor", rtmp[:, 0:512], od, cs[:, 0:512],
                      ALU.mult, sem=sv, deps=[t3])
            t5 = P.op("vector", "tensor_tensor", rtmp[:, 512:1024], ev, cs[:, 512:1024],
                      ALU.mult, sem=sv)
            t6 = P.op("vector", "tensor_tensor", rv[:, :, 1], rtmp[:, 0:512],
                      rtmp[:, 512:1024], ALU.add, sem=sv, deps=[t4, t5])
            self._rp_csfree[ii] = t5
            self._rp_psfree[ii] = t5
            last = None
            for c in range(DC):
                last = P.op("tensor", "transpose", self.psT16[:, c * 128:(c + 1) * 128],
                            rs[:, c * 128:(c + 1) * 128], self.ident[:, :],
                            sem=(st if c == DC - 1 else None),
                            deps=([t6, self._rp_trfree.get(ii - 1), self.tok_consts]
                                  if c == 0 else ()))
            dstv = dstb[:, :].rearrange("p (m t) -> p m t", t=T)[:, :, tt * 128:(tt + 1) * 128]
            srcv = self.psT16[:, :].rearrange("p (c j) -> p c j", j=128)
            ta = P.op("vector", "tensor_tensor", dstv, dstv, srcv, ALU.add,
                      sem=sv, deps=[last])
            self._rp_trfree[ii] = ta
            self._rp_rsfree[ii] = last
        return ta

    def l0_attn(self, rope_cos, rope_sin, deps=()):
        P = self.P
        st, sv, sa = self.st, self.sv, self.sa
        self._rp_csfree, self._rp_psfree, self._rp_trfree, self._rp_rsfree = {}, {}, {}, {}
        scale = 1.0 / math.sqrt(16.0 + 512.0 + 512.0)
        es2 = ExitStack()
        with es2:
            sb2 = lambda name, shape, dt: es2.enter_context(self.nc.sbuf_tensor(name, shape, dt))
            Qb = self.sT
            Kb = sb2("Kb", [128, DC * T], BF16)
            Vt = sb2("Vt", [128, 16 * 16 * 65], BF16)
            tok_addq = tok_addk = tok_v = None
            with ExitStack() as s1:
                sc1 = lambda n, sh, dt: s1.enter_context(self.nc.sbuf_tensor(n, sh, dt))
                ctq = sc1("ctq", [128, 4 * T], BF16)
                ctkv = sc1("ctkv", [128, 4 * T], BF16)
                with ExitStack() as s1a:
                    sc1a = lambda n, sh, dt: s1a.enter_context(self.nc.sbuf_tensor(n, sh, dt))
                    wdq, t_wdq = self.load_w_nat("L0_attn_Wd_q_w", 1024, 512, sc1a, deps=[self.tok_xnT] + list(deps))
                    wdkv, t_wdkv = self.load_w_nat("L0_attn_Wd_kv_w", 1024, 512, sc1a, deps=[self.tok_xnT])
                    bdq, t_bdq = self.load_bias_cols2(sc1a, "L0_attn_Wd_q_b", 4, deps=[self.tok_xnT])
                    bdkv, t_bdkv = self.load_bias_cols2(sc1a, "L0_attn_Wd_kv_b", 4, deps=[self.tok_xnT])
                    P.wtok("scalar", t_bdq, t_bdkv)
                    tctq = self.proj_T(wdq, t_wdq, 8, ctq,
                                       lambda m, c: ctq[:, m * T + c * 512: m * T + c * 512 + 512],
                                       4, bdq, self.xnT, 0, extra_dep=self.tok_xnT)
                    tctkv = self.proj_T(wdkv, t_wdkv, 8, ctkv,
                                        lambda m, c: ctkv[:, m * T + c * 512: m * T + c * 512 + 512],
                                        4, bdkv, self.xnT, 0)
                    self.dbg("ctq", ctq[:, :], tctq)
                    self.dbg("ctkv", ctkv[:, :], tctkv)
                with ExitStack() as s1b:
                    sc1b = lambda n, sh, dt: s1b.enter_context(self.nc.sbuf_tensor(n, sh, dt))
                    wuq, t_wuq = self.load_w_nat("L0_attn_Wu_q_w", 512, 1024, sc1b, deps=[tctkv])
                    buq, t_buq = self.load_bias_cols2(sc1b, "L0_attn_Wu_q_b", 8, deps=[tctkv])
                    P.wtok("scalar", t_buq)
                    tq_c = self.proj_T(wuq, t_wuq, 4, Qb,
                                       lambda m, c: Qb[:, m * T + c * 512: m * T + c * 512 + 512],
                                       8, buq, ctq, 0, extra_dep=tctq)
                with ExitStack() as s1c:
                    sc1c = lambda n, sh, dt: s1c.enter_context(self.nc.sbuf_tensor(n, sh, dt))
                    wuk, t_wuk = self.load_w_nat("L0_attn_Wu_k_w", 512, 1024, sc1c, deps=[tq_c])
                    buk, t_buk = self.load_bias_cols2(sc1c, "L0_attn_Wu_k_b", 8, deps=[tq_c])
                    P.wtok("scalar", t_buk)
                    tk_c = self.proj_T(wuk, t_wuk, 4, Kb,
                                       lambda m, c: Kb[:, m * T + c * 512: m * T + c * 512 + 512],
                                       8, buk, ctkv, 0, extra_dep=tctkv)
                with ExitStack() as s1d:
                    sc1d = lambda n, sh, dt: s1d.enter_context(self.nc.sbuf_tensor(n, sh, dt))
                    wuv, t_wuv = self.load_w_nat("L0_attn_Wu_v_w", 512, 1024, sc1d, deps=[tk_c])
                    bvr, t_bvr = self.load_bias_row2(sc1d, "L0_attn_Wu_v_b", 1024, deps=[tk_c])
                    vones = Vt[:, :].rearrange("p (x c) -> p x c", c=65)[:, :, 64:65]
                    P.op("vector", "memset", vones, 1.0, sem=sv)
                    tok_vfree = None
                    for tt in range(TT):
                        for c2 in range(2):
                            prs = [(ctkv[:, k * T + tt * 128: k * T + tt * 128 + 128],
                                    wuv[:, k * 1024 + c2 * 512: k * 1024 + c2 * 512 + 512])
                                   for k in range(4)]
                            ex = [(self.onesrow[0:1, 0:128], bvr[0:1, c2 * 512:c2 * 512 + 512])]
                            tokmm = self.mm_acc(self.psC[:, :], prs + ex, sem=st,
                                                deps=[t_wuv, t_bvr, self.tok_ones, tctkv, tok_vfree])
                            for hh in range(8):
                                h = c2 * 8 + hh
                                tok_v = P.op("vector", "tensor_copy",
                                             Vt[:, (h * 16 + tt) * 65:(h * 16 + tt) * 65 + 64],
                                             self.psC[:, hh * 64: hh * 64 + 64],
                                             sem=sv, deps=([tokmm] if hh == 0 else ()))
                            tok_vfree = tok_v
                with ExitStack() as s1e:
                    sc1e = lambda n, sh, dt: s1e.enter_context(self.nc.sbuf_tensor(n, sh, dt))
                    wqr, t_wqr = self.load_w_nat("L0_attn_Wq_r_w", 512, 1024, sc1e, deps=[tok_v])
                    bqr, t_bqr = self.load_bias_row2(sc1e, "L0_attn_Wq_r_b", 1024, deps=[tok_v])
                    tok_addq = self._rope_stream(ctq, 4, wqr, t_wqr, bqr, t_bqr, Qb, tq_c,
                                                 rope_cos, rope_sin, 0, sc1e)
            self.dbg("Qb", Qb[:, :], tok_addq)
            with ExitStack() as s2:
                sc2 = lambda n, sh, dt: s2.enter_context(self.nc.sbuf_tensor(n, sh, dt))
                wkr, t_wkr = self.load_w_nat("L0_attn_Wk_r_w", 1024, 1024, sc2, deps=[tok_addq])
                bkr, t_bkr = self.load_bias_row2(sc2, "L0_attn_Wk_r_b", 1024, deps=[tok_addq])
                tok_addk = self._rope_stream(self.xnT, 8, wkr, t_wkr, bkr, t_bkr, Kb, tk_c,
                                             rope_cos, rope_sin, 1, sc2)
            self.dbg("Kb", Kb[:, :], tok_addk)
            self.dbg("Vt", Vt[:, :], tok_v)
            with ExitStack() as s3:
                sc3 = lambda n, sh, dt: s3.enter_context(self.nc.sbuf_tensor(n, sh, dt))
                attnb = sc3("attnb", [128, DC * T], BF16)
                s3b = ExitStack()
                sc3b = lambda n, sh, dt: s3b.enter_context(self.nc.sbuf_tensor(n, sh, dt))
                aqb = sc3b("aqb", [65, 2 * T], BF16)
                akb = sc3b("akb", [65, 2 * T], BF16)

                def augq(h, prev):
                    prev_attn = prev["attn"]
                    aq = aqb[:, (h % 2) * T:(h % 2 + 1) * T]
                    tok = P.op("vector", "tensor_scalar", aq[0:64, :],
                               Qb[64 * (h % 2):64 * (h % 2) + 64, (h // 2) * T:(h // 2 + 1) * T],
                               scale, None, ALU.mult, sem=sv,
                               deps=[tok_addq, tok_addk, prev_attn])
                    return aq, tok

                def augk(h, prev):
                    prev_attn = prev["attn"]
                    ak = akb[:, (h % 2) * T:(h % 2 + 1) * T]
                    P.op("vector", "tensor_copy", ak[0:64, :],
                         Kb[64 * (h % 2):64 * (h % 2) + 64, (h // 2) * T:(h // 2 + 1) * T],
                         sem=sv, deps=[prev_attn])
                    t2 = P.op("vector", "memset", ak[64:65, :], 1.0, sem=sv)
                    return ak, t2

                def vcol(h, kt):
                    return Vt[:, (h * 16 + kt) * 65:(h * 16 + kt) * 65 + 65]

                def attn_dst(h, qg):
                    return attnb[64 * (h % 2):64 * (h % 2) + 64,
                                 (h // 2) * T + qg * 512:(h // 2) * T + qg * 512 + 512]

                last_rp = self._rp_trfree.get(2 * TT - 1)
                tok_attn = self.attn_heads(sc3b, 16, augq, augk, vcol, attn_dst,
                                           lambda qt: T, causal=False,
                                           deps=[last_rp], pst_free=last_rp)
                self.dbg("attnb", attnb[:, :], tok_attn)
                s3b.close()
                with ExitStack() as s4:
                    sc4 = lambda n, sh, dt: s4.enter_context(self.nc.sbuf_tensor(n, sh, dt))
                    wo, t_wo = self.load_w_nat("L0_attn_Wo_w", 1024, 1024, sc4, deps=[tok_attn])
                    bo, t_bo = self.load_bias_cols2(sc4, "L0_attn_Wo_b", 8, deps=[tok_attn])
                    P.wtok("scalar", t_bo)
                    tok_s = self.proj_T(wo, t_wo, 8, self.sT,
                                        lambda m, c: self.sT[:, m * T + c * 512: m * T + c * 512 + 512],
                                        8, bo, attnb, 0, extra_dep=tok_attn)
        self.tok_sT_attn = tok_s
        return tok_s

    def layer0(self, x_src, x_dst, rope_cos, rope_sin, deps=()):
        self.ln_phase(x_src, deps=deps)
        ta = self.l0_attn(rope_cos, rope_sin, deps=deps)
        tf = self.l0_ffn(sT_ready=ta, deps=[ta])
        return self.residual_phase(x_src, x_dst, deps=[tf])

    def l2_ffn(self, sT_ready=None, deps=()):
        P = self.P
        st, sv, sa = self.st, self.sv, self.sa
        NI = 2816
        NKI = NI // 128
        E = 7
        sg_bf, t_sg = self.cast_weight_to_dram("L2_ffn_sg")
        su_bf, t_su = self.cast_weight_to_dram("L2_ffn_su")
        sd_bf, t_sd = self.cast_weight_to_dram("L2_ffn_sd")
        rg_bf, t_rg = self.cast_weight_to_dram("L2_ffn_rg")
        ru_bf, t_ru = self.cast_weight_to_dram("L2_ffn_ru")
        rd_bf, t_rd = self.cast_weight_to_dram("L2_ffn_rd")
        acc_dram = self.nc.dram_tensor("l2_acc", [D, T], F32)
        es2 = ExitStack()
        tok_gate = None
        with es2:
            sb2 = lambda n, sh, dt: es2.enter_context(self.nc.sbuf_tensor(n, sh, dt))
            self.cw_tm = sb2("l2_cw", [128, TT * 7], F32)
            self.cwT2 = sb2("l2_cwT", [1, 7 * T], BF16)
            with ExitStack() as sr:
                sbr = lambda n, sh, dt: sr.enter_context(self.nc.sbuf_tensor(n, sh, dt))
                wrb = sbr("l2_wr", [7, 1024], BF16)
                t_wr = P.dma("gpsimd", wrb[:, :], self.w["L2_ffn_wr"][:, :],
                             self.stream("l2wr"), deps=list(deps))
                rbb = sbr("l2_rb", [1, 7], BF16)
                t_rb = P.dma("gpsimd", rbb[0:1, :],
                             self.w["L2_ffn_rb"].ap().rearrange("(o n) -> o n", o=1),
                             self.stream("l2rb"), deps=list(deps))
                wrT = sbr("l2_wrT", [128, 8 * 7], BF16)
                tw = None
                for k in range(8):
                    tt1 = P.op("tensor", "transpose", self.psT16[:, 0:7],
                               wrb[0:7, k * 128:(k + 1) * 128], self.ident[0:7, 0:7],
                               sem=st, deps=[t_wr, tw, self.tok_consts])
                    tw = P.op("vector", "tensor_copy", wrT[:, k * 7:(k + 1) * 7],
                              self.psT16[:, 0:7], sem=sv, deps=[tt1])
                lgs = sbr("l2_lg", [128, 7], F32)
                sts = sbr("l2_sts", [128, 8], F32)
                cwbf_t = sbr("l2_cwbf", [128, 7], BF16)
                tok_cwT = None
                tok_lg = None
                for tt in range(TT):
                    prs = [(self.xnT[:, k * T + tt * 128: k * T + tt * 128 + 128],
                            wrT[:, k * 7:(k + 1) * 7]) for k in range(8)]
                    ex = [(self.onesrow[0:1, 0:128], rbb[0:1, :])]
                    tmm = self.mm_acc(self.psB[:, 0:7], prs + ex, sem=st,
                                      deps=[tw, t_rb, self.tok_xnT, self.tok_ones, tok_lg]
                                      + (list(deps) if tt == 0 else []))
                    tok_lg = P.op("vector", "tensor_copy", lgs[:, :], self.psB[:, 0:7],
                                  sem=sv, deps=[tmm])
                    m1 = sts[:, 0:1]; m2 = sts[:, 1:2]; dd = sts[:, 2:3]
                    w1 = sts[:, 3:4]; w2 = sts[:, 4:5]
                    t1 = P.op("vector", "tensor_reduce", m1, lgs[:, :], AX.X, ALU.max,
                              sem=sv, deps=[tok_lg])
                    cw = self.cw_tm[:, tt * 7:(tt + 1) * 7]
                    t2 = P.op("vector", "tensor_scalar", cw, lgs[:, :], m1, None,
                              ALU.is_equal, sem=sv, deps=[t1])
                    t3 = P.op("vector", "scalar_tensor_tensor", lgs[:, :], cw, NEG,
                              lgs[:, :], ALU.mult, ALU.add, sem=sv, deps=[t2])
                    t4 = P.op("vector", "tensor_reduce", m2, lgs[:, :], AX.X, ALU.max,
                              sem=sv, deps=[t3])
                    t5 = P.op("vector", "tensor_scalar", lgs[:, :], lgs[:, :], m2, None,
                              ALU.is_equal, sem=sv, deps=[t4])
                    t6 = P.op("vector", "tensor_tensor", dd, m1, m2, ALU.subtract,
                              sem=sv, deps=[t5])
                    t7 = P.op("scalar", "activation", w1, dd, AF.Sigmoid, sem=sa, deps=[t6])
                    t8 = P.op("scalar", "activation", w2, dd, AF.Sigmoid, scale=-1.0,
                              sem=sa, deps=[t7])
                    t9 = P.op("vector", "tensor_scalar", cw, cw, w1, None, ALU.mult,
                              sem=sv, deps=[t8])
                    t10 = P.op("vector", "scalar_tensor_tensor", cw, lgs[:, :], w2,
                               cw, ALU.mult, ALU.add, sem=sv, deps=[t9])
                    cwb = sts[:, 5:6]
                    for e in range(E):
                        tcb = P.op("vector", "tensor_copy", cwbf_t[:, e:e + 1], cw[:, e:e + 1],
                                   sem=sv, deps=([t10] if e == 0 else ()))
                        tte = P.op("tensor", "transpose", self.psT16[0:1, 0:128],
                                   cwbf_t[:, e:e + 1], self.ident[:, :], sem=st,
                                   deps=[tcb, tok_cwT])
                        tok_cwT = P.op("scalar", "copy",
                                       self.cwT2[0:1, e * T + tt * 128: e * T + tt * 128 + 128],
                                       self.psT16[0:1, 0:128], sem=sa, deps=[tte])
            se = ExitStack()
            sb3 = lambda n, sh, dt: se.enter_context(self.nc.sbuf_tensor(n, sh, dt))
            wdT = sb3("l2_wdT", [128, NKI * 1024], BF16)
            wgb = sb3("l2_wgb", [128, 2 * 8 * 512], BF16)
            wub = sb3("l2_wub", [128, 2 * 8 * 512], BF16)
            hbuf = sb3("l2_h", [128, NKI * 512], BF16)
            gstg = sb3("l2_gs", [128, 2 * 512], BF16)
            dstg = sb3("l2_ds", [128, 2 * 512], F32)
            cwbc = sb3("l2_cwbc", [128, T], BF16)
            wd_str = self.stream("l2wd")
            wg_str = [self.stream("l2wg0"), self.stream("l2wg1")]
            ICH = [512, 512, 512, 512, 512, 256]
            gfree = {1: sT_ready}
            ufree = {1: sT_ready}
            dfree = {1: sT_ready}
            gsl_free = {}
            dsl_free = {}
            wpar_free = {}
            down_done = None
            tok_bc = None
            gidx = didx = 0
            accst2 = [self.stream("l2acc0"), self.stream("l2acc1")]
            for ei in range(E + 1):
                routed = ei > 0
                e = ei - 1
                wdep = [t_sd if not routed else t_rd, down_done]
                if ei == 0:
                    wdep += [tok_cwT] + list(deps)
                for j in range(NKI):
                    src = (sd_bf[:, j * 128:(j + 1) * 128] if not routed
                           else rd_bf[e, :, j * 128:(j + 1) * 128])
                    wd_ld = P.dma("sync", wdT[:, j * 1024:(j + 1) * 1024], src, wd_str,
                                  deps=(wdep if j == 0 else ()), transpose=True)
                if routed:
                    bcs = None
                    for c in range(4):
                        tb = P.op("tensor", "matmul", self.psA[:, 1536:2048],
                                  self.onesrow[0:1, 0:128],
                                  self.cwT2[0:1, e * T + c * 512: e * T + c * 512 + 512],
                                  start=True, stop=True, sem=st,
                                  deps=[tok_cwT, tok_bc, bcs]
                                  + (list(deps) if ei == 1 and c == 0 else []),
                                  skip_group_check=True)
                        bcs = P.op("scalar", "activation", cwbc[:, c * 512:(c + 1) * 512],
                                   self.psA[:, 1536:2048], AF.Identity,
                                   sem=sa, deps=[tb])
                    tok_bc = bcs
                for nc4 in range(4):
                    i0k = 0
                    for ci, IC in enumerate(ICH):
                        par = ci % 2
                        nk = IC // 128
                        pi = (ei * 4 + nc4) * 6 + ci
                        wg_sl = wgb[:, par * 4096: par * 4096 + 8 * IC]
                        wu_sl = wub[:, par * 4096: par * 4096 + 8 * IC]
                        i0 = sum(ICH[:ci])
                        ldp = [t_sg if not routed else t_rg,
                               t_su if not routed else t_ru, wpar_free.get(pi - 2)]
                        for dk in range(8):
                            src = (sg_bf[i0:i0 + IC, dk * 128:(dk + 1) * 128] if not routed
                                   else rg_bf[e, i0:i0 + IC, dk * 128:(dk + 1) * 128])
                            lg = P.dma("sync", wg_sl[:, dk * IC:(dk + 1) * IC], src,
                                       wg_str[par], deps=(ldp if dk == 0 else ()),
                                       transpose=True)
                        for dk in range(8):
                            src = (su_bf[i0:i0 + IC, dk * 128:(dk + 1) * 128] if not routed
                                   else ru_bf[e, i0:i0 + IC, dk * 128:(dk + 1) * 128])
                            lu = P.dma("sync", wu_sl[:, dk * IC:(dk + 1) * IC], src,
                                       wg_str[par], transpose=True)
                        for mt in range(nk):
                            gp = gidx % 2
                            gsl = gstg[:, gp * 512:(gp + 1) * 512]
                            pg = self.psB if gp == 0 else self.psA[:, 0:512]
                            pu = self.psC if gp == 0 else self.psA[:, 512:1024]
                            xsl = lambda dk: self.xnT[:, dk * T + nc4 * 512: dk * T + nc4 * 512 + 512]
                            prsg = [(wg_sl[:, dk * IC + mt * 128: dk * IC + mt * 128 + 128], xsl(dk))
                                    for dk in range(8)]
                            g_tok = self.mm_acc(pg[:, 0:512] if gp == 0 else pg, prsg, sem=st,
                                                deps=[lu, self.tok_xnT, gfree.get(gp)]
                                                + (list(deps) if gidx == 0 else []))
                            sil = P.op("scalar", "activation", gsl, pg[:, 0:512] if gp == 0 else pg,
                                       AF.Silu, sem=sa, deps=[g_tok, gsl_free.get(gidx - 2)])
                            gfree[gp] = sil
                            if routed:
                                sil = P.op("vector", "tensor_tensor", gsl, gsl,
                                           cwbc[:, nc4 * 512:(nc4 + 1) * 512], ALU.mult,
                                           sem=sv, deps=[sil, tok_bc])
                            prsu = [(wu_sl[:, dk * IC + mt * 128: dk * IC + mt * 128 + 128], xsl(dk))
                                    for dk in range(8)]
                            u_tok = self.mm_acc(pu[:, 0:512] if gp == 0 else pu, prsu, sem=st,
                                                deps=[ufree.get(gp)] + (list(deps) if gidx == 0 else []))
                            kh = i0k + mt
                            hdeps = [u_tok, sil]
                            if kh == 0:
                                hdeps.append(down_done)
                            hv = P.op("vector", "tensor_tensor",
                                      hbuf[:, kh * 512:(kh + 1) * 512],
                                      pu[:, 0:512] if gp == 0 else pu, gsl,
                                      ALU.mult, sem=sv, deps=hdeps)
                            ufree[gp] = hv
                            gsl_free[gidx] = hv
                            gidx += 1
                        wpar_free[pi] = u_tok
                        i0k += nk
                    for dm in range(8):
                        dp = didx % 2
                        pd = self.psD if dp == 0 else self.psA[:, 1024:1536]
                        prd = [(wdT[:, kh * 1024 + dm * 128: kh * 1024 + dm * 128 + 128],
                                hbuf[:, kh * 512:(kh + 1) * 512]) for kh in range(NKI)]
                        d_tok = self.mm_acc(pd[:, 0:512] if dp == 0 else pd, prd, sem=st,
                                            deps=[hv, wd_ld, dfree.get(dp)]
                                            + (list(deps) if didx == 0 else []))
                        dsl = dstg[:, dp * 512:(dp + 1) * 512]
                        ev = P.op("scalar", "activation", dsl,
                                  pd[:, 0:512] if dp == 0 else pd, AF.Identity,
                                  sem=sa, deps=[d_tok, dsl_free.get(didx - 2)])
                        dfree[dp] = ev
                        stw = P.dma("gpsimd", acc_dram[dm * 128:(dm + 1) * 128,
                                                       nc4 * 512:(nc4 + 1) * 512], dsl,
                                    accst2[dp], deps=[ev],
                                    accum_op=(ALU.add if routed else ALU.bypass))
                        dsl_free[didx] = stw
                        didx += 1
                    down_done = d_tok
            se.close()
            accrd = sb2("l2_accrd", [128, 2 * 2048], F32)
            gbf = sb2("l2_gbf", [128, 2 * 2048], BF16)
            rd_str2 = [self.stream("l2accr0"), self.stream("l2accr1")]
            rfree = {}
            mv = None
            for dm in range(8):
                par = dm % 2
                arr = accrd[:, par * 2048:(par + 1) * 2048]
                gb = gbf[:, par * 2048:(par + 1) * 2048]
                ld = P.dma("sync", arr, acc_dram[dm * 128:(dm + 1) * 128, :], rd_str2[par],
                           deps=[rfree.get(dm - 2), (accst2[0], accst2[0].n),
                                 (accst2[1], accst2[1].n)])
                sg2 = P.op("scalar", "activation", gb, arr, AF.Silu, sem=sa,
                           deps=[ld, rfree.get(dm - 2)])
                mdeps = [sg2]
                if dm == 0 and sT_ready is not None:
                    mdeps.append(sT_ready)
                mv = P.op("vector", "tensor_tensor", self.sT[:, dm * T:(dm + 1) * T],
                          self.sT[:, dm * T:(dm + 1) * T], gb, ALU.mult, sem=sv,
                          deps=mdeps)
                rfree[dm] = mv
        self.tok_sT = mv
        return mv

    def l2_attn(self, deps=()):
        P = self.P
        st, sv, sa = self.st, self.sv, self.sa
        es2 = ExitStack()
        with es2:
            sb2 = lambda n, sh, dt: es2.enter_context(self.nc.sbuf_tensor(n, sh, dt))
            Qb = self.sT
            Kb = sb2("l2Kb", [128, DC * T], BF16)
            Vt = sb2("l2Vt", [128, 16 * 16 * 65], BF16)
            part_done = None
            for part, dstb in enumerate([Qb, Kb, None]):
                with ExitStack() as s1:
                    sc1 = lambda n, sh, dt: s1.enter_context(self.nc.sbuf_tensor(n, sh, dt))
                    wp = sc1(f"l2wqkv{part}", [128, 8 * 1024], BF16)
                    strm = self.stream(f"l2wqkv{part}")
                    tw = None
                    for k in range(8):
                        tw = P.dma("gpsimd", wp[:, k * 1024:(k + 1) * 1024],
                                   self.w["L2_attn_W_qkv"][k * 128:(k + 1) * 128,
                                                           part * 1024:(part + 1) * 1024],
                                   strm, deps=(([self.tok_xnT, part_done] + list(deps)) if k == 0 else ()))
                    if part < 2:
                        part_done = self.proj_T(wp, tw, 8, dstb,
                                    lambda m, c, dstb=dstb: dstb[:, m * T + c * 512: m * T + c * 512 + 512],
                                    8, None, self.xnT, 0, extra_dep=self.tok_xnT)
                    else:
                        vones = Vt[:, :].rearrange("p (x c) -> p x c", c=65)[:, :, 64:65]
                        P.op("vector", "memset", vones, 1.0, sem=sv)
                        tok_vfree = None
                        for tt in range(TT):
                            for c2 in range(2):
                                prs = [(self.xnT[:, k * T + tt * 128: k * T + tt * 128 + 128],
                                        wp[:, k * 1024 + c2 * 512: k * 1024 + c2 * 512 + 512])
                                       for k in range(8)]
                                tokmm = self.mm_acc(self.psC[:, :], prs, sem=st,
                                                    deps=[tw, self.tok_xnT, tok_vfree])
                                for hh in range(8):
                                    h = c2 * 8 + hh
                                    tok_v = P.op("vector", "tensor_copy",
                                                 Vt[:, (h * 16 + tt) * 65:(h * 16 + tt) * 65 + 64],
                                                 self.psC[:, hh * 64: hh * 64 + 64],
                                                 sem=sv, deps=([tokmm] if hh == 0 else ()))
                                tok_vfree = tok_v
            wql = sb2("l2wql", [128, 16 * 16], BF16)
            wkl = sb2("l2wkl", [128, 16 * 16], BF16)
            corec = sb2("l2core", [16, 16], F32)
            qlst = self.stream("l2qlsr")
            twl = None
            for h in range(16):
                for half in range(2):
                    twl = P.dma("gpsimd", wql[64 * half:64 * half + 64, h * 16:(h + 1) * 16],
                                self.w["L2_attn_W_q_lsr"][h, :, :], qlst,
                                deps=([tok_v] if (h == 0 and half == 0) else ()))
            for h in range(16):
                for half in range(2):
                    twl = P.dma("gpsimd", wkl[64 * half:64 * half + 64, h * 16:(h + 1) * 16],
                                self.w["L2_attn_W_k_lsr"][h, :, :], qlst)
            tco = P.dma("sync", corec[:, :],
                        self.w["L2_attn_core"].ap().rearrange("h r -> r h"),
                        self.stream("l2core"), deps=[tok_v],
                        allow_slow_non_contiguous=True)
            tcs = P.op("vector", "tensor_scalar", corec[:, :], corec[:, :], 0.25, None,
                       ALU.mult, sem=sv, deps=[tco])
            with ExitStack() as s3:
                sc3 = lambda n, sh, dt: s3.enter_context(self.nc.sbuf_tensor(n, sh, dt))
                attnb = sc3("l2attnb", [128, DC * T], BF16)
                s3b = ExitStack()
                sc3b = lambda n, sh, dt: s3b.enter_context(self.nc.sbuf_tensor(n, sh, dt))
                aqb = sc3b("l2aqb", [65, 2 * T], BF16)
                akb = sc3b("l2akb", [65, 2 * T], BF16)

                qev = [None]

                def lr_fill(h, prev, wl, srcb, dst, is_q):
                    zz = P.op("vector", "memset", dst[0:65, :], 0.0, sem=sv,
                              deps=[prev["attn"], prev["exp"] if not is_q else None])
                    mms = []
                    hb = 64 * (h % 2)
                    for c in range(4):
                        mm = P.op("tensor", "matmul", self.psA[0:16, c * 512:(c + 1) * 512],
                                  wl[hb:hb + 64, h * 16:(h + 1) * 16],
                                  srcb[hb:hb + 64,
                                       (h // 2) * T + c * 512:(h // 2) * T + c * 512 + 512],
                                  start=True, stop=True, sem=st,
                                  deps=([twl, tcs, prev["red"], zz,
                                         None if is_q else qev[0]] if c == 0 else ()),
                                  skip_group_check=True)
                        mms.append(mm)
                    kw = {}
                    if is_q:
                        kw["scale"] = corec[:, h:h + 1]
                    ev = P.op("scalar", "activation", dst[0:16, :], self.psA[0:16, :],
                              AF.Identity, sem=sa, deps=[mms[-1]], **kw)
                    if is_q:
                        qev[0] = ev
                    return ev

                def augq(h, prev):
                    aq = aqb[:, (h % 2) * T:(h % 2 + 1) * T]
                    return aq, lr_fill(h, prev, wql, Qb, aq, True)

                def augk(h, prev):
                    ak = akb[:, (h % 2) * T:(h % 2 + 1) * T]
                    ev = lr_fill(h, prev, wkl, Kb, ak, False)
                    t2 = P.op("vector", "memset", ak[64:65, :], 1.0, sem=sv, deps=[ev])
                    return ak, t2

                def vcol(h, kt):
                    return Vt[:, (h * 16 + kt) * 65:(h * 16 + kt) * 65 + 65]

                def attn_dst(h, qg):
                    return attnb[64 * (h % 2):64 * (h % 2) + 64,
                                 (h // 2) * T + qg * 512:(h // 2) * T + qg * 512 + 512]

                tok_attn = self.attn_heads(sc3b, 16, augq, augk, vcol, attn_dst,
                                           lambda qt: (qt + 1) * 128, causal=True,
                                           deps=[tok_v], pst_free=None)
                s3b.close()
                with ExitStack() as s4:
                    sc4 = lambda n, sh, dt: s4.enter_context(self.nc.sbuf_tensor(n, sh, dt))
                    wo, t_wo = self.load_w_nat("L2_attn_W_o", 1024, 1024, sc4, deps=[tok_attn])
                    tok_s = self.proj_T(wo, t_wo, 8, self.sT,
                                        lambda m, c: self.sT[:, m * T + c * 512: m * T + c * 512 + 512],
                                        8, None, attnb, 0, extra_dep=tok_attn)
        self.tok_sT_attn = tok_s
        return tok_s

    def layer2(self, x_src, x_dst, deps=()):
        self.ln_phase(x_src, deps=deps)
        ta = self.l2_attn(deps=deps)
        tf = self.l2_ffn(sT_ready=ta, deps=[ta])
        return self.residual_phase(x_src, x_dst, deps=[tf])

    def load_w_trans(self, name, R, C, scope_sb, deps=(), tag=""):
        bfh, tcast = self.cast_weight_to_dram(name)
        kt = C // 128
        sb = scope_sb("wt_" + name + tag, [128, kt * R], BF16)
        strm = self.stream("wtl")
        for k in range(kt):
            self.P.dma("sync", sb[:, k * R:(k + 1) * R],
                       bfh[:, k * 128:(k + 1) * 128], strm,
                       deps=(list(deps) + [tcast] if k == 0 else ()),
                       transpose=True)
        return sb, (strm, None)

    def l1_attn(self, f_cos, f_sin, deps=()):
        P = self.P
        st, sv, sa = self.st, self.sv, self.sa
        es2 = ExitStack()
        with es2:
            sb2 = lambda n, sh, dt: es2.enter_context(self.nc.sbuf_tensor(n, sh, dt))
            Qb = self.sT
            Kb = sb2("l1Kb", [128, DC * T], BF16)
            Vt = sb2("l1Vt", [128, 16 * 16 * 65], BF16)
            wq_bf, t_wq = self.cast_weight_to_dram("L1_attn_Wqkv")
            part_done = None
            for part, dstb in enumerate([Qb, Kb, None]):
                with ExitStack() as s1:
                    sc1 = lambda n, sh, dt: s1.enter_context(self.nc.sbuf_tensor(n, sh, dt))
                    wp = sc1(f"l1wqkv{part}", [128, 8 * 1024], BF16)
                    strm = self.stream(f"l1wqkv{part}")
                    tw = None
                    for k in range(8):
                        tw = P.dma("sync", wp[:, k * 1024:(k + 1) * 1024],
                                   wq_bf[part * 1024:(part + 1) * 1024, k * 128:(k + 1) * 128],
                                   strm, transpose=True,
                                   deps=(([t_wq, self.tok_xnT, part_done] + list(deps))
                                         if k == 0 else ()))
                    if part < 2:
                        part_done = self.proj_T(
                            wp, tw, 8, dstb,
                            lambda m, c, dstb=dstb: dstb[:, m * T + c * 512: m * T + c * 512 + 512],
                            8, None, self.xnT, 0, extra_dep=self.tok_xnT,
                            scale=(0.125 if part == 0 else None))
                    else:
                        vones = Vt[:, :].rearrange("p (x c) -> p x c", c=65)[:, :, 64:65]
                        P.op("vector", "memset", vones, 1.0, sem=sv)
                        tok_vfree = None
                        for tt in range(TT):
                            for c2 in range(2):
                                prs = [(self.xnT[:, k * T + tt * 128: k * T + tt * 128 + 128],
                                        wp[:, k * 1024 + c2 * 512: k * 1024 + c2 * 512 + 512])
                                       for k in range(8)]
                                tokmm = self.mm_acc(self.psC[:, :], prs, sem=st,
                                                    deps=[tw, self.tok_xnT, tok_vfree])
                                for hh in range(8):
                                    h = c2 * 8 + hh
                                    tok_v = P.op("vector", "tensor_copy",
                                                 Vt[:, (h * 16 + tt) * 65:(h * 16 + tt) * 65 + 64],
                                                 self.psC[:, hh * 64: hh * 64 + 64],
                                                 sem=sv, deps=([tokmm] if hh == 0 else ()))
                                tok_vfree = tok_v
            cs32 = sb2("l1cs32", [32, T], BF16)
            sn32 = sb2("l1sn32", [32, T], BF16)
            Kf = sb2("l1Kf", [65, T], BF16)
            ab = sb2("l1ab", [32, 3 * 16], F32)
            fst = self.stream("l1f")
            t1 = P.dma("gpsimd", cs32[:, :], f_cos[:, :], self.stream("l1fg"), deps=[tok_v])
            t2 = P.dma("gpsimd", sn32[:, :], f_sin[:, :], self.stream("l1fg"))
            tk1 = P.dma("gpsimd", Kf[0:32, :], f_cos[:, :], self.stream("l1fk"), deps=[tok_v])
            tk2 = P.dma("gpsimd", Kf[32:64, :], f_sin[:, :], self.stream("l1fk"))
            tk3 = P.op("vector", "memset", Kf[64:65, :], 1.0, sem=sv, deps=[tk2])
            ta1 = P.dma("sync", ab[:, 0:16],
                        self.w["L1_attn_alpha"].ap().rearrange("h r -> r h"), fst,
                        deps=[tok_v], allow_slow_non_contiguous=True)
            ta2 = P.dma("sync", ab[:, 16:32],
                        self.w["L1_attn_beta"].ap().rearrange("h r -> r h"), fst,
                        allow_slow_non_contiguous=True)
            tneg = P.op("vector", "tensor_scalar", ab[:, 32:48], ab[:, 16:32], -1.0, None,
                        ALU.mult, sem=sv, deps=[ta2, (fst, fst.n)])
            with ExitStack() as s3:
                sc3 = lambda n, sh, dt: s3.enter_context(self.nc.sbuf_tensor(n, sh, dt))
                attnb = sc3("l1attnb", [128, DC * T], BF16)
                s3b = ExitStack()
                sc3b = lambda n, sh, dt: s3b.enter_context(self.nc.sbuf_tensor(n, sh, dt))
                Qf = sc3b("l1Qf", [65, 2 * T], BF16)
                ptb = sc3b("l1pt", [128, 8 * 512], BF16)
                ftmp = ptb[0:32, 0:2048]
                aq0 = sc3b("l1aq0", [64, T], BF16)
                ak0 = sc3b("l1ak0", [64, T], BF16)

                def augq(h, prev):
                    par = h % 2
                    qf = Qf[:, par * T:(par + 1) * T]
                    d1 = [prev["attn"], tneg, t2]
                    x1 = P.op("vector", "tensor_scalar", ftmp, cs32[:, :],
                              ab[:, h:h + 1], None, ALU.mult, sem=sv, deps=d1)
                    x2 = P.op("vector", "scalar_tensor_tensor", qf[0:32, :], sn32[:, :],
                              ab[:, 16 + h:17 + h], ftmp, ALU.mult, ALU.add,
                              sem=sv, deps=[x1])
                    x3 = P.op("vector", "tensor_scalar", ftmp, sn32[:, :],
                              ab[:, h:h + 1], None, ALU.mult, sem=sv, deps=[x2])
                    x4 = P.op("vector", "scalar_tensor_tensor", qf[32:64, :], cs32[:, :],
                              ab[:, 32 + h:33 + h], ftmp, ALU.mult, ALU.add,
                              sem=sv, deps=[x3])
                    aq = aq0[:, 0:T]
                    cq = P.op("vector", "tensor_copy", aq[0:64, :],
                              Qb[64 * (h % 2):64 * (h % 2) + 64, (h // 2) * T:(h // 2 + 1) * T],
                              sem=sv, deps=[x4, prev["attn"]])
                    return _PlainAug(aq, qf), cq

                def augk(h, prev):
                    ak = ak0[:, 0:T]
                    ck = P.op("vector", "tensor_copy", ak[0:64, :],
                              Kb[64 * (h % 2):64 * (h % 2) + 64, (h // 2) * T:(h // 2 + 1) * T],
                              sem=sv, deps=[prev["attn"], tk3 if h == 0 else None])
                    return _PlainAug(ak, Kf), ck

                def sT_extra(h):
                    par = h % 2
                    qf = Qf[:, par * T:(par + 1) * T]
                    return (lambda kt: Kf[0:65, kt * 128:(kt + 1) * 128],
                            lambda qg: qf[0:65, qg * 512:(qg + 1) * 512])

                def qk_extra(h):
                    par = h % 2
                    qf = Qf[:, par * T:(par + 1) * T]
                    return (lambda qt: qf[0:64, qt * 128:(qt + 1) * 128],
                            lambda c, k1: Kf[0:64, c * 512:c * 512 + k1])

                def vcol(h, kt):
                    return Vt[:, (h * 16 + kt) * 65:(h * 16 + kt) * 65 + 65]

                def attn_dst(h, qg):
                    return attnb[64 * (h % 2):64 * (h % 2) + 64,
                                 (h // 2) * T + qg * 512:(h // 2) * T + qg * 512 + 512]

                tok_attn = self.attn_heads(sc3b, 16, augq, augk, vcol, attn_dst,
                                           lambda qt: (qt + 1) * 128, causal=True,
                                           sT_extra=sT_extra, qk_extra=qk_extra,
                                           deps=[tok_v], pst_free=None, pt_buf=ptb)
                s3b.close()
                with ExitStack() as s4:
                    sc4 = lambda n, sh, dt: s4.enter_context(self.nc.sbuf_tensor(n, sh, dt))
                    wo, t_wo = self.load_w_trans("L1_attn_Wout", 1024, 1024, sc4,
                                                 deps=[tok_attn])
                    tok_s = self.proj_T(wo, t_wo, 8, self.sT,
                                        lambda m, c: self.sT[:, m * T + c * 512: m * T + c * 512 + 512],
                                        8, None, attnb, 0, extra_dep=tok_attn)
        self.tok_sT_attn = tok_s
        return tok_s

    def l1_ffn(self, replica_groups, sT_ready=None, deps=()):
        P = self.P
        st, sv, sa = self.st, self.sv, self.sa
        NS = 2048
        lg_dram = self.nc.dram_tensor("l1_logits", [T, NS], BF16)
        d_dram = self.nc.dram_tensor("l1_D", [T, NS], BF16)
        c_dram = self.nc.dram_tensor("l1_C", [T, NS], BF16)
        xp_dram = self.nc.dram_tensor("l1_xpart", [NS, D], F32)
        xr_dram = self.nc.dram_tensor("l1_xred", [NS, D], F32)
        self._l1_xr = xr_dram
        self._l1_c = c_dram
        self._l1ps_free = None
        self._l1_ttfree = None
        self._l1xp_psfree = None
        es2 = ExitStack()
        with es2:
            sb2 = lambda n, sh, dt: es2.enter_context(self.nc.sbuf_tensor(n, sh, dt))
            phib = sb2("l1_phib", [1, NS], BF16)
            tpb = P.dma("gpsimd", phib[0:1, :],
                        self.w["L1_ffn_phi_b"].ap().rearrange("(o n) -> o n", o=1),
                        self.stream("l1pb"), deps=list(deps))
            phi_bf, t_phi = self.cast_weight_to_dram("L1_ffn_phi_w")
            lgst = [self.stream("l1lg0"), self.stream("l1lg1")]
            with ExitStack() as sA:
                scA = lambda n, sh, dt: sA.enter_context(self.nc.sbuf_tensor(n, sh, dt))
                phT = scA("l1_phT", [128, 2 * 8 * 512], BF16)
                lstg = scA("l1_lstg", [128, 2 * 512], BF16)
                phst = [self.stream("l1ph0"), self.stream("l1ph1")]
                wfree = {}
                lfree = {}
                tok_lg = None
                for sc in range(4):
                    par = sc % 2
                    ph = phT[:, par * 4096:(par + 1) * 4096]
                    tw = None
                    for k in range(8):
                        tw = P.dma("sync", ph[:, k * 512:(k + 1) * 512],
                                   phi_bf[sc * 512:(sc + 1) * 512, k * 128:(k + 1) * 128],
                                   phst[par], transpose=True,
                                   deps=(([t_phi, self.tok_xnT, wfree.get(sc - 2)] + list(deps))
                                         if k == 0 else ()))
                    for tt in range(TT):
                        ii = sc * TT + tt
                        lp = ii % 2
                        ls = lstg[:, lp * 512:(lp + 1) * 512]
                        prs = [(self.xnT[:, k * T + tt * 128: k * T + tt * 128 + 128],
                                ph[:, k * 512:(k + 1) * 512]) for k in range(8)]
                        ex = [(self.onesrow[0:1, 0:128], phib[0:1, sc * 512:(sc + 1) * 512])]
                        mm = self.mm_acc(self.psB[:, :], prs + ex, sem=st,
                                         deps=[tw, tpb, self.tok_ones, self.tok_xnT,
                                               self._l1ps_free]
                                         + (list(deps) if (sc == 0 and tt == 0) else []))
                        ev = P.op("scalar", "activation", ls, self.psB[:, :], AF.Identity,
                                  sem=sa, deps=[mm, lfree.get(ii - 2)])
                        self._l1ps_free = ev
                        stw = P.dma("sync", lg_dram[tt * 128:(tt + 1) * 128,
                                                    sc * 512:(sc + 1) * 512], ls,
                                    lgst[lp], deps=[ev])
                        lfree[ii] = stw
                        tok_lg = stw
                    wfree[sc] = mm
            with ExitStack() as sB:
                scB = lambda n, sh, dt: sB.enter_context(self.nc.sbuf_tensor(n, sh, dt))
                xn_tm = scB("l1_xntm", [128, TT * 1024], BF16)
                ttok = None
                etok = None
                for c in range(DC):
                    for t in range(TT):
                        tdeps = [self._l1_ttfree, sT_ready] if (t % 8 == 0) else []
                        if c == 0 and t == 0:
                            tdeps += [(lgst[0], lgst[0].n), (lgst[1], lgst[1].n)]
                        ttok = P.op("tensor", "transpose",
                                    self.psT16[:, (t % 8) * 128:(t % 8 + 1) * 128],
                                    self.xnT[:, c * T + t * 128: c * T + t * 128 + 128],
                                    self.ident[:, :],
                                    sem=(st if t % 8 == 7 else None),
                                    deps=tdeps)
                        if t % 8 == 7:
                            dst = xn_tm[:, :].rearrange("p (u j) -> p u j", j=1024)[
                                :, t - 7:t + 1, c * 128:(c + 1) * 128]
                            src8 = self.psT16[:, :].rearrange("p (u j) -> p u j", j=128)
                            etok = P.op("vector", "tensor_copy", dst, src8, sem=sv,
                                        deps=[ttok])
                            self._l1_ttfree = etok
                Ebuf = scB("l1_E", [128, NS], BF16)
                lrow = scB("l1_lrow", [128, NS], BF16)
                sds = scB("l1_sds", [128, 64 + 64], F32)
                scs = scB("l1_scs", [128, 32 + 32], F32)
                dcst = [self.stream("l1dc0"), self.stream("l1dc1")]
                ldst = [self.stream("l1ld0"), self.stream("l1ld1")]
                dfree = {}
                tok_dc = None
                prev_std = None
                prev_stc = None
                for tt in range(TT):
                    lddeps = [prev_std]
                    if tt == 0:
                        lddeps += [(lgst[0], lgst[0].n), (lgst[1], lgst[1].n)]
                    ld = P.dma("sync", lrow[:, :],
                               lg_dram[tt * 128:(tt + 1) * 128, :], ldst[0],
                               deps=lddeps)
                    ex1 = P.op("scalar", "activation", Ebuf[:, :], lrow[:, :], AF.Exp,
                               sem=sa, deps=[ld, prev_stc])
                    eV = Ebuf[:, :].rearrange("p (n q) -> p n q", q=32)
                    sD = sds[:, 0:64]
                    rD = sds[:, 64:128]
                    s1 = P.op("vector", "tensor_reduce", sD, eV, AX.X, ALU.add, sem=sv,
                              deps=[ex1])
                    r1 = P.op("vector", "reciprocal", rD, sD, sem=sv, deps=[s1])
                    eVq = Ebuf[:, :].rearrange("p (n q) -> p q n", q=32)
                    sC = scs[:, 0:32]
                    rC = scs[:, 32:64]
                    s2 = P.op("vector", "tensor_reduce", sC, eVq, AX.X, ALU.add, sem=sv,
                              deps=[ex1])
                    r2 = P.op("vector", "reciprocal", rC, sC, sem=sv, deps=[s2])
                    from concourse.bass import broadcast_tensor_aps as _bc
                    dV = lrow[:, :].rearrange("p (n q) -> p n q", q=32)
                    a2, b2 = _bc(eV, rD.rearrange("p (n o) -> p n o", o=1))
                    d1 = P.op("vector", "tensor_tensor", dV, a2, b2, ALU.mult, sem=sv,
                              deps=[r1])
                    std = P.dma("sync", d_dram[tt * 128:(tt + 1) * 128, :], lrow[:, :],
                                dcst[0], deps=[d1])
                    a3, b3 = _bc(eV, rC.rearrange("p (o q) -> p o q", o=1))
                    c1 = P.op("vector", "tensor_tensor", eV, a3, b3, ALU.mult, sem=sv,
                              deps=[d1, r2])
                    stc = P.dma("sync", c_dram[tt * 128:(tt + 1) * 128, :], Ebuf[:, :],
                                dcst[1], deps=[c1])
                    prev_std = std
                    prev_stc = stc
                    tok_dc = stc
                dtl = scB("l1_dtl", [128, 2 * 2048], BF16)
                xps = scB("l1_xps", [128, 2 * 1024], F32)
                dstr = [self.stream("l1dt0"), self.stream("l1dt1")]
                xpst = [self.stream("l1xp0"), self.stream("l1xp1")]
                dldf = {}
                xpf = {}
                tok_xp = None
                for sl in range(16):
                    lp2 = sl % 2
                    dts = dtl[:, lp2 * 2048:(lp2 + 1) * 2048]
                    tdl = None
                    for tt in range(TT):
                        tdl = P.dma("sync", dts[:, tt * 128:(tt + 1) * 128],
                                    d_dram[tt * 128:(tt + 1) * 128, sl * 128:(sl + 1) * 128],
                                    dstr[lp2],
                                    deps=(([dldf.get(sl - 2)] +
                                           ([(dcst[0], dcst[0].n), (dcst[1], dcst[1].n)]
                                            if sl == 0 else [])) if tt == 0 else ()))
                    mm = None
                    for tt in range(TT):
                        for c2 in range(2):
                            mm = P.op("tensor", "matmul",
                                      self.psA[:, c2 * 512:c2 * 512 + 512],
                                      dts[:, tt * 128:(tt + 1) * 128],
                                      xn_tm[:, tt * 1024 + c2 * 512: tt * 1024 + c2 * 512 + 512],
                                      start=(tt == 0), stop=(tt == TT - 1),
                                      sem=(st if (tt == TT - 1 and c2 == 1) else None),
                                      deps=(([tdl, etok, self._l1xp_psfree]
                                             + (list(deps) if sl == 0 else []))
                                            if (tt == 0 and c2 == 0) else ()),
                                      skip_group_check=True)
                    dldf[sl] = mm
                    xp = xps[:, lp2 * 1024:(lp2 + 1) * 1024]
                    ev = P.op("scalar", "activation", xp, self.psA[:, 0:1024], AF.Identity,
                              sem=sa, deps=[mm, xpf.get(sl - 2)])
                    self._l1xp_psfree = ev
                    stx = P.dma("sync", xp_dram[sl * 128:(sl + 1) * 128, :], xp,
                                xpst[lp2], deps=[ev])
                    xpf[sl] = stx
                    tok_xp = stx
            P.wtok("gpsimd", tok_xp)
            for s_ in [self.stream("l1xp0"), self.stream("l1xp1")]:
                P.wait("gpsimd", s_)
            tcc = P.op("gpsimd", "collective_compute", "AllReduce", ALU.add,
                       replica_groups, [xp_dram[:, :]], [xr_dram[:, :]], sem=self.cc)
        return tcc

    def l1_ffn_post(self, tcc, sT_ready=None):
        P = self.P
        st, sv, sa = self.st, self.sv, self.sa
        NS = 2048
        es2 = ExitStack()
        with es2:
            sb2 = lambda n, sh, dt: es2.enter_context(self.nc.sbuf_tensor(n, sh, dt))
            Ytl = sb2("l1_Ytl", [128, TT * 1024], BF16)
            ytfree = None
            with ExitStack() as sC:
                scC = lambda n, sh, dt: sC.enter_context(self.nc.sbuf_tensor(n, sh, dt))
                xpT = scC("l1_xpT", [128, 8 * NS], BF16)
                xrs = scC("l1_xrs", [128, 2 * 1024], F32)
                xts = scC("l1_xts", [128, 2 * 1024], BF16)
                stt_ = scC("l1_stats", [128, 16], F32)
                xrst = [self.stream("l1xr0"), self.stream("l1xr1")]
                rfree = {}
                ttfree = None
                for sl in range(16):
                    lp = sl % 2
                    xr = xrs[:, lp * 1024:(lp + 1) * 1024]
                    xt = xts[:, lp * 1024:(lp + 1) * 1024]
                    ld = P.dma("sync", xr, self._l1_xr[sl * 128:(sl + 1) * 128, :],
                               xrst[lp], deps=[tcc if sl == 0 else None, rfree.get(sl - 2)])
                    c1 = P.op("vector", "tensor_scalar", xr, xr, -33000.0, None,
                              ALU.max, sem=sv, deps=[ld])
                    c2 = P.op("vector", "tensor_scalar", xr, xr, 65000.0, None,
                              ALU.min, sem=sv, deps=[c1])
                    so = lp * 8
                    s1 = stt_[:, so:so + 1]; s2 = stt_[:, so + 1:so + 2]
                    mean = stt_[:, so + 2:so + 3]; msq = stt_[:, so + 3:so + 4]
                    var = stt_[:, so + 4:so + 5]; rstd = stt_[:, so + 5:so + 6]
                    nmr = stt_[:, so + 6:so + 7]
                    t1 = P.op("vector", "tensor_reduce", s1, xr, AX.X, ALU.add, sem=sv, deps=[c2])
                    t2 = P.op("vector", "tensor_tensor", xt, xr, xr, ALU.mult, sem=sv,
                              deps=[rfree.get(sl - 2)])
                    t3 = P.op("vector", "tensor_reduce", s2, xt, AX.X, ALU.add, sem=sv, deps=[t2])
                    t4 = P.op("vector", "tensor_scalar", mean, s1, 1.0 / D, None, ALU.mult, sem=sv, deps=[t1])
                    t5 = P.op("vector", "tensor_tensor", msq, mean, mean, ALU.mult, sem=sv, deps=[t4])
                    t6 = P.op("vector", "scalar_tensor_tensor", var, s2, 1.0 / D, msq,
                              ALU.mult, ALU.subtract, sem=sv, deps=[t3, t5])
                    t6b = P.op("vector", "tensor_scalar", var, var, 1e-5, None, ALU.add,
                               sem=sv, deps=[t6])
                    t7 = P.op("scalar", "activation", rstd, var, AF.Sqrt, sem=sa, deps=[t6b])
                    t8 = P.op("vector", "reciprocal", rstd, rstd, sem=sv, deps=[t7])
                    t9 = P.op("vector", "scalar_tensor_tensor", nmr, mean, -1.0, rstd,
                              ALU.mult, ALU.mult, sem=sv, deps=[t8])
                    t10 = P.op("scalar", "activation", xt, xr, AF.Identity, bias=nmr,
                               scale=rstd, sem=sa, deps=[t9])
                    t11 = P.op("scalar", "activation", xt, xt, AF.Tanh, sem=sa, deps=[t10])
                    rfree[sl] = t11
                    last = None
                    for c in range(DC):
                        last = P.op("tensor", "transpose",
                                    self.psT16[:, c * 128:(c + 1) * 128],
                                    xt[:, c * 128:(c + 1) * 128], self.ident[:, :],
                                    sem=(st if c == DC - 1 else None),
                                    deps=([t11, ttfree,
                                           sT_ready if sl == 0 else None] if c == 0 else ()))
                    dst = xpT[:, :].rearrange("p (c j) -> p c j", j=NS)[:, :, sl * 128:(sl + 1) * 128]
                    srcv = self.psT16[:, :].rearrange("p (c j) -> p c j", j=128)
                    ttfree = P.op("vector", "tensor_copy", dst, srcv, sem=sv, deps=[last])
                eb = scC("l1_eb", [64, 1024], BF16)
                ebT = scC("l1_ebT", [128, 8 * 64], F32)
                ystg = scC("l1_ystg", [128, 32], BF16)
                tb1 = P.dma("gpsimd", eb[:, :], self.w["L1_ffn_expert_b"][:, :],
                            self.stream("l1eb"), deps=[tcc])
                tbT = None
                for c in range(DC):
                    tt2 = P.op("tensor", "transpose", self.psT16[:, 0:64],
                               eb[:, c * 128:(c + 1) * 128], self.ident[0:64, 0:64],
                               sem=st, deps=[tb1, tbT, ttfree if c == 0 else None])
                    tbT = P.op("vector", "tensor_copy", ebT[:, c * 64:(c + 1) * 64],
                               self.psT16[:, 0:64], sem=sv, deps=[tt2])
                wes = scC("l1_wes", [128, 2 * 8 * 1024], BF16)
                wst_ = [self.stream("l1we0"), self.stream("l1we1")]
                wefree = {}
                ps_free = None
                ystg_free = None
                tr_free = None
                for n in range(64):
                    lp = n % 2
                    we = wes[:, lp * 8192:(lp + 1) * 8192]
                    tw = None
                    for k in range(DC):
                        tw = P.dma("gpsimd", we[:, k * 1024:(k + 1) * 1024],
                                   self.w["L1_ffn_expert_w"][n, k * 128:(k + 1) * 128, :],
                                   wst_[lp],
                                   deps=([wefree.get(n - 2), tbT if n == 0 else None]
                                         if k == 0 else ()))
                    mm = None
                    for hm in range(8):
                        prs = [(we[:, k * 1024 + hm * 128: k * 1024 + hm * 128 + 128],
                                xpT[:, k * NS + n * 32: k * NS + n * 32 + 32])
                               for k in range(DC)]
                        mm = self.mm_acc(self.psB[:, 0:32], prs, sem=st,
                                         deps=[tw, ttfree, ps_free])
                        ev = P.op("scalar", "activation", ystg[:, 0:32],
                                  self.psB[:, 0:32], AF.Identity,
                                  bias=ebT[:, hm * 64 + n:hm * 64 + n + 1],
                                  sem=sa, deps=[mm, ystg_free])
                        ps_free = ev
                        half = n % 2
                        tt3 = P.op("tensor", "transpose",
                                   self.psT16[half * 32:half * 32 + 32, hm * 128:(hm + 1) * 128],
                                   ystg[:, 0:32], self.ident[:, :],
                                   sem=st, deps=[ev, tr_free])
                        ystg_free = tt3
                        tr_free = tt3
                    if n % 2 == 1:
                        npair = n // 2
                        sl2 = (npair * 64) // 128
                        off = (npair * 64) % 128
                        ytfree = P.op("vector", "tensor_copy",
                                      Ytl[:, sl2 * 1024:(sl2 + 1) * 1024][off:off + 64, :],
                                      self.psT16[0:64, :], sem=sv, deps=[tt3])
                        tr_free = ytfree
                    wefree[n] = mm
            with ExitStack() as sD:
                scD = lambda n, sh, dt: sD.enter_context(self.nc.sbuf_tensor(n, sh, dt))
                cts = scD("l1_cts", [128, 2 * T], BF16)
                gstg2 = scD("l1_g2", [128, 2 * 512], BF16)
                ctst = [self.stream("l1ct0"), self.stream("l1ct1")]
                ctfree = {}
                gfree = {}
                mv = None
                cb_psfree = None
                gidx = 0
                for hm in range(8):
                    mm2 = None
                    for sl in range(16):
                        ci = hm * 16 + sl
                        lp = sl % 2
                        ct = cts[:, lp * T:(lp + 1) * T]
                        ld = P.dma("sync", ct, self._l1_c[:, sl * 128:(sl + 1) * 128],
                                   ctst[lp], transpose=True,
                                   deps=[ctfree.get(ci - 2), ytfree if ci == 0 else None])
                        for tc in range(4):
                            dd = []
                            if tc == 0:
                                dd = [ld]
                                if sl == 0:
                                    dd += [ytfree, cb_psfree]
                            mm2 = P.op("tensor", "matmul",
                                       self.psA[:, tc * 512:(tc + 1) * 512],
                                       Ytl[:, sl * 1024 + hm * 128: sl * 1024 + hm * 128 + 128],
                                       ct[:, tc * 512:(tc + 1) * 512],
                                       start=(sl == 0), stop=(sl == 15),
                                       sem=(st if tc == 3 else None),
                                       deps=dd, skip_group_check=True)
                        ctfree[ci] = mm2
                    for tc in range(4):
                        gp = gidx % 2
                        gsl = gstg2[:, gp * 512:(gp + 1) * 512]
                        sil = P.op("scalar", "activation", gsl,
                                   self.psA[:, tc * 512:(tc + 1) * 512], AF.Silu,
                                   sem=sa, deps=[mm2, gfree.get(gidx - 2)])
                        ssl = self.sT[:, hm * T + tc * 512: hm * T + tc * 512 + 512]
                        mdeps = [sil]
                        if hm == 0 and tc == 0 and sT_ready is not None:
                            mdeps.append(sT_ready)
                        mv = P.op("vector", "tensor_tensor", ssl, ssl, gsl, ALU.mult,
                                  sem=sv, deps=mdeps)
                        gfree[gidx] = mv
                        gidx += 1
                    cb_psfree = mv
        self.tok_sT = mv
        return mv

    def layer1(self, x_src, x_dst, f_cos, f_sin, replica_groups, deps=()):
        self.ln_phase(x_src, deps=deps)
        ta = self.l1_attn(f_cos, f_sin, deps=deps)
        tcc = self.l1_ffn(replica_groups, sT_ready=ta, deps=[ta])
        tf = self.l1_ffn_post(tcc, sT_ready=ta)
        return self.residual_phase(x_src, x_dst, deps=[tf])

    def l3_attn(self, deps=()):
        P = self.P
        st, sv, sa = self.st, self.sv, self.sa
        LAMBDA_INIT = 0.8 - 0.6 * math.exp(-0.3)
        qkv_dram = self.nc.dram_tensor("l3_qkv", [T, 3072], BF16)
        es2 = ExitStack()
        with es2:
            sb2 = lambda n, sh, dt: es2.enter_context(self.nc.sbuf_tensor(n, sh, dt))
            qst = [self.stream("l3q0"), self.stream("l3q1")]
            self._l3ps_free = None
            with ExitStack() as s1:
                sc1 = lambda n, sh, dt: s1.enter_context(self.nc.sbuf_tensor(n, sh, dt))
                qstg = sc1("l3qstg", [128, 2 * 512], BF16)
                sfree = {}
                qi = 0
                part_mm = None
                for part in range(3):
                    with ExitStack() as s1a:
                        sc1a = lambda n, sh, dt: s1a.enter_context(self.nc.sbuf_tensor(n, sh, dt))
                        wp = sc1a(f"l3w{part}", [128, 8 * 1024], BF16)
                        strm = self.stream(f"l3w{part}")
                        tw = None
                        for k in range(8):
                            tw = P.dma("gpsimd", wp[:, k * 1024:(k + 1) * 1024],
                                       self.w["L3_attn_Wqkv"][k * 128:(k + 1) * 128,
                                                              part * 1024:(part + 1) * 1024],
                                       strm, deps=(([self.tok_xnT, part_mm] + list(deps))
                                                   if k == 0 else ()))
                        for tt in range(TT):
                            for c2 in range(2):
                                gp = qi % 2
                                qs = qstg[:, gp * 512:(gp + 1) * 512]
                                prs = [(self.xnT[:, k * T + tt * 128: k * T + tt * 128 + 128],
                                        wp[:, k * 1024 + c2 * 512: k * 1024 + c2 * 512 + 512])
                                       for k in range(8)]
                                mm = self.mm_acc(self.psB[:, :], prs, sem=st,
                                                 deps=[tw, self.tok_xnT, self._l3ps_free])
                                ev = P.op("scalar", "activation", qs, self.psB[:, :],
                                          AF.Identity, scale=(0.125 if part <= 1 else 1.0),
                                          sem=sa, deps=[mm, sfree.get(qi - 2)])
                                self._l3ps_free = ev
                                stw = P.dma("sync", qkv_dram[tt * 128:(tt + 1) * 128,
                                                             part * 1024 + c2 * 512:
                                                             part * 1024 + c2 * 512 + 512],
                                            qs, qst[gp], deps=[ev])
                                sfree[qi] = stw
                                qi += 1
                        part_mm = mm
            import os as _os
            if _os.environ.get("L3STOP") == "qkv":
                self.tok_sT_attn = (qst[0], qst[0].n)
                P.wait("sync", qst[0])
                P.wait("sync", qst[1])
                return (self.sv, self.sv.n)
            lamb = sb2("l3lam", [1, 4 * 512], F32)
            lsc = sb2("l3lsc", [1, 8], F32)
            lam128 = sb2("l3lam128", [128, 1], F32)
            lambf = sb2("l3lambf", [1, 1], BF16)
            lst = self.stream("l3lam")
            tl = None
            for i, nm in enumerate(["lambda_q1", "lambda_k1", "lambda_q2", "lambda_k2"]):
                tl = P.dma("sync", lamb[0:1, i * 512:(i + 1) * 512],
                           self.w["L3_attn_" + nm].ap().rearrange("(o n) -> o n", o=1),
                           lst, deps=([(qst[0], qst[0].n), (qst[1], qst[1].n)]
                                      if i == 0 else ()))
            u1 = P.op("vector", "tensor_tensor", lamb[0:1, 0:512], lamb[0:1, 0:512],
                      lamb[0:1, 512:1024], ALU.mult, sem=sv, deps=[tl])
            u2 = P.op("vector", "tensor_tensor", lamb[0:1, 1024:1536], lamb[0:1, 1024:1536],
                      lamb[0:1, 1536:2048], ALU.mult, sem=sv, deps=[u1])
            r1 = P.op("vector", "tensor_reduce", lsc[0:1, 0:1], lamb[0:1, 0:512], AX.X,
                      ALU.add, sem=sv, deps=[u2])
            r2 = P.op("vector", "tensor_reduce", lsc[0:1, 1:2], lamb[0:1, 1024:1536], AX.X,
                      ALU.add, sem=sv, deps=[r1])
            e1 = P.op("scalar", "activation", lsc[0:1, 2:3], lsc[0:1, 0:1], AF.Exp,
                      sem=sa, deps=[r2])
            e2 = P.op("scalar", "activation", lsc[0:1, 3:4], lsc[0:1, 1:2], AF.Exp,
                      sem=sa, deps=[e1])
            l1_ = P.op("vector", "tensor_tensor", lsc[0:1, 4:5], lsc[0:1, 3:4],
                       lsc[0:1, 2:3], ALU.subtract, sem=sv, deps=[e2])
            l2_ = P.op("vector", "tensor_scalar", lsc[0:1, 4:5], lsc[0:1, 4:5],
                       -LAMBDA_INIT, None, ALU.add, sem=sv, deps=[l1_])
            l3_ = P.op("vector", "tensor_scalar", lsc[0:1, 4:5], lsc[0:1, 4:5],
                       1.0 - LAMBDA_INIT, None, ALU.mult, sem=sv, deps=[l2_])
            lcp = P.op("vector", "tensor_copy", lambf[0:1, :], lsc[0:1, 4:5], sem=sv,
                       deps=[l3_])
            bcl = P.op("tensor", "matmul", self.psD[:, 0:1], self.onesrow[0:1, 0:128],
                       lambf[0:1, :], start=True, stop=True, sem=st,
                       deps=[lcp, self._l3ps_free], skip_group_check=True)
            lcv = P.op("vector", "tensor_copy", lam128[:, :], self.psD[:, 0:1], sem=sv,
                       deps=[bcl])
            ybuf = sb2("l3y", [128, DC * T], BF16)
            with ExitStack() as s3:
                sc3 = lambda n, sh, dt: s3.enter_context(self.nc.sbuf_tensor(n, sh, dt))
                attnb = sc3("l3attnb", [128, DC * T], BF16)
                s3b = ExitStack()
                sc3b = lambda n, sh, dt: s3b.enter_context(self.nc.sbuf_tensor(n, sh, dt))
                Ka = sc3b("l3Ka", [65, T], BF16)
                Qa = sc3b("l3Qa", [65, 2 * T], BF16)
                Vh = sc3b("l3Vh", [128, 16 * 65], BF16)
                kbl = sc3b("l3kbl", [128, 1024], BF16)
                vbl = sc3b("l3vbl", [128, 1024], BF16)
                vT = sc3b("l3vT", [64, T], BF16)
                qbl = sc3b("l3qbl", [64, T], BF16)
                PTa = sc3b("l3PTa", [128, 8 * 512], BF16)
                PTx = sc3b("l3PTb", [128, 8 * 512], BF16)
                rstg = sc3b("l3rstg", [128, 2 * 512], F32)
                rabs = sc3b("l3rabs", [64, 2 * 512], BF16)
                sst = sc3b("l3sst", [128, 4], F32)
                mnbf = sc3b("l3mnbf", [128, 1], BF16)
                self._l3rbf16 = sc3b("l3rb16", [1, 1024], BF16)
                vstr = self.stream("l3v")
                kstr = self.stream("l3k")
                qstr = self.stream("l3qh")
                tok_attn = None
                tok_red = None
                tok_exp = {0: None, 1: None}
                tok_pv = {0: None, 1: None}
                tok_stat = None
                tr_free = None
                q_free = None
                pv_hist = {}
                for kvh in range(8):
                    ldk = []
                    for at in range(2):
                        ldk.append(P.dma("sync", kbl[:, at * 512:(at + 1) * 512],
                                         qkv_dram[256 * kvh + at * 128:256 * kvh + at * 128 + 128,
                                                  2048:2560], kstr,
                                         deps=([tr_free, tok_exp[0], tok_exp[1],
                                                (qst[0], qst[0].n) if kvh == 0 else None,
                                                (qst[1], qst[1].n) if kvh == 0 else None]
                                               if at == 0 else ())))
                        ldk.append(P.dma("sync", vbl[:, at * 512:(at + 1) * 512],
                                         qkv_dram[256 * kvh + at * 128:256 * kvh + at * 128 + 128,
                                                  2560:3072], kstr))
                    ko = P.op("vector", "memset", Ka[64:65, :], 1.0, sem=sv,
                              deps=[lcv if kvh == 0 else None, tok_attn])
                    for which, (blk, dstb) in enumerate([(kbl, Ka), (vbl, vT)]):
                        for at in range(2):
                            last4 = None
                            for c in range(4):
                                last4 = P.op("tensor", "transpose",
                                             self.psT16[:, c * 128:(c + 1) * 128],
                                             blk[:, at * 512 + c * 128: at * 512 + c * 128 + 128],
                                             self.ident[:, :],
                                             sem=(st if c == 3 else None),
                                             deps=([ldk[-1], tr_free, tok_stat] if c == 0 else ()))
                            dv = dstb[0:64, :].rearrange("p (a f) -> p a f", f=8)
                            for c in range(4):
                                for hh in range(2):
                                    f = 2 * c + hh
                                    tr_free = P.op("vector", "tensor_copy",
                                                   dv[:, at * 128:(at + 1) * 128, f],
                                                   self.psT16[64 * hh:64 * hh + 64,
                                                              c * 128:(c + 1) * 128],
                                                   sem=sv, deps=[last4])
                    vo = P.op("vector", "memset",
                              Vh[:, :].rearrange("p (u c) -> p u c", c=65)[:, :, 64:65],
                              1.0, sem=sv, deps=[tok_pv[0], tok_pv[1]])
                    for u in range(TT):
                        lv = P.op("tensor", "transpose",
                                  self.psT16[:, (u % 8) * 64:(u % 8) * 64 + 64],
                                  vT[0:64, u * 128:(u + 1) * 128], self.ident[0:64, 0:64],
                                  sem=(st if u % 8 == 7 else None),
                                  deps=([tr_free, vo, tok_stat] if u == 0 else
                                        ([tr_free] if u % 8 == 0 else ())))
                        if u % 8 == 7:
                            vdv = Vh[:, :].rearrange("p (u2 c) -> p u2 c", c=65)[
                                :, u - 7:u + 1, 0:64]
                            svv = self.psT16[:, 0:512].rearrange("p (u2 c) -> p u2 c", c=64)
                            tr_free = P.op("vector", "tensor_copy", vdv, svv, sem=sv,
                                           deps=[lv])
                    if kvh == 0:
                        self.dbg("l3Ka", Ka[0:64, :], tr_free)
                        self.dbg("l3Vh", Vh[:, :], tr_free)
                        if self.dbg_on:
                            tr_free = self._dbg_tok
                    if _os.environ.get("L3STOP") == "kv":
                        continue
                    for ip in range(2):
                        effh = 2 * kvh + ip
                        if _os.environ.get("L3STOP") == "q1" and (kvh > 0 or ip > 0):
                            continue
                        for sub in range(2):
                            qh = 4 * kvh + 2 * ip + sub
                            if _os.environ.get("L3STOP") == "q1" and sub > 0:
                                continue
                            ldq = P.dma("sync", qbl[0:64, :],
                                        qkv_dram[64 * qh:64 * (qh + 1), 0:2048], qstr,
                                        deps=[tr_free, q_free, tok_stat])
                            qdv = Qa[0:64, sub * T:(sub + 1) * T].rearrange(
                                "p (a f) -> p a f", f=32)
                            for g in range(2):
                                lastq = None
                                for cl in range(8):
                                    c = g * 8 + cl
                                    lastq = P.op("tensor", "transpose",
                                                 self.psT16[:, cl * 64:cl * 64 + 64],
                                                 qbl[0:64, c * 128:(c + 1) * 128],
                                                 self.ident[0:64, 0:64],
                                                 sem=(st if cl == 7 else None),
                                                 deps=([ldq if g == 0 else None, tr_free,
                                                        tok_stat] if cl == 0 else ()))
                                for cl in range(8):
                                    c = g * 8 + cl
                                    for hh in range(2):
                                        m = 2 * c + hh
                                        tr_free = P.op("vector", "tensor_copy",
                                                       qdv[:, :, m],
                                                       self.psT16[64 * hh:64 * hh + 64,
                                                                  cl * 64:cl * 64 + 64],
                                                       sem=sv,
                                                       deps=([lastq] if (cl == 0 and hh == 0) else ()))
                        q_free = tr_free
                        if _os.environ.get("L3STOP") in ("q", "q1"):
                            continue
                        if kvh == 0 and ip == 0:
                            self.dbg(f"l3Qa", Qa[0:64, :], tr_free)
                            if self.dbg_on:
                                tr_free = self._dbg_tok
                                tok_stat = self._dbg_tok
                        for qg in range(4):
                            qts = [4 * qg + i for i in range(4)]
                            ktn = cdiv((qts[-1] + 1) * 128, 128)
                            for sub in range(2):
                                PT = PTa if sub == 0 else PTx
                                for qt in qts:
                                    kmax = (qt + 1) * 128
                                    nch = cdiv(kmax, 512)
                                    tok_qk = None
                                    for c in range(nch):
                                        k1 = min(512, kmax - c * 512)
                                        tok_qk = P.op(
                                            "tensor", "matmul",
                                            self.psA[:, c * 512:c * 512 + k1],
                                            Qa[0:64, sub * T + qt * 128: sub * T + qt * 128 + 128],
                                            Ka[0:64, c * 512:c * 512 + k1],
                                            start=True, stop=True, sem=st,
                                            deps=([tr_free, tok_red, tok_attn] if c == 0 else ()),
                                            skip_group_check=True)
                                    tqm = P.op("vector", "tensor_tensor",
                                               self.psA[:, qt * 128:qt * 128 + 128],
                                               self.psA[:, qt * 128:qt * 128 + 128],
                                               self.diagmask[:, :], ALU.add, sem=sv,
                                               deps=[tok_qk])
                                    tok_red = P.op("vector", "tensor_reduce", sst[:, 0:1],
                                                   self.psA[:, 0:kmax], AX.X, ALU.max,
                                                   sem=sv, deps=[tqm, tok_stat])
                                    tneg = P.op("scalar", "activation", mnbf[:, :],
                                                sst[:, 0:1], AF.Identity, scale=-1.0,
                                                sem=sa, deps=[tok_red, tok_stat])
                                    ttr = P.op("tensor", "transpose", self.psT16[0:1, 0:128],
                                               mnbf[:, :], self.ident[:, :], sem=st,
                                               deps=[tneg, tok_stat, tr_free])
                                    tok_stat = P.op("scalar", "copy",
                                                    Qa[64:65, sub * T + qt * 128:
                                                       sub * T + qt * 128 + 128],
                                                    self.psT16[0:1, 0:128], sem=sa,
                                                    deps=[ttr])
                                for kt in (range(ktn) if _os.environ.get("L3STOP") != "qk" else []):
                                    ks = kt % 8
                                    tok_sT = P.op("tensor", "matmul", self.psB[:, :],
                                                  Ka[0:65, kt * 128:(kt + 1) * 128],
                                                  Qa[0:65, sub * T + qg * 512:
                                                     sub * T + qg * 512 + 512],
                                                  start=True, stop=True, sem=st,
                                                  deps=[tok_stat, tok_exp[sub]],
                                                  skip_group_check=True)
                                    mt = self._mask_subtiles(qg, kt, lambda q: (q + 1) * 128)
                                    tok_m = tok_sT
                                    for (sub_i, mask) in mt:
                                        tok_m = P.op("vector", "tensor_tensor",
                                                     self.psB[:, sub_i * 128:(sub_i + 1) * 128],
                                                     self.psB[:, sub_i * 128:(sub_i + 1) * 128],
                                                     mask, ALU.add, sem=sv, deps=[tok_m])
                                    tok_exp[sub] = P.op("scalar", "activation",
                                                        PT[:, ks * 512:(ks + 1) * 512],
                                                        self.psB[:, :], AF.Exp, sem=sa,
                                                        deps=[tok_m, pv_hist.get((sub, kt - 8))])
                                    ps_pv = self.psC if sub == 0 else self.psD
                                    tok_pv[sub] = P.op("tensor", "matmul", ps_pv[0:65, :],
                                                       Vh[:, kt * 65:(kt + 1) * 65],
                                                       PT[:, ks * 512:(ks + 1) * 512],
                                                       start=(kt == 0), stop=(kt == ktn - 1),
                                                       sem=st,
                                                       deps=[tok_exp[sub], vo,
                                                             tok_attn if kt == 0 else None],
                                                       skip_group_check=True)
                                    pv_hist[(sub, kt)] = tok_pv[sub]
                            if kvh == 0 and ip == 0 and qg == 0 and self.dbg_on:
                                self.dbg("l3PTa0", PTa[:, 0:2048], tok_pv[0])
                                self.dbg("l3PTx0", PTx[:, 0:2048], tok_pv[1])
                                tok_exp[0] = self._dbg_tok
                                tok_exp[1] = self._dbg_tok
                            if _os.environ.get("L3STOP") == "qk":
                                continue
                            rla = P.op("vector", "reciprocal", rstg[0:1, 0:512],
                                       self.psC[64:65, :], sem=sv, deps=[tok_pv[0], tok_attn])
                            rlb = P.op("vector", "reciprocal", rstg[0:1, 512:1024],
                                       self.psD[64:65, :], sem=sv, deps=[tok_pv[1]])
                            rbf = P.op("vector", "tensor_copy", self._l3rbf16[0:1, 0:1024],
                                       rstg[0:1, 0:1024], sem=sv, deps=[rla, rlb])
                            ba = P.op("tensor", "matmul", self.psA[0:64, 0:512],
                                      self.onesrow[0:1, 0:64], self._l3rbf16[0:1, 0:512],
                                      start=True, stop=True, sem=st,
                                      deps=[rbf, tok_red], skip_group_check=True)
                            bb = P.op("tensor", "matmul", self.psA[0:64, 512:1024],
                                      self.onesrow[0:1, 0:64], self._l3rbf16[0:1, 512:1024],
                                      start=True, stop=True, sem=st,
                                      skip_group_check=True)
                            eva = P.op("scalar", "activation", rabs[:, 0:512],
                                       self.psA[0:64, 0:512], AF.Identity, sem=sa,
                                       deps=[ba, bb])
                            evb = P.op("scalar", "activation", rabs[:, 512:1024],
                                       self.psA[0:64, 512:1024], AF.Identity, sem=sa,
                                       deps=[eva])
                            v1 = P.op("vector", "tensor_tensor", rstg[0:64, 0:512],
                                      self.psC[0:64, :], rabs[0:64, 0:512], ALU.mult,
                                      sem=sv, deps=[evb])
                            v2 = P.op("vector", "scalar_tensor_tensor", rstg[0:64, 512:1024],
                                      self.psD[0:64, :], lam128[0:64, 0:1], rabs[0:64, 512:1024],
                                      ALU.mult, ALU.mult, sem=sv, deps=[v1, lcv])
                            dsta = attnb[64 * (effh % 2):64 * (effh % 2) + 64,
                                         (effh // 2) * T + qg * 512:
                                         (effh // 2) * T + qg * 512 + 512]
                            tok_attn = P.op("vector", "scalar_tensor_tensor", dsta,
                                            rstg[0:64, 0:512], 1.0 - LAMBDA_INIT,
                                            rstg[0:64, 512:1024], ALU.mult, ALU.add,
                                            sem=sv, deps=[v2])
                self.dbg("l3attnb", attnb[:, :], tok_attn)
                if _os.environ.get("L3STOP") in ("heads", "kv", "q", "qk", "q1"):
                    cpy = P.op("vector", "tensor_copy", self.sT[:, :], attnb[:, :],
                               sem=self.sv, deps=[tok_attn])
                    s3b.close()
                    self.tok_sT_attn = cpy
                    return cpy
                s3b.close()
                with ExitStack() as s4:
                    sc4 = lambda n, sh, dt: s4.enter_context(self.nc.sbuf_tensor(n, sh, dt))
                    wpj, t_wpj = self.load_w_nat("L3_attn_Wproj", 1024, 1024, sc4,
                                                 deps=[tok_attn])
                    t_y = self.proj_T(wpj, t_wpj, 8, ybuf,
                                      lambda m, c: ybuf[:, m * T + c * 512: m * T + c * 512 + 512],
                                      8, None, attnb, 0, extra_dep=tok_attn)
                    y2b = sc4("l3y2", [128, DC * T], BF16)
                    prfree = None
                    for t2 in range(2):
                        for ct in range(DC):
                            for cc in range(DC):
                                tp = P.op("tensor", "transpose",
                                          self.psT16[:, (cc % 8) * 128:(cc % 8) * 128 + 128],
                                          ybuf[:, ct * T + t2 * 1024 + cc * 128:
                                               ct * T + t2 * 1024 + cc * 128 + 128],
                                          self.ident[:, :],
                                          sem=(st if cc == DC - 1 else None),
                                          deps=([t_y, prfree] if cc == 0 else ()))
                            dv2 = y2b[:, :].rearrange("p (cc g m2) -> p cc g m2", g=T // 256,
                                                      m2=256)[:, :, ct, :].rearrange(
                                "p cc (m two) -> p cc m two", two=2)[:, :, :, t2]
                            srcv = self.psT16[:, :].rearrange("p (cc j) -> p cc j", j=128)
                            prfree = P.op("vector", "tensor_copy", dv2, srcv, sem=sv,
                                          deps=[tp])
                    self.dbg("l3ybuf", ybuf[:, :], t_y)
                    self.dbg("l3y2b", y2b[:, :], prfree)
                    tok_s = self.proj_T(wpj, t_wpj, 8, self.sT,
                                        lambda m, c: self.sT[:, m * T + c * 512: m * T + c * 512 + 512],
                                        8, None, y2b, 0, extra_dep=prfree)
        self.tok_sT_attn = tok_s
        return tok_s

    def l3_ffn(self, sT_ready=None, deps=()):
        P = self.P
        st, sv, sa = self.st, self.sv, self.sa
        from concourse.bass import broadcast_tensor_aps as _bc
        acc_dram = self.nc.dram_tensor("l3_acc", [D, T], F32)
        cw_dram = self.nc.dram_tensor("l3_cw", [48, T], BF16)
        es2 = ExitStack()
        with es2:
            sb2 = lambda n, sh, dt: es2.enter_context(self.nc.sbuf_tensor(n, sh, dt))
            with ExitStack() as sr:
                sbr = lambda n, sh, dt: sr.enter_context(self.nc.sbuf_tensor(n, sh, dt))
                gwb = sbr("l3_gw", [32, 1024], BF16)
                t_gw = P.dma("gpsimd", gwb[:, :], self.w["L3_ffn_gate_weight"][:, :],
                             self.stream("l3gw"), deps=list(deps))
                gbb = sbr("l3_gb", [1, 32], BF16)
                t_gb = P.dma("gpsimd", gbb[0:1, :],
                             self.w["L3_ffn_gate_bias"].ap().rearrange("(o n) -> o n", o=1),
                             self.stream("l3gb"), deps=list(deps))
                gwT = sbr("l3_gwT", [128, 8 * 32], BF16)
                gbx = sbr("l3_gbx", [128, 32], F32)
                tw = None
                for k in range(8):
                    tt1 = P.op("tensor", "transpose", self.psT16[:, 0:32],
                               gwb[0:32, k * 128:(k + 1) * 128], self.ident[0:32, 0:32],
                               sem=st, deps=[t_gw, tw, self.tok_consts] + (list(deps) if k == 0 else []))
                    tw = P.op("vector", "tensor_copy", gwT[:, k * 32:(k + 1) * 32],
                              self.psT16[:, 0:32], sem=sv, deps=[tt1])
                bcb = P.op("tensor", "matmul", self.psB[:, 0:32], self.onesrow[0:1, 0:128],
                           gbb[0:1, :], start=True, stop=True, sem=st,
                           deps=[t_gb, self.tok_ones, tw], skip_group_check=True)
                bcv = P.op("vector", "tensor_copy", gbx[:, :], self.psB[:, 0:32], sem=sv,
                           deps=[bcb])
                lgs = sbr("l3_lg", [128, 32], F32)
                scb = sbr("l3_scb", [128, 32], F32)
                scm = sbr("l3_scm", [128, 32], F32)
                mall = sbr("l3_mall", [128, 32], F32)
                grp = sbr("l3_grp", [128, 16], F32)
                stt_ = sbr("l3_st", [128, 16], F32)
                cwa = sbr("l3_cwa", [128, 48], BF16)
                self._l3_mtmp = sbr("l3_mtmp", [128, 32], F32)
                cwT_stage = sbr("l3_cwts", [48, T], BF16)
                tok_lg = None
                tok_cws = None
                for tt in range(TT):
                    prs = [(self.xnT[:, k * T + tt * 128: k * T + tt * 128 + 128],
                            gwT[:, k * 32:(k + 1) * 32]) for k in range(8)]
                    mm = self.mm_acc(self.psB[:, 0:32], prs, sem=st,
                                     deps=[tw, bcv, self.tok_xnT, tok_lg])
                    sig = P.op("scalar", "activation", lgs[:, :], self.psB[:, 0:32],
                               AF.Sigmoid, sem=sa, deps=[mm, tok_cws])
                    tok_lg = sig
                    a1 = P.op("vector", "tensor_tensor", scb[:, :], lgs[:, :], gbx[:, :],
                              ALU.add, sem=sv, deps=[sig])
                    scv = scb[:, :].rearrange("p (g e) -> p g e", e=8)
                    g1 = stt_[:, 0:4]; g2 = stt_[:, 4:8]; gs = stt_[:, 8:12]
                    gm1 = stt_[:, 12:13]; gm2 = stt_[:, 13:14]
                    t1 = P.op("vector", "tensor_reduce", g1, scv, AX.X, ALU.max, sem=sv,
                              deps=[a1])
                    m1v, m1b = _bc(scv, g1.rearrange("p (g o) -> p g o", o=1))
                    t2 = P.op("vector", "tensor_tensor", grp[:, :].rearrange(
                        "p (g e) -> p g e", e=8) if False else scm[:, :].rearrange(
                        "p (g e) -> p g e", e=8), m1v, m1b, ALU.is_equal, sem=sv, deps=[t1])
                    t3 = P.op("vector", "scalar_tensor_tensor", scm[:, :], scm[:, :], NEG,
                              scb[:, :], ALU.mult, ALU.add, sem=sv, deps=[t2])
                    t4 = P.op("vector", "tensor_reduce", g2, scm[:, :].rearrange(
                        "p (g e) -> p g e", e=8), AX.X, ALU.max, sem=sv, deps=[t3])
                    t5 = P.op("vector", "tensor_tensor", gs, g1, g2, ALU.add, sem=sv,
                              deps=[t4])
                    t6 = P.op("vector", "tensor_reduce", gm1, gs, AX.X, ALU.max, sem=sv,
                              deps=[t5])
                    t7 = P.op("vector", "tensor_scalar", grp[:, 0:4], gs, gm1, None,
                              ALU.is_equal, sem=sv, deps=[t6])
                    t8 = P.op("vector", "scalar_tensor_tensor", grp[:, 4:8], grp[:, 0:4],
                              NEG, gs, ALU.mult, ALU.add, sem=sv, deps=[t7])
                    t9 = P.op("vector", "tensor_reduce", gm2, grp[:, 4:8], AX.X, ALU.max,
                              sem=sv, deps=[t8])
                    t10 = P.op("vector", "tensor_scalar", grp[:, 8:12], grp[:, 4:8], gm2,
                               None, ALU.is_equal, sem=sv, deps=[t9])
                    t11 = P.op("vector", "tensor_tensor", grp[:, 0:4], grp[:, 0:4],
                               grp[:, 8:12], ALU.add, sem=sv, deps=[t10])
                    ev8, eb8 = _bc(scb[:, :].rearrange("p (g e) -> p g e", e=8),
                                   grp[:, 0:4].rearrange("p (g o) -> p g o", o=1))
                    t12 = P.op("vector", "scalar_tensor_tensor",
                               scm[:, :].rearrange("p (g e) -> p g e", e=8),
                               eb8, -NEG, ev8, ALU.mult, ALU.add, sem=sv, deps=[t11])
                    zz = P.op("vector", "memset", mall[:, :], 0.0, sem=sv, deps=[t12])
                    tprev = zz
                    for it in range(6):
                        mi = stt_[:, 14:15]
                        tca = P.op("vector", "tensor_reduce", mi, scm[:, :], AX.X, ALU.max,
                                   sem=sv, deps=[tprev])
                        tcc2 = P.op("vector", "tensor_scalar", self._l3_mtmp[:, :],
                                    scm[:, :], mi, None, ALU.is_equal, sem=sv, deps=[tca])
                        if it == 0:
                            acw = cwa[:, 32:48]
                            pr = self._l3_mtmp[:, :].rearrange("p (l two) -> p l two", two=2)
                            tpr = P.op("vector", "tensor_tensor", grp[:, 0:16],
                                       pr[:, :, 0], pr[:, :, 1], ALU.add, sem=sv, deps=[tcc2])
                            tpr2 = P.op("vector", "tensor_scalar", acw, grp[:, 0:16],
                                        0.1, None, ALU.mult, sem=sv, deps=[tpr])
                        tm2 = P.op("vector", "tensor_tensor", mall[:, :], mall[:, :],
                                   self._l3_mtmp[:, :], ALU.add, sem=sv, deps=[tcc2])
                        tprev = P.op("vector", "scalar_tensor_tensor", scm[:, :],
                                     self._l3_mtmp[:, :], NEG, scm[:, :], ALU.mult,
                                     ALU.add, sem=sv, deps=[tm2])
                    tt1_ = P.op("vector", "tensor_tensor", scm[:, :], mall[:, :], lgs[:, :],
                                ALU.mult, sem=sv, deps=[tprev])
                    ts1 = P.op("vector", "tensor_reduce", stt_[:, 14:15], scm[:, :], AX.X,
                               ALU.add, sem=sv, deps=[tt1_])
                    ts2 = P.op("vector", "tensor_scalar", stt_[:, 14:15], stt_[:, 14:15],
                               1e-20, None, ALU.add, sem=sv, deps=[ts1])
                    ts3 = P.op("vector", "reciprocal", stt_[:, 15:16], stt_[:, 14:15],
                               sem=sv, deps=[ts2])
                    ts4 = P.op("vector", "tensor_scalar", stt_[:, 15:16], stt_[:, 15:16],
                               0.2, None, ALU.mult, sem=sv, deps=[ts3])
                    ts5 = P.op("vector", "tensor_scalar", cwa[:, 0:32], scm[:, :],
                               stt_[:, 15:16], None, ALU.mult, sem=sv, deps=[ts4])
                    ttr = P.op("tensor", "transpose", self.psT16[0:48, 0:128],
                               cwa[:, :], self.ident[:, :], sem=st,
                               deps=[ts5, tok_cws])
                    tok_cws = P.op("vector", "tensor_copy",
                                   cwT_stage[:, tt * 128:(tt + 1) * 128],
                                   self.psT16[0:48, 0:128], sem=sv, deps=[ttr])
                tok_cwd = P.dma("sync", cw_dram[:, :], cwT_stage[:, :],
                                self.stream("l3cwd"), deps=[tok_cws])
            wgb = sb2("l3_ew1", [128, 2 * 8 * 512], BF16)
            wub = sb2("l3_ew2", [128, 2 * 8 * 512], BF16)
            wdb = sb2("l3_ew3", [128, 2 * 4 * 1024], BF16)
            hbuf = sb2("l3_h", [128, 4 * 512], BF16)
            gstg = sb2("l3_gs", [128, 2 * 512], BF16)
            dstg = sb2("l3_ds", [128, 2 * 512], F32)
            cwbc = sb2("l3_cwbc", [128, T], BF16)
            cwrow = sb2("l3_cwrow", [1, 2 * T], BF16)
            accst2 = [self.stream("l3acc0"), self.stream("l3acc1")]
            wstr = [self.stream("l3ew0"), self.stream("l3ew1")]
            cwst2 = [self.stream("l3cwr0"), self.stream("l3cwr1")]
            experts = [(0, 0)] + [(1, e) for e in range(32)] + [(2, l) for l in range(16)]
            gfree = {1: sT_ready}
            ufree = {1: sT_ready}
            dfree = {1: sT_ready}
            gsl_free = {}
            dsl_free = {}
            wfree = {}
            tok_bc = None
            cwr_free = {}
            gidx = didx = 0
            down_done = None
            for ei, (kind, e) in enumerate(experts):
                lp = ei % 2
                NI = 256 if kind == 2 else 512
                nki = NI // 128
                wg_sl = wgb[:, lp * 4096: lp * 4096 + 8 * NI]
                wu_sl = wub[:, lp * 4096: lp * 4096 + 8 * NI]
                wd_sl = wdb[:, lp * 4096: lp * 4096 + nki * 1024]
                if kind == 0:
                    srcs = [self.w["L3_ffn_shared_w_gate"], self.w["L3_ffn_shared_w_up"],
                            self.w["L3_ffn_shared_w_down"]]
                elif kind == 1:
                    srcs = [self.w["L3_ffn_w_gate"][e], self.w["L3_ffn_w_up"][e],
                            self.w["L3_ffn_w_down"][e]]
                else:
                    srcs = [self.w["L3_ffn_anchor_w_gate"][e], self.w["L3_ffn_anchor_w_up"][e],
                            self.w["L3_ffn_anchor_w_down"][e]]
                ldp = [wfree.get(ei - 2), tok_cwd if ei == 0 else None] + \
                    (list(deps) if ei == 0 else [])
                tw = None
                for k in range(8):
                    tw = P.dma("gpsimd", wg_sl[:, k * NI:(k + 1) * NI],
                               srcs[0][k * 128:(k + 1) * 128, :], wstr[lp],
                               deps=(ldp if k == 0 else ()))
                for k in range(8):
                    tw = P.dma("gpsimd", wu_sl[:, k * NI:(k + 1) * NI],
                               srcs[1][k * 128:(k + 1) * 128, :], wstr[lp])
                for k in range(nki):
                    tw = P.dma("gpsimd", wd_sl[:, k * 1024:(k + 1) * 1024],
                               srcs[2][k * 128:(k + 1) * 128, :], wstr[lp])
                if kind != 0:
                    crow = cwrow[0:1, lp * T:(lp + 1) * T]
                    ri = e if kind == 1 else 32 + e
                    tcr = P.dma("sync", crow, cw_dram[ri:ri + 1, :], cwst2[lp],
                                deps=[cwr_free.get(ei - 2), tok_cwd if ei <= 1 else None])
                    bcs = None
                    for c in range(4):
                        tb = P.op("tensor", "matmul", self.psA[:, 1536:2048],
                                  self.onesrow[0:1, 0:128],
                                  crow[0:1, c * 512:(c + 1) * 512],
                                  start=True, stop=True, sem=st,
                                  deps=([tcr, tok_bc] if c == 0 else [bcs]),
                                  skip_group_check=True)
                        bcs = P.op("scalar", "activation", cwbc[:, c * 512:(c + 1) * 512],
                                   self.psA[:, 1536:2048], AF.Identity,
                                   sem=sa, deps=[tb])
                    tok_bc = bcs
                    cwr_free[ei] = bcs
                for nc4 in range(4):
                    hv = None
                    for mt in range(nki):
                        gp = gidx % 2
                        gsl = gstg[:, gp * 512:(gp + 1) * 512]
                        pg = self.psB[:, 0:512] if gp == 0 else self.psA[:, 0:512]
                        pu = self.psC[:, 0:512] if gp == 0 else self.psA[:, 512:1024]
                        xsl = lambda dk: self.xnT[:, dk * T + nc4 * 512: dk * T + nc4 * 512 + 512]
                        prsg = [(wg_sl[:, dk * NI + mt * 128: dk * NI + mt * 128 + 128], xsl(dk))
                                for dk in range(8)]
                        g_tok = self.mm_acc(pg, prsg, sem=st,
                                            deps=[tw, self.tok_xnT, gfree.get(gp)]
                                            + (list(deps) if gidx == 0 else []))
                        sil = P.op("scalar", "activation", gsl, pg, AF.Silu,
                                   sem=sa, deps=[g_tok, gsl_free.get(gidx - 2)])
                        gfree[gp] = sil
                        if kind != 0:
                            sil = P.op("vector", "tensor_tensor", gsl, gsl,
                                       cwbc[:, nc4 * 512:(nc4 + 1) * 512], ALU.mult,
                                       sem=sv, deps=[sil, tok_bc])
                        prsu = [(wu_sl[:, dk * NI + mt * 128: dk * NI + mt * 128 + 128], xsl(dk))
                                for dk in range(8)]
                        u_tok = self.mm_acc(pu, prsu, sem=st,
                                            deps=[ufree.get(gp)] + (list(deps) if gidx == 0 else []))
                        hdeps = [u_tok, sil]
                        if mt == 0:
                            hdeps.append(down_done)
                        hv = P.op("vector", "tensor_tensor",
                                  hbuf[:, mt * 512:(mt + 1) * 512], pu, gsl,
                                  ALU.mult, sem=sv, deps=hdeps)
                        ufree[gp] = hv
                        gsl_free[gidx] = hv
                        gidx += 1
                    for dm in range(8):
                        dp = didx % 2
                        pd = self.psD[:, 0:512] if dp == 0 else self.psA[:, 1024:1536]
                        prd = [(wd_sl[:, kh * 1024 + dm * 128: kh * 1024 + dm * 128 + 128],
                                hbuf[:, kh * 512:(kh + 1) * 512]) for kh in range(nki)]
                        d_tok = self.mm_acc(pd, prd, sem=st,
                                            deps=[hv, tw, dfree.get(dp)]
                                            + (list(deps) if didx == 0 else []))
                        dsl = dstg[:, dp * 512:(dp + 1) * 512]
                        ev = P.op("scalar", "activation", dsl, pd, AF.Identity,
                                  sem=sa, deps=[d_tok, dsl_free.get(didx - 2)])
                        dfree[dp] = ev
                        stw = P.dma("gpsimd", acc_dram[dm * 128:(dm + 1) * 128,
                                                       nc4 * 512:(nc4 + 1) * 512], dsl,
                                    accst2[dp], deps=[ev],
                                    accum_op=(ALU.add if ei > 0 else ALU.bypass))
                        dsl_free[didx] = stw
                        didx += 1
                    down_done = d_tok
                wfree[ei] = down_done
            accrd = sb2("l3_accrd", [128, 2 * 2048], F32)
            gbf = sb2("l3_gbf", [128, 2 * 2048], BF16)
            rd_str2 = [self.stream("l3ar0"), self.stream("l3ar1")]
            rfree = {}
            mv = None
            for dm in range(8):
                par = dm % 2
                arr = accrd[:, par * 2048:(par + 1) * 2048]
                gb = gbf[:, par * 2048:(par + 1) * 2048]
                ld = P.dma("sync", arr, acc_dram[dm * 128:(dm + 1) * 128, :], rd_str2[par],
                           deps=[rfree.get(dm - 2), (accst2[0], accst2[0].n),
                                 (accst2[1], accst2[1].n)])
                sg2 = P.op("scalar", "activation", gb, arr, AF.Silu, sem=sa,
                           deps=[ld, rfree.get(dm - 2)])
                mdeps = [sg2]
                if dm == 0 and sT_ready is not None:
                    mdeps.append(sT_ready)
                mv = P.op("vector", "tensor_tensor", self.sT[:, dm * T:(dm + 1) * T],
                          self.sT[:, dm * T:(dm + 1) * T], gb, ALU.mult, sem=sv,
                          deps=mdeps)
                rfree[dm] = mv
        self.tok_sT = mv
        return mv

    def layer3(self, x_src, x_dst, deps=()):
        self.ln_phase(x_src, deps=deps)
        ta = self.l3_attn(deps=deps)
        tf = self.l3_ffn(sT_ready=ta, deps=[ta])
        return self.residual_phase(x_src, x_dst, deps=[tf])


WSHAPES = {
    "L0_attn_Wd_q_w": ([1024, 512], "f"), "L0_attn_Wd_q_b": ([512], "f"),
    "L0_attn_Wu_q_w": ([512, 1024], "f"), "L0_attn_Wu_q_b": ([1024], "f"),
    "L0_attn_Wq_r_w": ([512, 1024], "f"), "L0_attn_Wq_r_b": ([1024], "f"),
    "L0_attn_Wk_r_w": ([1024, 1024], "f"), "L0_attn_Wk_r_b": ([1024], "f"),
    "L0_attn_Wd_kv_w": ([1024, 512], "f"), "L0_attn_Wd_kv_b": ([512], "f"),
    "L0_attn_Wu_k_w": ([512, 1024], "f"), "L0_attn_Wu_k_b": ([1024], "f"),
    "L0_attn_Wu_v_w": ([512, 1024], "f"), "L0_attn_Wu_v_b": ([1024], "f"),
    "L0_attn_Wo_w": ([1024, 1024], "f"), "L0_attn_Wo_b": ([1024], "f"),
    "L0_ffn_w_gate": ([3072, 1024], "i"), "L0_ffn_w_up": ([3072, 1024], "i"),
    "L0_ffn_w_down": ([1024, 3072], "i"),
    "L1_attn_Wqkv": ([3072, 1024], "f"), "L1_attn_Wout": ([1024, 1024], "f"),
    "L1_attn_alpha": ([16, 32], "f"), "L1_attn_beta": ([16, 32], "f"),
    "L1_ffn_phi_w": ([2048, 1024], "f"), "L1_ffn_phi_b": ([2048], "f"),
    "L1_ffn_expert_w": ([64, 1024, 1024], "f"), "L1_ffn_expert_b": ([64, 1024], "f"),
    "L2_attn_W_qkv": ([1024, 3072], "f"), "L2_attn_W_q_lsr": ([16, 64, 16], "f"),
    "L2_attn_W_k_lsr": ([16, 64, 16], "f"), "L2_attn_core": ([16, 16], "f"),
    "L2_attn_W_o": ([1024, 1024], "f"),
    "L2_ffn_sg": ([2816, 1024], "f"), "L2_ffn_su": ([2816, 1024], "f"),
    "L2_ffn_sd": ([1024, 2816], "f"),
    "L2_ffn_rg": ([7, 2816, 1024], "f"), "L2_ffn_ru": ([7, 2816, 1024], "f"),
    "L2_ffn_rd": ([7, 1024, 2816], "f"),
    "L2_ffn_wr": ([7, 1024], "f"), "L2_ffn_rb": ([7], "f"),
    "L3_attn_Wqkv": ([1024, 3072], "f"), "L3_attn_Wproj": ([1024, 1024], "f"),
    "L3_attn_lambda_q1": ([512], "f"), "L3_attn_lambda_k1": ([512], "f"),
    "L3_attn_lambda_q2": ([512], "f"), "L3_attn_lambda_k2": ([512], "f"),
    "L3_ffn_gate_weight": ([32, 1024], "f"), "L3_ffn_gate_bias": ([32], "f"),
    "L3_ffn_w_gate": ([32, 1024, 512], "f"), "L3_ffn_w_up": ([32, 1024, 512], "f"),
    "L3_ffn_w_down": ([32, 512, 1024], "f"),
    "L3_ffn_shared_w_gate": ([1024, 512], "f"), "L3_ffn_shared_w_up": ([1024, 512], "f"),
    "L3_ffn_shared_w_down": ([512, 1024], "f"),
    "L3_ffn_anchor_w_gate": ([16, 1024, 256], "f"), "L3_ffn_anchor_w_up": ([16, 1024, 256], "f"),
    "L3_ffn_anchor_w_down": ([16, 256, 1024], "f"),
}


def _rope_tables():
    pos = np.arange(T, dtype=np.float64)[:, None]
    ids = np.arange(512, dtype=np.float64)
    theta = 10000.0 ** (-2.0 * ids) / 512.0
    r = pos * theta[None, :]
    return np.cos(r).astype(np.float32), np.sin(r).astype(np.float32)


def _fourier_tables():
    r = np.arange(32, dtype=np.float64)
    om = 10000.0 ** (-2.0 * r / 64.0)
    k = np.arange(T, dtype=np.float64)
    ang = om[:, None] * k[None, :]
    return np.cos(ang).astype(np.float32), np.sin(ang).astype(np.float32)


def _build(ncores):
    nc = bass.Bass("TRN2", target_bir_lowering=False)
    es = ExitStack()
    with es:
        w = {}
        x_in = nc.dram_tensor("x_in", [T, D], F32, kind="ExternalInput")
        x_out = nc.dram_tensor("x_out", [T, D], F32, kind="ExternalOutput")
        for nm, (shp, k) in WSHAPES.items():
            w[nm] = nc.dram_tensor(nm, shp, I32 if k == "i" else F32, kind="ExternalInput")
        rc = nc.dram_tensor("rope_cos", [T, 512], F32, kind="ExternalInput")
        rs = nc.dram_tensor("rope_sin", [T, 512], F32, kind="ExternalInput")
        fc = nc.dram_tensor("f_cos", [32, T], F32, kind="ExternalInput")
        fs = nc.dram_tensor("f_sin", [32, T], F32, kind="ExternalInput")
        xb1 = nc.dram_tensor("xb1", [T, D], F32)
        xb2 = nc.dram_tensor("xb2", [T, D], F32)
        xb3 = nc.dram_tensor("xb3", [T, D], F32)
        ctx = Ctx(nc, es, w)
        r0 = ctx.layer0(x_in, xb1, rc, rs)
        r1 = ctx.layer1(xb1, xb2, fc, fs, [list(range(ncores))], deps=[r0])
        r2 = ctx.layer2(xb2, xb3, deps=[r1])
        r3 = ctx.layer3(xb3, x_out, deps=[r2])
        ctx.finish()
        ctx.P.run()
    return nc


def kernel(**inputs):
    NC = 8
    x = np.ascontiguousarray(inputs["L0_attn_x"]).astype(np.float32)
    rc, rs = _rope_tables()
    fc, fs = _fourier_tables()
    base = {}
    for nm, (shp, k) in WSHAPES.items():
        base[nm] = np.ascontiguousarray(inputs[nm]).astype(np.int32 if k == "i" else np.float32)
    base["rope_cos"] = rc
    base["rope_sin"] = rs
    base["f_cos"] = fc
    base["f_sin"] = fs
    in_maps = []
    for b in range(NC):
        m = dict(base)
        m["x_in"] = np.ascontiguousarray(x[b])
        in_maps.append(m)
    nc = _build(NC)
    res = run_bass_kernel_spmd(nc, in_maps, core_ids=list(range(NC)))
    out = np.stack([res.results[b]["x_out"] for b in range(NC)], axis=0)
    return out.astype(np.float32)
